# version 09
# baseline (speedup 1.0000x reference)
import numpy as np
import concourse.bass as bass
import concourse.mybir as mybir
from concourse.bass_utils import run_bass_kernel_spmd

from contextlib import ExitStack

COMPUTE_STREAMS = ("T", "V", "S", "P")
QUEUE_STREAMS = ("Q",)
ALL_STREAMS = COMPUTE_STREAMS + QUEUE_STREAMS


class Op:
    __slots__ = ("stream", "fn", "reads", "writes", "dsem", "cc",
                 "idx", "need_inc", "waits", "tag")

    def __init__(self, stream, fn, reads, writes, dsem=None, cc=False, tag=""):
        self.stream = stream
        self.fn = fn
        self.reads = list(reads)
        self.writes = list(writes)
        self.dsem = dsem
        self.cc = cc
        self.idx = -1
        self.need_inc = False
        self.waits = []
        self.tag = tag


class Em:
    def __init__(self):
        self.ops = {s: [] for s in ALL_STREAMS}
        self.last_w = {}
        self.readers = {}
        self.dsem_count = {}

    def op(self, stream, fn, reads=(), writes=(), tag=""):
        o = Op(stream, fn, reads, writes, tag=tag)
        o.idx = len(self.ops[stream])
        self.ops[stream].append(o)
        tok = ("c", stream, o.idx)
        self._dep(o, tok)
        return tok

    def dma(self, qstream, fn, dsem, reads=(), writes=(), tag=""):
        o = Op(qstream, fn, reads, writes, dsem=dsem, tag=tag)
        o.idx = len(self.ops[qstream])
        self.ops[qstream].append(o)
        self.dsem_count[dsem] = self.dsem_count.get(dsem, 0) + 1
        tok = ("d", dsem, self.dsem_count[dsem], qstream, o.idx)
        self._dep(o, tok)
        return tok

    def cc(self, fn, dsem, reads=(), writes=(), tag=""):
        o = Op("P", fn, reads, writes, dsem=dsem, cc=True, tag=tag)
        o.idx = len(self.ops["P"])
        self.ops["P"].append(o)
        self.dsem_count[dsem] = self.dsem_count.get(dsem, 0) + 1
        tok = ("x", dsem, self.dsem_count[dsem], "P", o.idx)
        self._dep(o, tok)
        return tok

    def inherit(self, new_region, old_regions):
        toks = []
        for r in old_regions:
            t = self.last_w.get(r)
            if t is not None:
                toks.append(t)
            toks.extend(self.readers.get(r, ()))
        if toks:
            self.readers.setdefault(new_region, []).extend(toks)

    def _dep(self, o, tok):
        for r in o.reads:
            t = self.last_w.get(r)
            if t is not None:
                o.waits.append(t)
            self.readers.setdefault(r, []).append(tok)
        for w in o.writes:
            t = self.last_w.get(w)
            if t is not None:
                o.waits.append(t)
            for rt in self.readers.get(w, ()):
                o.waits.append(rt)
            self.readers[w] = []
            self.last_w[w] = tok
        seen = set()
        uw = []
        for t in o.waits:
            if t[0] == "c" and t[1] == o.stream:
                if t[2] == o.idx:
                    continue
                if o.stream == "T":
                    continue
            if t in seen:
                continue
            seen.add(t)
            uw.append(t)
        o.waits = uw
        for t in o.waits:
            if t[0] == "c":
                self.ops[t[1]][t[2]].need_inc = True

    def build(self, nc, block, ctx):
        inc_no = {}
        for s in COMPUTE_STREAMS:
            n = 0
            for o in self.ops[s]:
                if o.need_inc:
                    n += 1
                    inc_no[(s, o.idx)] = n
        nxt = {}
        for s in COMPUTE_STREAMS:
            lst = self.ops[s]
            nf = None
            arr = [None] * len(lst)
            for i in range(len(lst) - 1, -1, -1):
                if lst[i].need_inc:
                    nf = i
                arr[i] = nf
            nxt[s] = arr

        qcnt = {}
        qtot = {}
        for s in ("Q", "P"):
            k = 0
            for o in self.ops[s]:
                if o.dsem is not None and not o.cc and not o.dsem.startswith("pb_"):
                    k += 1
                    qcnt[(s, o.idx)] = k
            qtot[s] = k

        def resolve(tok):
            if tok[0] == "c":
                _, s, i = tok
                j = nxt[s][i]
                assert j is not None, f"no inc for {tok}"
                return ("sem_" + s, inc_no[(s, j)])
            if tok[0] == "d":
                _, dsem, cnt, q, idx = tok
                if dsem.startswith("pb_"):
                    return (dsem, 16 * cnt)
                return ("dq_" + q, 16 * qcnt[(q, idx)])
            return (tok[1], tok[2])

        sems = {}
        for s in COMPUTE_STREAMS:
            sems["sem_" + s] = ctx.enter_context(nc.semaphore("sem_" + s))
        for s in ("Q", "P"):
            sems["dq_" + s] = ctx.enter_context(nc.semaphore("dq_" + s))
        for d in self.dsem_count:
            if d.startswith("cs_") or d.startswith("pb_"):
                sems[d] = ctx.enter_context(nc.semaphore(d))

        resolved = {s: [[resolve(t) for t in o.waits] for o in self.ops[s]]
                    for s in self.ops}
        knowA = {s: [] for s in self.ops}
        for s in self.ops:
            k = {}
            for o in self.ops[s]:
                for sem, v in resolved[s][o.idx]:
                    if k.get(sem, 0) < v:
                        k[sem] = v
                knowA[s].append(dict(k))

        def know_at(tok):
            if tok[0] == "c":
                _, s, i = tok
                return knowA[s][nxt[s][i]]
            _, dsem, cnt, s, i = tok
            return knowA[s][i]

        final_waits = {s: [None] * len(self.ops[s]) for s in self.ops}
        for s in self.ops:
            k = {}
            for o in self.ops[s]:
                ws = []
                for t, rv in zip(o.waits, resolved[s][o.idx]):
                    sem, v = rv
                    if k.get(sem, 0) >= v:
                        continue
                    ws.append((sem, v))
                    k[sem] = v
                    for ksem, kv in know_at(t).items():
                        if k.get(ksem, 0) < kv:
                            k[ksem] = kv
                final_waits[s][o.idx] = ws

        def section(stream):
            def go(eng):
                for o in self.ops[stream]:
                    for sem, v in final_waits[stream][o.idx]:
                        eng.wait_ge(sems[sem], v)
                    if o.dsem is not None and not o.cc and not o.dsem.startswith("pb_"):
                        k = qcnt[(stream, o.idx)]
                        if k > 1:
                            eng.wait_ge(sems["dq_" + stream], 16 * (k - 1))
                    inst = o.fn(eng)
                    if o.cc:
                        inst.then_inc(sems[o.dsem], 1)
                    elif o.dsem is not None and o.dsem.startswith("pb_"):
                        inst.then_inc(sems[o.dsem], 16)
                    elif o.dsem is not None:
                        inst.then_inc(sems["dq_" + stream], 16)
                    elif o.need_inc:
                        inst.then_inc(sems["sem_" + stream], 1)
            return go

        @block.sync
        def _(eng):
            section("Q")(eng)

        @block.vector
        def _(eng):
            section("V")(eng)

        @block.scalar
        def _(eng):
            section("S")(eng)

        @block.gpsimd
        def _(eng):
            section("P")(eng)

        @block.tensor
        def _(eng):
            section("T")(eng)
        return sems

    def stats(self):
        r = {}
        for s in self.ops:
            n = len(self.ops[s])
            ninc = sum(1 for o in self.ops[s] if o.need_inc or o.dsem)
            r[s] = (n, ninc)
        return r


class Arena:

    def __init__(self, em, total):
        self.em = em
        self.total = total
        self.live = {}
        self.addr = {}
        self.freed = []

    def alloc(self, name, size, align=16):
        req = size
        size = (size + align - 1) // align * align
        taken = sorted([(o, s) for (o, s) in self.live.values()])
        prev = 0
        best = None
        for (o, s) in taken + [(self.total, 0)]:
            if o - prev >= size:
                best = prev
                break
            prev = max(prev, o + s)
        assert best is not None, f"arena OOM for {name} size {size}: live={self.live}"
        off = best
        olds = []
        keep = []
        for (fo, fs, fn_) in self.freed:
            if fo < off + size and off < fo + fs:
                olds.append(fn_)
                if fo < off:
                    keep.append((fo, off - fo, fn_))
                if fo + fs > off + size:
                    keep.append((off + size, fo + fs - (off + size), fn_))
            else:
                keep.append((fo, fs, fn_))
        self.freed = keep
        if olds:
            self.em.inherit(name, olds)
        assert name not in self.addr, f"arena name reused: {name}"
        self.live[name] = (off, size)
        self.addr[name] = (off, req)
        return off

    def free(self, name, aliases=()):
        off, size = self.live.pop(name)
        self.freed.append((off, size, name))
        for a in aliases:
            self.freed.append((off, size, a))

    def off(self, name):
        return self.addr[name][0]


F32 = mybir.dt.float32
BF16 = mybir.dt.bfloat16
FP16 = mybir.dt.float16
AF = mybir.ActivationFunctionType
ALU = mybir.AluOpType
AX = mybir.AxisListType


CH = 1024
TL = 2048
W = TL
WH = TL + 1
EPS = 1e-5
ARW = 49664
WINS = [(0, 1), (1, 513), (513, 1025), (1025, 1537), (1537, 2049)]
WINS_W = [(0, 512), (512, 1024), (1024, 1536), (1536, 2048)]

VTMK, VTMV, VTMR, VG1, VB1, VG2, VB2, VFMK, VFMR, VTD, VTF = [8 * i for i in range(11)]
VLNG0, VLNB0, VLNG1, VLNB1, VB2F0, VB2F1 = [88 + 8 * i for i in range(6)]
NVEC = 136


SC_NK0, SC_EU, SC_LAM, SC_W, SC_TMP = 0, 8, 16, 24, 32


class MK:
    def __init__(self, nc, debug=()):
        self.nc = nc
        self.em = Em()
        self.debug = list(debug)
        self.dbg_shapes = {}
        self._names = set()

    def din(self, name, shape, dtype=F32):
        return self.nc.dram_tensor(name, list(shape), dtype, kind="ExternalInput").ap()

    def dout(self, name, shape, dtype=F32):
        return self.nc.dram_tensor(name, list(shape), dtype, kind="ExternalOutput").ap()

    def dint(self, name, shape, dtype=F32):
        return self.nc.dram_tensor(name, list(shape), dtype).ap()

    def T(self, fn, r, w, tag=""):
        return self.em.op("T", fn, r, w, tag=tag)

    def V(self, fn, r, w, tag=""):
        return self.em.op("V", fn, r, w, tag=tag)

    def S(self, fn, r, w, tag=""):
        return self.em.op("S", fn, r, w, tag=tag)

    def P(self, fn, r, w, tag=""):
        return self.em.op("P", fn, r, w, tag=tag)

    def dmaQ(self, fn, dsem, r, w, tag=""):
        return self.em.dma("Q", fn, dsem, r, w, tag=tag)

    def dmaG(self, fn, dsem, r, w, tag=""):
        return self.em.dma("P", fn, dsem, r, w, tag=tag)

    def dbg(self, name, ap_fn, shape, rgns, dtype=F32):
        if name not in self.debug:
            return
        d = self.dout("dbg_" + name, shape, dtype)
        self.dmaQ(lambda e: e.dma_start(out=d, in_=ap_fn()), "ds_dbg_" + name,
                  rgns, [], tag="dbg_" + name)
        self.dbg_shapes[name] = (shape, dtype)


def build(debug=(), seg=99, subseg=99):
    nc = bass.Bass("TRN2", target_bir_lowering=False)
    mk = MK(nc, debug=debug)
    em = mk.em

    xin = mk.din("xin", [128, 8 * WH])
    wk_d = mk.din("wk", [128, 8 * CH])
    wv_d = mk.din("wv", [128, 8 * CH])
    wr_d = mk.din("wr", [128, 8 * CH])
    wo_d = mk.din("wo", [128, 8 * CH])
    fk_d = mk.din("fk", [128, 8 * 4096])
    fv_d = mk.din("fv", [128, 32 * CH])
    fr_d = mk.din("fr", [128, 8 * CH])
    vecs_d = mk.din("vecs", [128, NVEC])
    tabs_d = mk.din("tabs", [128, 4 * 2048])
    qs_d = mk.din("qst", [128, 8])
    tri_d = mk.din("tri", [128, 256])
    edge_d = mk.din("edge", [128, 2048])
    ident_d = mk.din("ident", [128, 128])
    hseg_d = mk.din("hseg", [128, 128])
    epair_d = mk.din("epair", [16, 512])
    ehset_d = mk.din("ehset", [8, 512])
    b1_d = mk.din("b1re", [128, 88])
    wqkv_d = [mk.din(f"wqkv{l}", [128, 8 * 2560]) for l in range(2)]
    woff_d = [mk.din(f"woff{l}", [128, 4 * 1024]) for l in range(2)]
    w1f_d = [mk.din(f"w1f{l}", [128, 8 * 5632]) for l in range(2)]
    w2f_d = [mk.din(f"w2f{l}", [128, 22 * 1024]) for l in range(2)]
    iota_d = mk.din("iota1", [128, TL])
    onesr_d = mk.din("onesr", [1, 128])
    onesc_d = mk.din("onesc", [128, 1])
    smask_d = mk.din("smask", [128, 1])
    out_d = mk.dout("out", [128, 8 * W])
    mk.out_d = out_d

    agi = [mk.dint(f"agi{i}", [1, 384]) for i in range(2)]
    ago = [mk.dint(f"ago{i}", [2, 384]) for i in range(2)]
    aggi = mk.dint("aggi", [1, 1024])
    aggo = mk.dint("aggo", [2, 1024])
    tok_i = [mk.dint(f"toki{j}", [1, 1024]) for j in range(4)]
    tok_o = [mk.dint(f"toko{j}", [2, 1024]) for j in range(4)]
    hal_i = [mk.dint(f"hali{l}", [1, 128 * 4096], BF16) for l in range(2)]
    hal_o = [mk.dint(f"halo{l}", [2, 128 * 4096], BF16) for l in range(2)]

    s_xi = mk.dint("s_xi", [128, 8 * W])
    s_mid = mk.dint("s_mid", [128, 8 * W])
    s_x1 = mk.dint("s_x1", [128, 8 * W])

    AR_h = nc.alloc_sbuf_tensor("AR", [128, ARW], F32)
    AR = AR_h.ap()
    ARB = AR.bitcast(BF16)
    ARH = AR.bitcast(FP16)
    VEC_h = nc.alloc_sbuf_tensor("VEC", [128, NVEC], F32)
    VEC = VEC_h.ap()
    STATE_h = nc.alloc_sbuf_tensor("STATE", [128, 64], F32)
    STATE = STATE_h.ap()
    AGS_h = nc.alloc_sbuf_tensor("AGS", [128, 64], F32)
    AGS = AGS_h.ap()
    ONESR_h = nc.alloc_sbuf_tensor("ONESR", [1, 128], F32)
    ONESR = ONESR_h.ap()
    ONESB_h = nc.alloc_sbuf_tensor("ONESB", [128, 1], BF16)
    ONESB = ONESB_h.ap()
    SMASK_h = nc.alloc_sbuf_tensor("SMASK", [128, 1], F32)
    SMASK = SMASK_h.ap()
    CONST_h = nc.alloc_sbuf_tensor("CONST", [128, 8], F32)
    CONST = CONST_h.ap()
    TRI_h = nc.alloc_sbuf_tensor("TRI", [128, 256], BF16)
    TRI = TRI_h.ap()
    EDGE_h = nc.alloc_sbuf_tensor("EDGE", [128, 2048], BF16)
    EDGE = EDGE_h.ap()
    IDENT_h = nc.alloc_sbuf_tensor("IDENT", [128, 128], BF16)
    IDENT = IDENT_h.ap()
    IDENTF_h = nc.alloc_sbuf_tensor("IDENTF", [128, 128], F32)
    IDENTF = IDENTF_h.ap()
    HSEG_h = nc.alloc_sbuf_tensor("HSEG", [128, 128], BF16)
    HSEG = HSEG_h.ap()
    EPAIR_h = nc.alloc_sbuf_tensor("EPAIR", [16, 512], F32)
    EPAIR = EPAIR_h.ap()
    EHSET_h = nc.alloc_sbuf_tensor("EHSET", [8, 512], F32)
    EHSET = EHSET_h.ap()
    QS_h = nc.alloc_sbuf_tensor("QS", [128, 8], F32)
    QS = QS_h.ap()
    SUMS_h = nc.alloc_sbuf_tensor("SUMS", [128, 32], F32)
    SUMS = SUMS_h.ap()
    B1_h = nc.alloc_sbuf_tensor("B1", [128, 88], F32)
    B1 = B1_h.ap()
    RST_h = nc.alloc_sbuf_tensor("RST", [8, 128], F32)
    RST = RST_h.ap()
    BTOK_h = nc.alloc_sbuf_tensor("BTOK", [128, 32], F32)
    XBND_h = nc.alloc_sbuf_tensor("XBND", [128, 16], F32)
    XBND = XBND_h.ap()
    BTOK = BTOK_h.ap()

    PS_h = nc.alloc_psum_tensor("PS", [128, 4096], F32)
    PS = PS_h.ap()
    mk.AR = AR

    A = Arena(em, ARW)
    PA = Arena(em, 4096)

    def arf(name, lo=0, hi=None):
        off, size = A.addr[name]
        if hi is None:
            hi = size
        return AR[:, off + lo: off + hi]

    def arb(name, lo=0, hi=None):
        off, size = A.addr[name]
        if hi is None:
            hi = 2 * size
        return ARB[:, 2 * off + lo: 2 * off + hi]

    def arh(name, lo=0, hi=None):
        off, size = A.addr[name]
        if hi is None:
            hi = 2 * size
        return ARH[:, 2 * off + lo: 2 * off + hi]

    def psf(name, lo=0, hi=None):
        off, size = PA.addr[name]
        if hi is None:
            hi = size
        return PS[:, off + lo: off + hi]

    mk.dmaQ(lambda e: e.dma_start(out=VEC_h[:, :], in_=vecs_d), "ds_vec", [], ["vec"])
    
    mk.dmaQ(lambda e: e.dma_start(out=ONESR_h[:, :], in_=onesr_d), "ds_onesr", [], ["onesr"])
    mk.dmaQ(lambda e: e.dma_start(out=SMASK_h[:, :], in_=smask_d), "ds_smask", [], ["smask"])
    mk.dmaG(lambda e: e.dma_start(out=ONESB_h[:, :], in_=onesc_d), "dsg", [], ["onesb"])

    mk.dmaG(lambda e: e.dma_start(out=TRI_h[:, :], in_=tri_d), "ds_tri", [], ["tri"])
    mk.dmaG(lambda e: e.dma_start(out=EDGE_h[:, :], in_=edge_d), "ds_edge", [], ["edge"])
    mk.dmaG(lambda e: e.dma_start(out=IDENT_h[:, :], in_=ident_d), "ds_ident", [], ["ident"])
    mk.dmaQ(lambda e: e.dma_start(out=IDENTF_h[:, :], in_=ident_d), "ds_identf", [], ["identf"])
    mk.dmaG(lambda e: e.dma_start(out=HSEG_h[:, :], in_=hseg_d), "ds_hseg", [], ["hseg"])
    mk.dmaQ(lambda e: e.dma_start(out=EPAIR_h[:, :], in_=epair_d), "ds_epair", [], ["epair"])
    mk.dmaQ(lambda e: e.dma_start(out=EHSET_h[:, :], in_=ehset_d), "ds_ehset", [], ["ehset"])
    mk.dmaQ(lambda e: e.dma_start(out=QS_h[:, :], in_=qs_d), "ds_qs", [], ["qs"])
    mk.dmaQ(lambda e: e.dma_start(out=B1_h[:, :], in_=b1_d), "ds_b1", [], ["b1"])

    for ci, cv in ((0, EPS), (1, EPS + EPS * EPS), (2, 1e-30), (3, 1.0 + EPS), (4, 1.0)):
        def cf(e, ci=ci, cv=cv):
            return e.memset(CONST[:, ci:ci + 1], cv)
        mk.V(cf, [], ["const"], tag="const")

    mk.S(lambda e: e.activation(out=STATE[:, SC_W:SC_W + 8], in_=VEC[:, VTD:VTD + 8],
                                func=AF.Exp), ["vec"], ["st_w"])
    mk.V(lambda e: e.tensor_scalar_mul(out=STATE[:, SC_W:SC_W + 8],
                                       in0=STATE[:, SC_W:SC_W + 8], scalar1=-1.0),
         ["st_w"], ["st_w"])
    mk.S(lambda e: e.activation(out=STATE[:, SC_LAM:SC_LAM + 8],
                                in_=STATE[:, SC_W:SC_W + 8], func=AF.Exp),
         ["st_w"], ["st_lam"])
    mk.S(lambda e: e.activation(out=STATE[:, SC_EU:SC_EU + 8], in_=VEC[:, VTF:VTF + 8],
                                func=AF.Exp), ["vec"], ["st_eu"])

    def ln_stats(src_name, iw, wins, fold2=False, tag="ln", cast_name=None):
        n = wins[-1][1]
        sA, sB, sC, sD = (tag + "sA", tag + "sB", tag + "sC", tag + "sD")
        A.alloc(sA, n)
        A.alloc(sB, n)
        A.alloc(sC, n)
        A.alloc(sD, n)
        def row(rn, lo=0, hi=None):
            return arf({"skrA": sA, "skrB": sB, "skrC": sC, "skrD": sD}[rn], lo, hi)[0:1, :]
        if cast_name is None:
            A.alloc(tag + "_cast", (8 * iw + 1) // 2)
            rc = tag + "_cast"
            mk.P(lambda e: e.tensor_copy(out=arb(rc, 0, 8 * iw), in_=arf(src_name)),
                 [src_name], [rc], tag=tag + "cast")
        else:
            rc = cast_name
        A.alloc(tag + "_sq", (8 * iw + 1) // 2)
        rs = tag + "_sq"
        mk.S(lambda e: e.activation(out=arb(rs, 0, 8 * iw), in_=arb(rc, 0, 8 * iw),
                                    func=AF.Square), [rc], [rs], tag=tag + "sq")
        PA.alloc(tag + "_psum", 512)
        PA.alloc(tag + "_psq", 512)
        for (lo, hi) in wins:
            wl = hi - lo
            for c in range(8):
                st, sp = (c == 0), (c == 7)
                def f1(e, c=c, lo=lo, hi=hi, st=st, sp=sp, wl=wl):
                    return e.matmul(psf(tag + "_psum", 0, wl)[0:1, :], ONESB,
                                    arb(rc, c * iw + lo, c * iw + hi),
                                    start=st, stop=sp)
                mk.T(f1, [rc, "onesb"], [tag + "_psum"], tag=tag + "smm")
                def f2(e, c=c, lo=lo, hi=hi, st=st, sp=sp, wl=wl):
                    return e.matmul(psf(tag + "_psq", 0, wl)[0:1, :], ONESB,
                                    arb(rs, c * iw + lo, c * iw + hi),
                                    start=st, stop=sp)
                mk.T(f2, [rs, "onesb"], [tag + "_psq"], tag=tag + "qmm")
            def e1(e, lo=lo, wl=hi - lo):
                return e.tensor_copy(out=row("skrA", lo, lo + wl),
                                     in_=psf(tag + "_psum", 0, wl)[0:1, :])
            mk.V(e1, [tag + "_psum"], [tag + "_sum"], tag=tag + "sev")
            def e2(e, lo=lo, wl=hi - lo):
                return e.tensor_copy(out=row("skrB", lo, lo + wl),
                                     in_=psf(tag + "_psq", 0, wl)[0:1, :])
            mk.V(e2, [tag + "_psq"], [tag + "_sq_r"], tag=tag + "qev")
        PA.free(tag + "_psum")
        PA.free(tag + "_psq")
        if cast_name is None:
            A.free(rc)
        A.free(rs)
        mk.V(lambda e: e.tensor_scalar_mul(out=row("skrA"), in0=row("skrA"),
                                           scalar1=1.0 / CH), [tag + "_sum"], [tag + "_sum"])
        mk.V(lambda e: e.tensor_scalar_mul(out=row("skrB"), in0=row("skrB"),
                                           scalar1=1.0 / CH), [tag + "_sq_r"], [tag + "_sq_r"])
        mk.S(lambda e: e.activation(out=row("skrD"), in_=row("skrA"), func=AF.Square),
             [tag + "_sum"], [tag + "_mr1"])
        mk.V(lambda e: e.tensor_sub(out=row("skrB"), in0=row("skrB"), in1=row("skrD")),
             [tag + "_sq_r", tag + "_mr1"], [tag + "_sq_r"])
        if fold2:
            mk.S(lambda e: e.activation(out=row("skrC"), in_=row("skrB"), func=AF.Ln,
                                        scale=CONST[0:1, 3:4], bias=CONST[0:1, 1:2]),
                 [tag + "_sq_r", "const"], [tag + "_r2"])
            mk.S(lambda e: e.activation(out=row("skrC"), in_=row("skrC"), func=AF.Exp,
                                        scale=-0.5), [tag + "_r2"], [tag + "_r2"])
        mk.S(lambda e: e.activation(out=row("skrB"), in_=row("skrB"), func=AF.Ln,
                                    bias=CONST[0:1, 0:1]), [tag + "_sq_r", "const"], [tag + "_r1"])
        mk.S(lambda e: e.activation(out=row("skrB"), in_=row("skrB"), func=AF.Exp,
                                    scale=-0.5), [tag + "_r1"], [tag + "_r1"])
        mk.V(lambda e: e.tensor_mul(out=row("skrD"), in0=row("skrA"), in1=row("skrB")),
             [tag + "_sum", tag + "_r1"], [tag + "_mr1"])
        if fold2:
            mk.V(lambda e: e.tensor_mul(out=row("skrA"), in0=row("skrA"), in1=row("skrC")),
                 [tag + "_sum", tag + "_r2"], [tag + "_mr2"])
        else:
            A.free(sA, [tag + "_sum", tag + "_mr2"])
            A.free(sC, [tag + "_r2"])

    def bcast_row(tag, which, lo, wl, pname):
        rn = {"R1": tag + "sB", "R2": tag + "sC", "MR1": tag + "sD", "MR2": tag + "sA"}[which]
        rk = {"R1": tag + "_r1", "R2": tag + "_r2", "MR1": tag + "_mr1", "MR2": tag + "_mr2"}[which]
        def f(e):
            return e.matmul(psf(pname, 0, wl), ONESR,
                            arf(rn, lo, lo + wl)[0:1, :], start=True, stop=True)
        mk.T(f, [rk, "onesr"], [pname], tag="bcast")

    A.alloc("x0", 8 * WH)
    mk.dmaQ(lambda e: e.dma_start(out=arf("x0"), in_=xin), "ds_x0", [], ["x0"])
    ln_stats("x0", WH, WINS, fold2=True, tag="ln1")
    A.alloc("h", 8 * WH)
    A.alloc("xiw", 8 * 512)
    p1 = PA.alloc("bc1", 512)
    p2 = PA.alloc("bc2", 512)
    p3 = PA.alloc("bc3", 512)
    p4 = PA.alloc("bc4", 512)
    for wi, (lo, hi) in enumerate(WINS):
        wl = hi - lo
        bcast_row("ln1", "R1", lo, wl, "bc1")
        bcast_row("ln1", "MR1", lo, wl, "bc2")
        bcast_row("ln1", "R2", lo, wl, "bc3")
        bcast_row("ln1", "MR2", lo, wl, "bc4")
        for c in range(8):
            if lo >= 1:
                def a1(e, c=c, lo=lo, wl=wl):
                    return e.tensor_mul(out=arf("xiw", c * 512, c * 512 + wl),
                                        in0=arf("x0", c * WH + lo, c * WH + lo + wl),
                                        in1=psf("bc1", 0, wl))
                mk.V(a1, ["x0", "bc1"], ["xiw"], tag="xiap1")
                def a2(e, c=c, wl=wl):
                    return e.tensor_sub(out=arf("xiw", c * 512, c * 512 + wl),
                                        in0=arf("xiw", c * 512, c * 512 + wl),
                                        in1=psf("bc2", 0, wl))
                mk.V(a2, ["bc2"], ["xiw"], tag="xiap2")
            def a3(e, c=c, lo=lo, wl=wl):
                return e.tensor_mul(out=arf("h", c * WH + lo, c * WH + lo + wl),
                                    in0=arf("x0", c * WH + lo, c * WH + lo + wl),
                                    in1=psf("bc3", 0, wl))
            mk.V(a3, ["x0", "bc3"], ["h"], tag="hap1")
            def a4(e, c=c, lo=lo, wl=wl):
                return e.tensor_sub(out=arf("h", c * WH + lo, c * WH + lo + wl),
                                    in0=arf("h", c * WH + lo, c * WH + lo + wl),
                                    in1=psf("bc4", 0, wl))
            mk.V(a4, ["bc4"], ["h"], tag="hap2")
            def a5(e, c=c, lo=lo, wl=wl):
                return e.tensor_scalar(out=arf("h", c * WH + lo, c * WH + lo + wl),
                                       in0=arf("h", c * WH + lo, c * WH + lo + wl),
                                       scalar1=VEC[:, VG1 + c:VG1 + c + 1],
                                       scalar2=VEC[:, VB1 + c:VB1 + c + 1],
                                       op0=ALU.mult, op1=ALU.add)
            mk.P(a5, ["vec"], ["h"], tag="hap3")
        if lo >= 1:
            def sp(e, lo=lo, wl=wl):
                pat = [[8 * W, 128]] + [[W, 8], [1, wl]]
                return e.dma_start(
                    out=bass.AP(s_xi.tensor, lo - 1, pat),
                    in_=AR[:, A.off("xiw"):A.off("xiw") + 8 * 512].rearrange(
                        "p (c t) -> p c t", c=8)[:, :, 0:wl])
            mk.dmaQ(sp, "ds_sxi", ["xiw"], ["s_xi"], tag="spillxi")
    for nm in ("bc1", "bc2", "bc3", "bc4"):
        PA.free(nm)
    for c in range(8):
        def hz(e, c=c):
            return e.tensor_scalar_mul(out=arf("h", c * WH, c * WH + 1),
                                       in0=arf("h", c * WH, c * WH + 1),
                                       scalar1=SMASK[:, 0:1])
        mk.V(hz, ["smask"], ["h"], tag="hzero")
    A.free("x0")
    A.free("xiw")
    A.free("ln1sA", ["ln1_sum", "ln1_mr2"])
    A.free("ln1sB", ["ln1_sq_r", "ln1_r1"])
    A.free("ln1sC", ["ln1_r2"])
    A.free("ln1sD", ["ln1_mr1"])
    mk.dbg("h", lambda: arf("h"), [128, 8 * WH], ["h"])
    if seg < 2:
        return finish(nc, mk, A, PA)

    A.alloc("mxk", 8 * W // 2)
    A.alloc("mxv", 8 * W // 2)
    A.alloc("mxr", 8 * W // 2)
    A.alloc("dmix", W)
    for c in range(8):
        def d1(e, c=c):
            return e.tensor_sub(out=arf("dmix"),
                                in0=arf("h", c * WH + 1, c * WH + WH),
                                in1=arf("h", c * WH, c * WH + W))
        mk.V(d1, ["h"], ["dmix"], tag="dmix")
        for nm, vcol in (("mxr", VTMR), ("mxk", VTMK), ("mxv", VTMV)):
            def mx(e, c=c, nm=nm, vcol=vcol):
                return e.scalar_tensor_tensor(
                    out=arb(nm, c * W, c * W + W), in0=arf("dmix"),
                    scalar=VEC[:, vcol + c:vcol + c + 1],
                    in1=arf("h", c * WH, c * WH + W),
                    op0=ALU.mult, op1=ALU.add)
            mk.V(mx, ["dmix", "h", "vec"], [nm], tag=nm)
    A.free("dmix")
    A.free("h")
    mk.dbg("xk", lambda: arb("mxk"), [128, 8 * W], ["mxk"], BF16)

    A.alloc("wg0", 4096)
    A.alloc("wg1", 4096)
    mk.dmaG(lambda e: e.dma_start(out=arb("wg0"), in_=wr_d), "pb_wg0", [], ["wg0"])

    def mm_img(pname, plo, wname, m, rhs_name, rhs_iw, rlo, rhi, nk=8, tag="mm"):
        for k in range(nk):
            st, sp = (k == 0), (k == nk - 1)
            def f(e, k=k, st=st, sp=sp):
                return e.matmul(psf(pname, plo, plo + (rhi - rlo)),
                                arb(wname, k * CH + m * 128, k * CH + m * 128 + 128),
                                arb(rhs_name, k * rhs_iw + rlo, k * rhs_iw + rhi),
                                start=st, stop=sp)
            mk.T(f, [wname, rhs_name], [pname], tag=tag)

    A.alloc("sry", 8 * W // 2)
    PA.alloc("rp0", 512)
    PA.alloc("rp1", 512)
    for m in range(8):
        for wi, (lo, hi) in enumerate(WINS_W):
            pn = "rp%d" % (wi % 2)
            mm_img(pn, 0, "wg0", m, "mxr", W, lo, hi, tag="rmm")
            def sev(e, m=m, lo=lo, hi=hi, pn=pn):
                return e.activation(out=arh("sry", m * W + lo, m * W + hi),
                                    in_=psf(pn, 0, hi - lo), func=AF.Sigmoid)
            mk.S(sev, [pn], ["sry%d" % m], tag="srev")
    PA.free("rp0")
    PA.free("rp1")
    A.free("mxr")
    mk.dbg("sr", lambda: arh("sry"), [128, 8 * W], [f"sry{m}" for m in range(8)], FP16)

    mk.dmaG(lambda e: e.dma_start(out=arb("wg0"), in_=wk_d), "pb_wg0", [], ["wg0"])
    mk.dmaG(lambda e: e.dma_start(out=arb("wg1"), in_=wv_d), "pb_wg1", [], ["wg1"])

    for i in range(1):
        A.alloc(f"kv{i}", 2 * W)
        A.alloc(f"ab{i}", 2 * WH)
    A.alloc("nd0", 2 * W)
    A.alloc("nd1", 2 * W)
    PA.alloc("pwp", 2048)
    PA.alloc("kp0", 512)
    PA.alloc("kp1", 512)

    NSET = 1
    for m in range(8):
        i = m % NSET
        kv, ab, nd = f"kv{i}", f"ab{i}", f"nd{m % 2}"
        rk, rv = f"k{m}", f"v{m}"
        rab, rnd = f"ab{m}", f"nd{m}"
        if m >= NSET:
            em.inherit(rk, [f"k{m-NSET}"])
            em.inherit(rv, [f"v{m-NSET}"])
            em.inherit(rab, [f"ab{m-NSET}"])
        if m >= 2:
            em.inherit(rnd, [f"nd{m-2}"])
        for wi, (lo, hi) in enumerate(WINS_W):
            pn = "kp%d" % (wi % 2)
            mm_img(pn, 0, "wg0", m, "mxk", W, lo, hi, tag="kmm")
            def kev(e, lo=lo, hi=hi, pn=pn, kv=kv):
                return e.copy(out=arf(kv, lo, hi), in_=psf(pn, 0, hi - lo))
            mk.S(kev, [pn], [rk], tag="kev")
            mm_img(pn, 0, "wg1", m, "mxv", W, lo, hi, tag="vmm")
            def vev(e, lo=lo, hi=hi, pn=pn, kv=kv):
                return e.copy(out=arf(kv, W + lo, W + hi), in_=psf(pn, 0, hi - lo))
            mk.S(vev, [pn], [rv], tag="vev")
        def nmx(e, kv=kv, m=m):
            return e.reduce_max(out=STATE[:, SC_NK0 + m:SC_NK0 + m + 1],
                                in_=arf(kv, 0, W), axis=AX.X, negate=True)
        mk.V(nmx, [rk], [f"nk0_{m}"], tag="nk0")
        def ekf(e, kv=kv, m=m):
            return e.activation(out=arf(kv, 0, W), in_=arf(kv, 0, W), func=AF.Exp,
                                bias=STATE[:, SC_NK0 + m:SC_NK0 + m + 1])
        mk.S(ekf, [rk, f"nk0_{m}"], [rk], tag="ek")
        def ekvf(e, kv=kv):
            return e.tensor_mul(out=arf(kv, W, 2 * W), in0=arf(kv, W, 2 * W),
                                in1=arf(kv, 0, W))
        mk.V(ekvf, [rk, rv], [rv], tag="ekv")
        def z0(e, ab=ab):
            return e.memset(arf(ab, 0, 1), 0.0)
        mk.V(z0, [], [rab], tag="scz")
        def z1(e, ab=ab):
            return e.memset(arf(ab, WH, WH + 1), 0.0)
        mk.V(z1, [], [rab], tag="scz")
        lam_ap = bass.AP(STATE.tensor, SC_LAM + m, [[64, 128], [0, W]])
        def sca(e, ab=ab, kv=kv, lam_ap=lam_ap):
            return e.tensor_tensor_scan(out=arf(ab, 1, WH), data0=lam_ap,
                                        data1=arf(kv, W, 2 * W), initial=0.0,
                                        op0=ALU.mult, op1=ALU.add)
        mk.V(sca, [rv, "st_lam"], [rab], tag="scanA")
        def scb(e, ab=ab, kv=kv, lam_ap=lam_ap):
            return e.tensor_tensor_scan(out=arf(ab, WH + 1, 2 * WH), data0=lam_ap,
                                        data1=arf(kv, 0, W), initial=0.0,
                                        op0=ALU.mult, op1=ALU.add)
        mk.V(scb, [rk, "st_lam"], [rab], tag="scanB")
        ia = m % 2
        if m >= 2:
            em.inherit(f"agx{m}", [f"agx{m-2}"])
            em.inherit(f"agr2_{m}", [f"agr2_{m-2}"])
        def ex1(e, ab=ab, ia=ia):
            return e.tensor_copy(out=AGS[:, ia * 3:ia * 3 + 1], in_=arf(ab, W, WH))
        mk.V(ex1, [rab], [f"agx{m}"], tag="exA")
        def ex2(e, ab=ab, ia=ia):
            return e.tensor_copy(out=AGS[:, ia * 3 + 1:ia * 3 + 2], in_=arf(ab, WH + W, 2 * WH))
        mk.V(ex2, [rab], [f"agx{m}"], tag="exB")
        def ex3(e, ia=ia, m=m):
            return e.tensor_copy(out=AGS[:, ia * 3 + 2:ia * 3 + 3],
                                 in_=STATE[:, SC_NK0 + m:SC_NK0 + m + 1])
        mk.V(ex3, [f"nk0_{m}"], [f"agx{m}"], tag="exK")
        def nml(e, nd=nd, kv=kv, ab=ab, m=m):
            return e.scalar_tensor_tensor(out=arf(nd, 0, W), in0=arf(kv, W, 2 * W),
                                          scalar=STATE[:, SC_EU + m:SC_EU + m + 1],
                                          in1=arf(ab, 0, W), op0=ALU.mult, op1=ALU.add)
        mk.V(nml, [rv, rab, "st_eu"], [rnd], tag="num")
        def dnl(e, nd=nd, kv=kv, ab=ab, m=m):
            return e.scalar_tensor_tensor(out=arf(nd, W, 2 * W), in0=arf(kv, 0, W),
                                          scalar=STATE[:, SC_EU + m:SC_EU + m + 1],
                                          in1=arf(ab, WH, WH + W), op0=ALU.mult, op1=ALU.add)
        mk.V(dnl, [rk, rab, "st_eu"], [rnd], tag="den")
        ia = m % 2
        if m >= 2:
            em.inherit(f"agi_{m}", [f"agi_{m-2}"])
            em.inherit(f"ago_{m}", [f"ago_{m-2}"])
            em.inherit(f"agr_{m}", [f"agr_{m-2}"])
        def dout_f(e, ia=ia):
            return e.dma_start(out=bass.AP(agi[ia].tensor, 0, [[3, 128], [1, 3]]),
                               in_=AGS[:, ia * 3:ia * 3 + 3])
        mk.dmaG(dout_f, f"ds_agi{ia}", [f"agx{m}"], [f"agi_{m}"], tag="agdma")
        def ccf(e, ia=ia):
            return e.collective_compute(
                "AllGather", ALU.bypass,
                replica_groups=[[0, 1], [2, 3], [4, 5], [6, 7]],
                ins=[agi[ia].opt()], outs=[ago[ia].opt()])
        em.cc(ccf, f"cs_ag{ia}", [f"agi_{m}"], [f"ago_{m}"], tag="agcc")
        def din_f(e, ia=ia):
            return e.dma_start(out=AGS[:, 32 + ia * 3:32 + ia * 3 + 3],
                               in_=bass.AP(ago[ia].tensor, 0, [[3, 128], [1, 3]]))
        mk.dmaG(din_f, f"ds_ago{ia}", [f"ago_{m}"], [f"agr2_{m}"], tag="agback")
        def sp1(e, ia=ia, m=m):
            return e.tensor_sub(out=STATE[:, SC_TMP + 2:SC_TMP + 3],
                                in0=STATE[:, SC_NK0 + m:SC_NK0 + m + 1],
                                in1=AGS[:, 32 + ia * 3 + 2:32 + ia * 3 + 3])
        mk.V(sp1, [f"agr2_{m}", f"nk0_{m}"], ["st_f"], tag="stf1")
        mk.S(lambda e: e.activation(out=STATE[:, SC_TMP + 2:SC_TMP + 3],
                                    in_=STATE[:, SC_TMP + 2:SC_TMP + 3], func=AF.Exp),
             ["st_f"], ["st_f"], tag="stf2")
        mk.V(lambda e: e.tensor_mul(out=STATE[:, SC_TMP + 2:SC_TMP + 3],
                                    in0=STATE[:, SC_TMP + 2:SC_TMP + 3],
                                    in1=SMASK[:, 0:1]),
             ["st_f", "smask"], ["st_f"], tag="stf3")
        def spA(e, ia=ia):
            return e.tensor_mul(out=STATE[:, SC_TMP:SC_TMP + 1],
                                in0=AGS[:, 32 + ia * 3:32 + ia * 3 + 1],
                                in1=STATE[:, SC_TMP + 2:SC_TMP + 3])
        mk.V(spA, [f"agr2_{m}", "st_f"], ["st_ab"], tag="stA")
        def spB(e, ia=ia):
            return e.tensor_mul(out=STATE[:, SC_TMP + 1:SC_TMP + 2],
                                in0=AGS[:, 32 + ia * 3 + 1:32 + ia * 3 + 2],
                                in1=STATE[:, SC_TMP + 2:SC_TMP + 3])
        mk.V(spB, [f"agr2_{m}", "st_f"], ["st_ab"], tag="stB")
        lam_b2 = bass.AP(STATE.tensor, SC_LAM + m, [[64, 128], [0, W]])
        one_b = bass.AP(CONST.tensor, 4, [[8, 128], [0, W]])
        def pwf(e, m=m, lam_b2=lam_b2, one_b=one_b):
            return e.tensor_tensor_scan(out=psf("pwp", 0, 2048), data0=lam_b2,
                                        data1=one_b, initial=1.0,
                                        op0=ALU.mult, op1=ALU.mult)
        mk.V(pwf, ["st_lam", "const"], ["pw"], tag="pw")
        def c0(e, nd=nd):
            return e.tensor_add(out=arf(nd, 0, 1), in0=arf(nd, 0, 1),
                                in1=STATE[:, SC_TMP:SC_TMP + 1])
        mk.V(c0, ["st_ab"], [rnd], tag="corr")
        def c1(e, nd=nd):
            return e.scalar_tensor_tensor(out=arf(nd, 1, W), in0=psf("pwp", 0, W - 1),
                                          scalar=STATE[:, SC_TMP:SC_TMP + 1],
                                          in1=arf(nd, 1, W), op0=ALU.mult, op1=ALU.add)
        mk.V(c1, ["pw", "st_ab"], [rnd], tag="corr")
        def c2(e, nd=nd):
            return e.tensor_add(out=arf(nd, W, W + 1), in0=arf(nd, W, W + 1),
                                in1=STATE[:, SC_TMP + 1:SC_TMP + 2])
        mk.V(c2, ["st_ab"], [rnd], tag="corr")
        def c3(e, nd=nd):
            return e.scalar_tensor_tensor(out=arf(nd, W + 1, 2 * W), in0=psf("pwp", 0, W - 1),
                                          scalar=STATE[:, SC_TMP + 1:SC_TMP + 2],
                                          in1=arf(nd, W + 1, 2 * W), op0=ALU.mult, op1=ALU.add)
        mk.V(c3, ["pw", "st_ab"], [rnd], tag="corr")
        def rc_(e, nd=nd):
            return e.reciprocal(out=arf(nd, W, 2 * W), in_=arf(nd, W, 2 * W))
        mk.V(rc_, [rnd], [rnd], tag="recip")
        def ym(e, nd=nd):
            return e.tensor_mul(out=arf(nd, 0, W), in0=arf(nd, 0, W),
                                in1=arf(nd, W, 2 * W))
        mk.V(ym, [rnd], [rnd], tag="ymul")
        def ys(e, nd=nd, m=m):
            return e.tensor_mul(out=arb("sry", m * W, (m + 1) * W), in0=arf(nd, 0, W),
                                in1=arh("sry", m * W, (m + 1) * W))
        mk.V(ys, [rnd], [f"y{m}", f"sry{m}"], tag="ysr")
    for i in range(1):
        A.free(f"kv{i}", [f"k{m}" for m in range(8)] + [f"v{m}" for m in range(8)])
        A.free(f"ab{i}", [f"ab{m}" for m in range(8)])
    A.free("nd0", [f"nd{m}" for m in range(0, 8, 2)])
    A.free("nd1", [f"nd{m}" for m in range(1, 8, 2)])
    PA.free("pwp")
    A.free("mxk")
    A.free("mxv")
    PA.free("kp0")
    PA.free("kp1")
    mk.dbg("wkvy", lambda: arb("sry"), [128, 8 * W],
           [f"y{m}" for m in range(8)], BF16)
    if seg < 3:
        return finish(nc, mk, A, PA)

    mk.dmaG(lambda e: e.dma_start(out=arb("wg0"), in_=wo_d), "pb_wg0", [], ["wg0"])
    A.alloc("xi2", 8 * W)
    mk.dmaQ(lambda e: e.dma_start(out=arf("xi2"), in_=s_xi.tensor.ap()), "ds_sxi2",
            ["s_xi"], ["xi2"], tag="xilod")
    PA.alloc("wop0", 512)
    PA.alloc("wop1", 512)
    yrd = [f"y{m}" for m in range(8)]
    for mo in range(8):
        for wi, (lo, hi) in enumerate(WINS_W):
            pn = "wop%d" % (wi % 2)
            for k in range(8):
                st, sp = (k == 0), (k == 7)
                def f(e, k=k, mo=mo, lo=lo, hi=hi, st=st, sp=sp, pn=pn):
                    return e.matmul(psf(pn, 0, hi - lo),
                                    arb("wg0", k * CH + mo * 128, k * CH + mo * 128 + 128),
                                    arb("sry", k * W + lo, k * W + hi),
                                    start=st, stop=sp)
                mk.T(f, ["wg0"] + yrd, [pn], tag="womm")
            def ev(e, mo=mo, lo=lo, hi=hi, pn=pn):
                return e.tensor_add(out=arf("xi2", mo * W + lo, mo * W + hi),
                                    in0=arf("xi2", mo * W + lo, mo * W + hi),
                                    in1=psf(pn, 0, hi - lo))
            mk.V(ev, [pn, "xi2"], ["xi2"], tag="woev")
    PA.free("wop0")
    PA.free("wop1")
    A.free("sry", [f"sry{m}" for m in range(8)] + [f"y{m}" for m in range(8)])
    A.free("wg0")
    A.free("wg1")
    mk.dbg("xmid", lambda: arf("xi2"), [128, 8 * W], ["xi2"])
    mk.dmaQ(lambda e: e.dma_start(out=s_mid.tensor.ap(), in_=arf("xi2")), "ds_smid",
            ["xi2"], ["s_mid"], tag="midspill")
    ln_stats("xi2", W, WINS_W, fold2=False, tag="ln2")
    if seg < 4:
        return finish(nc, mk, A, PA)

    def stats_win(srcname, off, wl, tag):
        A.alloc(tag + "B", wl)
        A.alloc(tag + "D", wl)
        cb = tag + "_c"
        A.alloc(cb, (8 * wl + 1) // 2)
        def cst(e):
            return e.tensor_copy(out=arb(cb, 0, 8 * wl),
                                 in_=arf(srcname, off, off + 8 * wl))
        mk.P(cst, [srcname], [cb], tag=tag + "c")
        PA.alloc(tag + "_ps", 512)
        PA.alloc(tag + "_pq", 512)
        sqb = tag + "_s"
        A.alloc(sqb, (8 * wl + 1) // 2)
        def sqf(e):
            return e.activation(out=arb(sqb, 0, 8 * wl), in_=arb(cb, 0, 8 * wl),
                                func=AF.Square)
        mk.S(sqf, [cb], [sqb], tag=tag + "q")
        for c in range(8):
            st, sp = (c == 0), (c == 7)
            def f1(e, c=c, st=st, sp=sp):
                return e.matmul(psf(tag + "_ps", 0, wl)[0:1, :], ONESB,
                                arb(cb, c * wl, (c + 1) * wl), start=st, stop=sp)
            mk.T(f1, [cb, "onesb"], [tag + "_ps"], tag=tag + "m1")
            def f2(e, c=c, st=st, sp=sp):
                return e.matmul(psf(tag + "_pq", 0, wl)[0:1, :], ONESB,
                                arb(sqb, c * wl, (c + 1) * wl), start=st, stop=sp)
            mk.T(f2, [sqb, "onesb"], [tag + "_pq"], tag=tag + "m2")
        def rw(nm, lo=0, hi=None):
            return arf(nm, lo, hi)[0:1, :]
        def e1(e):
            return e.tensor_scalar_mul(out=rw(tag + "D"), in0=psf(tag + "_ps", 0, wl)[0:1, :],
                                       scalar1=1.0 / CH)
        mk.V(e1, [tag + "_ps"], [tag + "_mean"], tag=tag + "e1")
        def e2(e):
            return e.tensor_scalar_mul(out=rw(tag + "B"), in0=psf(tag + "_pq", 0, wl)[0:1, :],
                                       scalar1=1.0 / CH)
        mk.V(e2, [tag + "_pq"], [tag + "_ex2"], tag=tag + "e2")
        PA.free(tag + "_ps")
        PA.free(tag + "_pq")
        A.free(cb)
        A.free(sqb)
        A.alloc(tag + "T", wl)
        def e3(e):
            return e.activation(out=rw(tag + "T"), in_=rw(tag + "D"), func=AF.Square)
        mk.S(e3, [tag + "_mean"], [tag + "_msq"], tag=tag + "e3")
        def e4(e):
            return e.tensor_sub(out=rw(tag + "B"), in0=rw(tag + "B"), in1=rw(tag + "T"))
        mk.V(e4, [tag + "_ex2", tag + "_msq"], [tag + "_var"], tag=tag + "e4")
        def e5(e):
            return e.activation(out=rw(tag + "B"), in_=rw(tag + "B"), func=AF.Ln,
                                bias=CONST[0:1, 0:1])
        mk.S(e5, [tag + "_var", "const"], [tag + "_r"], tag=tag + "e5")
        def e6(e):
            return e.activation(out=rw(tag + "B"), in_=rw(tag + "B"), func=AF.Exp,
                                scale=-0.5)
        mk.S(e6, [tag + "_r"], [tag + "_r"], tag=tag + "e6")
        def e7(e):
            return e.tensor_mul(out=rw(tag + "D"), in0=rw(tag + "D"), in1=rw(tag + "B"))
        mk.V(e7, [tag + "_mean", tag + "_r"], [tag + "_mr"], tag=tag + "e7")
        A.free(tag + "T", [tag + "_msq"])
        return tag + "B", tag + "D", tag + "_r", tag + "_mr"

    def bc2(tag, pn1, pn2, wl, lo=0):
        def f1(e):
            return e.matmul(psf(pn1, 0, wl), ONESR, arf(tag + "B", lo, lo + wl)[0:1, :],
                            start=True, stop=True)
        mk.T(f1, [tag + "_r", "onesr"], [pn1], tag="bc2a")
        def f2(e):
            return e.matmul(psf(pn2, 0, wl), ONESR, arf(tag + "D", lo, lo + wl)[0:1, :],
                            start=True, stop=True)
        mk.T(f2, [tag + "_mr", "onesr"], [pn2], tag="bc2b")

    PA.alloc("gbt1", 512)
    PA.alloc("gbt2", 512)
    def gb1(e):
        return e.matmul(psf("gbt1", 0, 1), ONESR, arf("ln2sB", 2047, 2048)[0:1, :],
                        start=True, stop=True)
    mk.T(gb1, ["ln2_r1", "onesr"], ["gbt1"], tag="gbt")
    def gb2(e):
        return e.matmul(psf("gbt2", 0, 1), ONESR, arf("ln2sD", 2047, 2048)[0:1, :],
                        start=True, stop=True)
    mk.T(gb2, ["ln2_mr1", "onesr"], ["gbt2"], tag="gbt")
    for c in range(8):
        def g1(e, c=c):
            return e.tensor_mul(out=BTOK[:, c:c + 1],
                                in0=arf("xi2", c * W + 2047, c * W + 2048),
                                in1=psf("gbt1", 0, 1))
        mk.V(g1, ["xi2", "gbt1"], ["btok"], tag="gbt")
        def g2(e, c=c):
            return e.tensor_sub(out=BTOK[:, c:c + 1], in0=BTOK[:, c:c + 1],
                                in1=psf("gbt2", 0, 1))
        mk.V(g2, ["gbt2"], ["btok"], tag="gbt")
        def g3(e, c=c):
            return e.tensor_scalar(out=BTOK[:, c:c + 1], in0=BTOK[:, c:c + 1],
                                   scalar1=VEC[:, VG2 + c:VG2 + c + 1],
                                   scalar2=VEC[:, VB2 + c:VB2 + c + 1],
                                   op0=ALU.mult, op1=ALU.add)
        mk.V(g3, ["vec"], ["btok"], tag="gbt")
    PA.free("gbt1")
    PA.free("gbt2")
    mk.dmaG(lambda e: e.dma_start(out=bass.AP(aggi.tensor, 0, [[8, 128], [1, 8]]),
                                  in_=BTOK[:, 0:8]), "ds_aggi", ["btok"], ["aggi"], tag="gtokd")
    em.cc(lambda e: e.collective_compute("AllGather", ALU.bypass,
                                         replica_groups=[[0, 1], [2, 3], [4, 5], [6, 7]],
                                         ins=[aggi.opt()], outs=[aggo.opt()]),
          "cs_gtok", ["aggi"], ["aggo"], tag="gtokc")
    mk.dmaG(lambda e: e.dma_start(out=BTOK[:, 8:16],
                                  in_=bass.AP(aggo.tensor, 0, [[8, 128], [1, 8]])),
            "ds_aggo", ["aggo"], ["btokr"], tag="gtokr")
    A.free("xi2")

    A.alloc("fvw", 16384)
    A.alloc("frw", 4096)
    mk.dmaG(lambda e: e.dma_start(out=arb("fvw"), in_=fv_d), "pb_fvw", [], ["fvw"])
    mk.dmaG(lambda e: e.dma_start(out=arb("frw"), in_=fr_d), "pb_frw", [], ["frw"])

    for wi, (lo, hi) in enumerate(WINS_W):
        wl = 512
        xmw, gw = f"xmw{wi}", f"gw{wi}"
        A.alloc(xmw, 8 * wl)
        A.alloc(gw, 8 * 513)
        def xml(e, lo=lo, xmw=xmw):
            return e.dma_start(
                out=AR[:, A.off(xmw):A.off(xmw) + 8 * 512].rearrange(
                    "p (c t) -> p c t", c=8),
                in_=bass.AP(s_mid.tensor, lo, [[8 * W, 128], [W, 8], [1, 512]]))
        mk.dmaQ(xml, f"ds_{xmw}", ["s_mid"], [xmw], tag="xml")
        PA.alloc(f"gwb1_{wi}", 512)
        PA.alloc(f"gwb2_{wi}", 512)
        def b1f(e, lo=lo, wi=wi):
            return e.matmul(psf(f"gwb1_{wi}", 0, 512), ONESR,
                            arf("ln2sB", lo, lo + 512)[0:1, :], start=True, stop=True)
        mk.T(b1f, ["ln2_r1", "onesr"], [f"gwb1_{wi}"], tag="gwb")
        def b2f(e, lo=lo, wi=wi):
            return e.matmul(psf(f"gwb2_{wi}", 0, 512), ONESR,
                            arf("ln2sD", lo, lo + 512)[0:1, :], start=True, stop=True)
        mk.T(b2f, ["ln2_mr1", "onesr"], [f"gwb2_{wi}"], tag="gwb")
        for c in range(8):
            def a1(e, c=c, xmw=xmw, gw=gw, wi=wi):
                return e.tensor_mul(out=arf(gw, c * 513 + 1, c * 513 + 513),
                                    in0=arf(xmw, c * 512, c * 512 + 512),
                                    in1=psf(f"gwb1_{wi}", 0, 512))
            mk.V(a1, [xmw, f"gwb1_{wi}"], [gw], tag="gap1")
            def a2(e, c=c, gw=gw, wi=wi):
                return e.tensor_sub(out=arf(gw, c * 513 + 1, c * 513 + 513),
                                    in0=arf(gw, c * 513 + 1, c * 513 + 513),
                                    in1=psf(f"gwb2_{wi}", 0, 512))
            mk.V(a2, [f"gwb2_{wi}"], [gw], tag="gap2")
            def a3(e, c=c, gw=gw):
                return e.tensor_scalar(out=arf(gw, c * 513 + 1, c * 513 + 513),
                                       in0=arf(gw, c * 513 + 1, c * 513 + 513),
                                       scalar1=VEC[:, VG2 + c:VG2 + c + 1],
                                       scalar2=VEC[:, VB2 + c:VB2 + c + 1],
                                       op0=ALU.mult, op1=ALU.add)
            mk.P(a3, ["vec"], [gw], tag="gap3")
            if wi == 0:
                def a4(e, c=c, gw=gw):
                    return e.tensor_mul(out=arf(gw, c * 513, c * 513 + 1),
                                        in0=BTOK[:, 8 + c:9 + c], in1=SMASK[:, 0:1])
                mk.V(a4, ["btokr", "smask"], [gw], tag="gbnd")
            else:
                def a4(e, c=c, gw=gw):
                    return e.tensor_copy(out=arf(gw, c * 513, c * 513 + 1),
                                         in_=BTOK[:, 24 + c:25 + c])
                mk.V(a4, ["gsave"], [gw], tag="gbnd")
        PA.free(f"gwb1_{wi}")
        PA.free(f"gwb2_{wi}")
        for c in range(8):
            def gsv(e, c=c, gw=gw):
                return e.tensor_copy(out=BTOK[:, 24 + c:25 + c],
                                     in_=arf(gw, c * 513 + 512, c * 513 + 513))
            mk.V(gsv, [gw], ["gsave"], tag="gsv")
        gkw, grw, dgw = f"gkw{wi}", f"grw{wi}", f"dgw{wi}"
        A.alloc(gkw, 8 * wl // 2)
        A.alloc(grw, 8 * wl // 2)
        A.alloc(dgw, wl)
        for c in range(8):
            def d1(e, c=c, gw=gw, dgw=dgw):
                return e.tensor_sub(out=arf(dgw),
                                    in0=arf(gw, c * 513 + 1, c * 513 + 513),
                                    in1=arf(gw, c * 513, c * 513 + 512))
            mk.V(d1, [gw], [dgw], tag="dg")
            for nm, vcol in ((gkw, VFMK), (grw, VFMR)):
                def mx(e, c=c, nm=nm, vcol=vcol, gw=gw, dgw=dgw):
                    return e.scalar_tensor_tensor(
                        out=arb(nm, c * 512, c * 512 + 512), in0=arf(dgw),
                        scalar=VEC[:, vcol + c:vcol + c + 1],
                        in1=arf(gw, c * 513, c * 513 + 512),
                        op0=ALU.mult, op1=ALU.add)
                mk.V(mx, [dgw, gw, "vec"], [nm], tag="gmx")
        A.free(dgw)
        A.free(gw)
        srw = f"srw{wi}"
        A.alloc(srw, 8 * wl)
        PA.alloc(f"frp_{wi}_0", 512)
        PA.alloc(f"frp_{wi}_1", 512)
        for mo in range(8):
            pp = 512 * (mo % 2)
            for k in range(8):
                st, sp = (k == 0), (k == 7)
                def f(e, k=k, mo=mo, st=st, sp=sp, wi=wi, grw=grw, pp=pp):
                    return e.matmul(psf(f"frp_{wi}_{pp // 512}", 0, 512),
                                    arb("frw", k * CH + mo * 128, k * CH + mo * 128 + 128),
                                    arb(grw, k * 512, k * 512 + 512), start=st, stop=sp)
                mk.T(f, ["frw", grw], [f"frp_{wi}_{mo%2}"], tag="frmm")
            def sev(e, mo=mo, wi=wi, srw=srw, pp=pp):
                return e.activation(out=arf(srw, mo * 512, mo * 512 + 512),
                                    in_=psf(f"frp_{wi}_{pp // 512}", 0, 512), func=AF.Sigmoid)
            mk.S(sev, [f"frp_{wi}_{mo%2}"], [srw], tag="srev2")
        PA.free(f"frp_{wi}_0")
        PA.free(f"frp_{wi}_1")
        A.free(grw)
        kfb, kfw = f"kfb{wi}", f"kfw{wi}"
        A.alloc(kfb, 32 * 512 // 2)
        A.alloc(kfw, 512)
        PA.alloc(f"fkp_{wi}_0", 512)
        PA.alloc(f"fkp_{wi}_1", 512)
        if wi == 0:
            A.alloc("fkgA", 4096)
        for grp in range(4):
            fg = "fkgA"
            def ld(e, grp=grp, fg=fg):
                return e.dma_start(
                    out=ARB[:, 2 * A.off(fg):2 * A.off(fg) + 8192].rearrange(
                        "p (k m) -> p k m", k=8),
                    in_=bass.AP(fk_d.tensor, grp * 1024,
                                [[8 * 4096, 128], [4096, 8], [1, 1024]]))
            mk.dmaG(ld, "pb_" + fg, [], [fg], tag="fkld")
            for fi in range(8):
                fch = grp * 8 + fi
                pp = 512 * (fch % 2)
                for k in range(8):
                    st, sp = (k == 0), (k == 7)
                    def f(e, k=k, fi=fi, st=st, sp=sp, wi=wi, gkw=gkw, pp=pp, fg=fg):
                        return e.matmul(psf(f"fkp_{wi}_{pp // 512}", 0, 512),
                                        arb(fg, k * 1024 + fi * 128, k * 1024 + fi * 128 + 128),
                                        arb(gkw, k * 512, k * 512 + 512), start=st, stop=sp)
                    mk.T(f, [fg, gkw], [f"fkp_{wi}_{fch%2}"], tag="fkmm")
                def rev(e, wi=wi, kfw=kfw, fch=fch, pp=pp):
                    return e.activation(out=arf(kfw),
                                        in_=psf(f"fkp_{wi}_{pp // 512}", 0, 512), func=AF.Relu)
                mk.S(rev, [f"fkp_{wi}_{fch%2}"], [f"kfr{wi}"], tag="fkrelu")
                def sqv(e, fch=fch, kfb=kfb, kfw=kfw):
                    return e.tensor_mul(out=arb(kfb, fch * 512, fch * 512 + 512),
                                        in0=arf(kfw), in1=arf(kfw))
                mk.P(sqv, [f"kfr{wi}"], [kfb], tag="fksq")
        if wi == 3:
            A.free("fkgA")
        PA.free(f"fkp_{wi}_0")
        PA.free(f"fkp_{wi}_1")
        A.free(kfw, [f"kfr{wi}"])
        A.free(gkw)
        PA.alloc(f"kvp_{wi}_0", 512)
        PA.alloc(f"kvp_{wi}_1", 512)
        A.alloc(f"t5_{wi}", 512)
        for mo in range(8):
            pp = 512 * (mo % 2)
            for fch in range(32):
                st, sp = (fch == 0), (fch == 31)
                def f(e, fch=fch, mo=mo, st=st, sp=sp, wi=wi, kfb=kfb, pp=pp):
                    return e.matmul(psf(f"kvp_{wi}_{pp // 512}", 0, 512),
                                    arb("fvw", fch * CH + mo * 128, fch * CH + mo * 128 + 128),
                                    arb(kfb, fch * 512, fch * 512 + 512), start=st, stop=sp)
                mk.T(f, ["fvw", kfb], [f"kvp_{wi}_{mo%2}"], tag="fvmm")
            def cmb(e, mo=mo, wi=wi, srw=srw, pp=pp):
                return e.tensor_mul(out=arf(f"t5_{wi}"),
                                    in0=arf(srw, mo * 512, mo * 512 + 512),
                                    in1=psf(f"kvp_{wi}_{pp // 512}", 0, 512))
            mk.V(cmb, [srw, f"kvp_{wi}_{mo%2}"], [f"t5w_{wi}"], tag="cmb1")
            def cmb2(e, mo=mo, wi=wi, xmw=xmw):
                return e.tensor_add(out=arf(xmw, mo * 512, mo * 512 + 512),
                                    in0=arf(xmw, mo * 512, mo * 512 + 512),
                                    in1=arf(f"t5_{wi}"))
            mk.V(cmb2, [f"t5w_{wi}"], [xmw], tag="cmb2")
        PA.free(f"kvp_{wi}_0")
        PA.free(f"kvp_{wi}_1")
        A.free(f"t5_{wi}", [f"t5w_{wi}"])
        A.free(kfb)
        A.free(srw)
        stats_win(xmw, 0, 512, f"sw{wi}")
        x0w = f"x0w{wi}"
        A.alloc(x0w, 8 * wl)
        def x0l(e, lo=lo, x0w=x0w):
            return e.dma_start(
                out=AR[:, A.off(x0w):A.off(x0w) + 8 * 512].rearrange(
                    "p (c t) -> p c t", c=8),
                in_=bass.AP(xin.tensor, lo + 1, [[8 * WH, 128], [WH, 8], [1, 512]]))
        mk.dmaQ(x0l, f"ds_{x0w}", [], [x0w], tag="x0l")
        PA.alloc(f"swb1_{wi}", 512)
        PA.alloc(f"swb2_{wi}", 512)
        bc2(f"sw{wi}", f"swb1_{wi}", f"swb2_{wi}", 512)
        for c in range(8):
            def s1f(e, c=c, xmw=xmw, wi=wi):
                return e.tensor_mul(out=arf(xmw, c * 512, c * 512 + 512),
                                    in0=arf(xmw, c * 512, c * 512 + 512),
                                    in1=psf(f"swb1_{wi}", 0, 512))
            mk.V(s1f, [f"swb1_{wi}"], [xmw], tag="x1a")
            def s2f(e, c=c, xmw=xmw, wi=wi):
                return e.tensor_sub(out=arf(xmw, c * 512, c * 512 + 512),
                                    in0=arf(xmw, c * 512, c * 512 + 512),
                                    in1=psf(f"swb2_{wi}", 0, 512))
            mk.V(s2f, [f"swb2_{wi}"], [xmw], tag="x1b")
            def s3f(e, c=c, xmw=xmw, x0w=x0w):
                return e.tensor_add(out=arf(xmw, c * 512, c * 512 + 512),
                                    in0=arf(xmw, c * 512, c * 512 + 512),
                                    in1=arf(x0w, c * 512, c * 512 + 512))
            mk.V(s3f, [x0w], [xmw], tag="x1c")
        PA.free(f"swb1_{wi}")
        PA.free(f"swb2_{wi}")
        A.free(x0w)
        def x1s(e, lo=lo, xmw=xmw):
            return e.dma_start(
                out=bass.AP(s_x1.tensor, lo, [[8 * W, 128], [W, 8], [1, 512]]),
                in_=AR[:, A.off(xmw):A.off(xmw) + 8 * 512].rearrange(
                    "p (c t) -> p c t", c=8))
        mk.dmaQ(x1s, f"ds_x1s{wi}", [xmw], ["s_x1"], tag="x1s")
        stats_win(xmw, 0, 512, f"pw{wi}")
        PA.alloc(f"pwb1_{wi}", 512)
        PA.alloc(f"pwb2_{wi}", 512)
        bc2(f"pw{wi}", f"pwb1_{wi}", f"pwb2_{wi}", 512)
        for c in range(8):
            def p1f(e, c=c, xmw=xmw, wi=wi):
                return e.tensor_mul(out=arf(xmw, c * 512, c * 512 + 512),
                                    in0=arf(xmw, c * 512, c * 512 + 512),
                                    in1=psf(f"pwb1_{wi}", 0, 512))
            mk.V(p1f, [f"pwb1_{wi}"], [xmw], tag="pn1")
            def p2f(e, c=c, xmw=xmw, wi=wi):
                return e.tensor_sub(out=arf(xmw, c * 512, c * 512 + 512),
                                    in0=arf(xmw, c * 512, c * 512 + 512),
                                    in1=psf(f"pwb2_{wi}", 0, 512))
            mk.V(p2f, [f"pwb2_{wi}"], [xmw], tag="pn2")
        PA.free(f"pwb1_{wi}")
        PA.free(f"pwb2_{wi}")
        if wi == 3:
            for c in range(8):
                def xbs(e, c=c, xmw=xmw):
                    return e.tensor_copy(out=XBND[:, c:c + 1],
                                         in_=arf(xmw, c * 512 + 511, c * 512 + 512))
                mk.V(xbs, [xmw], ["xbnd0"], tag="xbs")
        def xis(e, lo=lo, xmw=xmw):
            return e.dma_start(
                out=bass.AP(s_xi.tensor, lo, [[8 * W, 128], [W, 8], [1, 512]]),
                in_=AR[:, A.off(xmw):A.off(xmw) + 8 * 512].rearrange(
                    "p (c t) -> p c t", c=8))
        mk.dmaQ(xis, f"ds_xis{wi}", [xmw], ["s_xi2"], tag="xis")
        A.free(f"sw{wi}B", [f"sw{wi}_ex2", f"sw{wi}_var", f"sw{wi}_r"])
        A.free(f"sw{wi}D", [f"sw{wi}_mean", f"sw{wi}_mr"])
        A.free(f"pw{wi}B", [f"pw{wi}_ex2", f"pw{wi}_var", f"pw{wi}_r"])
        A.free(f"pw{wi}D", [f"pw{wi}_mean", f"pw{wi}_mr"])
        A.free(xmw)
    A.free("fvw")
    A.free("frw")
    A.free("ln2sB", ["ln2_sq_r", "ln2_r1"])
    A.free("ln2sD", ["ln2_mr1"])
    if "xin2" in mk.debug:
        A.alloc("xin2dbg", 8 * W)
        mk.dmaQ(lambda e: e.dma_start(out=arf("xin2dbg"), in_=s_xi),
                "ds_xin2dbg", ["s_xi2"], ["xin2dbg"], tag="xd")
        mk.dbg("xin2", lambda: arf("xin2dbg"), [128, 8 * W], ["xin2dbg"])
        A.free("xin2dbg")
    if "x1" in mk.debug:
        A.alloc("x1dbg", 8 * W)
        mk.dmaQ(lambda e: e.dma_start(out=arf("x1dbg"), in_=s_x1),
                "ds_x1dbg", ["s_x1"], ["x1dbg"], tag="xd1")
        mk.dbg("x1", lambda: arf("x1dbg"), [128, 8 * W], ["x1dbg"])
        A.free("x1dbg")
    if seg < 5:
        return finish(nc, mk, A, PA)

    def ffn_layer(l, src_ap, src_rgn):
        L = f"L{l}"
        QSC = 4 * l
        def bt1(e, l=l):
            return e.dma_start(out=bass.AP(tok_i[l].tensor, 0, [[8, 128], [1, 8]]),
                               in_=XBND[:, 8 * l:8 * l + 8])
        mk.dmaG(bt1, f"ds_bti{L}", [f"xbnd{l}"], [f"toki{L}"], tag="btok")
        em.cc(lambda e, l=l: e.collective_compute(
            "AllGather", ALU.bypass, replica_groups=[[0, 1], [2, 3], [4, 5], [6, 7]],
            ins=[tok_i[l].opt()], outs=[tok_o[l].opt()]),
            f"cs_tok{L}", [f"toki{L}"], [f"toko{L}"], tag="btok")
        def bt2(e, l=l):
            return e.dma_start(out=BTOK[:, 8:16],
                               in_=bass.AP(tok_o[l].tensor, 0, [[8, 128], [1, 8]]))
        mk.dmaG(bt2, f"ds_bto{L}", [f"toko{L}"], [f"btr{L}"], tag="btok")
        hb = f"hb{L}"
        A.alloc(hb, 8192)
        for c in range(8):
            if c < 4:
                def hc(e, c=c, hb=hb):
                    return e.dma_start(out=arb(hb, c * W, (c + 1) * W),
                                       in_=bass.AP(src_ap.tensor, c * W, [[8 * W, 128], [1, W]]))
            else:
                def hc(e, c=c, hb=hb):
                    return e.dma_start(out=arb(hb, c * W + 1, (c + 1) * W),
                                       in_=bass.AP(src_ap.tensor, c * W, [[8 * W, 128], [1, W - 1]]))
            mk.dmaG(hc, "pb_hb" + L, [src_rgn], [hb], tag="hbld")
        for c in range(4, 8):
            def hc0(e, c=c, hb=hb):
                return e.tensor_tensor(out=arb(hb, c * W, c * W + 1),
                                       in0=BTOK[:, 8 + c:9 + c], in1=SMASK[:, 0:1],
                                       op=ALU.mult)
            mk.V(hc0, [f"btr{L}", "smask"], [hb], tag="hb0")
        if subseg == 11:
            return None
        for i, nm in enumerate(("tqA", "tqB", "tkA", "tkB")):
            A.alloc(nm + L, 2048)
            def tl(e, i=i, nm=nm):
                return e.dma_start(out=arf(nm + L),
                                   in_=bass.AP(tabs_d.tensor, i * 2048, [[8192, 128], [1, 2048]]))
            mk.dmaQ(tl, f"ds_{nm}{L}", [], [nm + L + "_c"], tag="tab")
            def tp(e, i=i, nm=nm):
                return e.tensor_scalar_mul(out=arf(nm + L), in0=arf(nm + L),
                                           scalar1=QS[:, QSC + i:QSC + i + 1])
            mk.V(tp, [nm + L + "_c", "qs"], [nm + L], tag="tabp")
        if subseg == 12:
            return None
        wq = f"wq{L}"
        A.alloc(wq, 10240)
        mk.dmaG(lambda e, l=l: e.dma_start(out=arb(wq), in_=wqkv_d[l]), "pb_wq" + L,
                [], [wq], tag="wql")
        qr, kr, vf = f"qr{L}", f"kr{L}", f"vf{L}"
        A.alloc(qr, 4096)
        A.alloc(kr, 5120)
        A.alloc(vf, 4096)
        A.alloc(f"rt1{L}", 512)
        A.alloc(f"rt2{L}", 512)
        A.alloc(f"sqs{L}", 256)
        A.alloc(f"qsr{L}", 2048)
        A.alloc(f"ksr{L}", 2048)
        PA.alloc(f"qp0{L}", 512)
        PA.alloc(f"qp1{L}", 512)
        PA.alloc(f"qp2{L}", 512)
        PA.alloc(f"qp3{L}", 512)
        PA.alloc(f"qst{L}", 512)

        def qkv_mm(mc, win, pn):
            lo = win * 512
            for k in range(8):
                st, sp = (k == 0), (k == 7)
                def f(e, k=k, mc=mc, lo=lo, st=st, sp=sp, pn=pn):
                    return e.matmul(psf(pn, 0, 512),
                                    arb(wq, k * 2560 + mc * 128, k * 2560 + mc * 128 + 128),
                                    arb(hb, k * W + lo, k * W + lo + 512),
                                    start=st, stop=sp)
                mk.T(f, [wq, hb], [pn], tag="qkvmm")

        if subseg == 13:
            return None
        for win in range(4):
            if subseg in (14, 15, 16, 17, 18, 19) and win > 0:
                break
            lo = win * 512
            for mq in range(4):
                qp0n = f"qp{2 * (mq % 2)}{L}"
                qp1n = f"qp{2 * (mq % 2) + 1}{L}"
                qkv_mm(mq, win, qp0n)
                if subseg == 15:
                    def pe_(e, mq=mq, win=win, qp0n=qp0n):
                        return e.copy(out=arf(f"rt1{L}"), in_=psf(qp0n, 0, 512))
                    mk.S(pe_, [qp0n], [f"rt1{L}"], tag="dbgev")
                    continue
                qkv_mm(12 + mq, win, qp1n)
                if subseg == 16:
                    def pe_(e, mq=mq, win=win, qp0n=qp0n):
                        return e.copy(out=arf(f"rt1{L}"), in_=psf(qp0n, 0, 512))
                    mk.S(pe_, [qp0n], [f"rt1{L}"], tag="dbgev")
                    def pe2_(e, mq=mq, win=win, qp1n=qp1n):
                        return e.copy(out=arf(f"rt2{L}"), in_=psf(qp1n, 0, 512))
                    mk.S(pe2_, [qp1n], [f"rt2{L}"], tag="dbgev2")
                    continue
                def sq_(e, win=win, mq=mq, qp0n=qp0n):
                    return e.activation(out=arb(f"sqs{L}", 0, 512), in_=psf(qp0n, 0, 512),
                                        func=AF.Square)
                if subseg != 18:
                    mk.S(sq_, [qp0n], [f"sqs{L}"], tag="sq")
                if subseg == 19:
                    continue
                if subseg not in (17, 18):
                    def hmm(e, mq=mq, win=win):
                        return e.matmul(psf(f"qst{L}", 0, 512)[0:16, :],
                                        HSEG[:, mq * 16:mq * 16 + 16], arb(f"sqs{L}", 0, 512),
                                        start=(mq == 0), stop=(mq == 3))
                    mk.T(hmm, [f"sqs{L}", "hseg"], [f"qst{L}"], tag="hmm")
                def r1(e, mq=mq, win=win, qp0n=qp0n):
                    return e.tensor_mul(out=arf(f"rt1{L}"), in0=psf(qp0n, 0, 512),
                                        in1=arf("tqA" + L, win * 512, win * 512 + 512))
                mk.V(r1, [qp0n, "tqA" + L, f"sqs{L}"], [f"rt1{L}"], tag="rot1")
                def r2(e, mq=mq, win=win, qp1n=qp1n):
                    return e.tensor_mul(out=arf(f"rt2{L}"), in0=psf(qp1n, 0, 512),
                                        in1=arf("tqB" + L, win * 512, win * 512 + 512))
                mk.V(r2, [qp1n, "tqB" + L], [f"rt2{L}"], tag="rot2")
                def r3(e, mq=mq, win=win):
                    return e.tensor_add(out=arb(qr, mq * W + win * 512, mq * W + win * 512 + 512),
                                        in0=arf(f"rt1{L}"), in1=arf(f"rt2{L}"))
                mk.V(r3, [f"rt1{L}", f"rt2{L}"], [qr], tag="rot3")
            if subseg not in (17, 18, 19):
                def qev(e, win=win):
                    return e.tensor_copy(out=AR[0:16, A.off(f"qsr{L}") + win * 512:A.off(f"qsr{L}") + win * 512 + 512],
                                         in_=psf(f"qst{L}", 0, 512)[0:16, :])
                mk.V(qev, [f"qst{L}"], [f"qsr{L}"], tag="qev")
            if subseg in (15, 16, 17, 18, 19):
                continue
            for mk_ in range(4):
                qkv_mm(4 + mk_, win, f"qp0{L}")
                qkv_mm(16 + mk_, win, f"qp1{L}")
                def sq_(e, mk_=mk_, win=win):
                    return e.activation(out=arb(f"sqs{L}", 0, 512), in_=psf(f"qp0{L}", 0, 512),
                                        func=AF.Square)
                mk.S(sq_, [f"qp0{L}"], [f"sqs{L}"], tag="sqk")
                def hmm(e, mk_=mk_, win=win):
                    return e.matmul(psf(f"qst{L}", 0, 512)[0:16, :],
                                    HSEG[:, mk_ * 16:mk_ * 16 + 16], arb(f"sqs{L}", 0, 512),
                                    start=(mk_ == 0), stop=(mk_ == 3))
                mk.T(hmm, [f"sqs{L}", "hseg"], [f"qst{L}"], tag="hmmk")
                def r1(e, mk_=mk_, win=win):
                    return e.tensor_mul(out=arf(f"rt1{L}"), in0=psf(f"qp0{L}", 0, 512),
                                        in1=arf("tkA" + L, win * 512, win * 512 + 512))
                mk.V(r1, [f"qp0{L}", "tkA" + L, f"sqs{L}"], [f"rt1{L}"], tag="krot1")
                def r2(e, mk_=mk_, win=win):
                    return e.tensor_mul(out=arf(f"rt2{L}"), in0=psf(f"qp1{L}", 0, 512),
                                        in1=arf("tkB" + L, win * 512, win * 512 + 512))
                mk.V(r2, [f"qp1{L}", "tkB" + L], [f"rt2{L}"], tag="krot2")
                def r3(e, mk_=mk_, win=win):
                    return e.tensor_add(
                        out=arb(kr, mk_ * 2560 + 512 + win * 512,
                                mk_ * 2560 + 512 + win * 512 + 512),
                        in0=arf(f"rt1{L}"), in1=arf(f"rt2{L}"))
                mk.V(r3, [f"rt1{L}", f"rt2{L}"], [kr], tag="krot3")
            def kev(e, win=win):
                return e.tensor_copy(out=AR[0:16, A.off(f"ksr{L}") + win * 512:A.off(f"ksr{L}") + win * 512 + 512],
                                     in_=psf(f"qst{L}", 0, 512)[0:16, :])
            mk.V(kev, [f"qst{L}"], [f"ksr{L}"], tag="kev")
            for mv in range(4):
                qkv_mm(8 + mv, win, f"qp0{L}")
                def vev(e, mv=mv, win=win):
                    return e.copy(out=arb(vf, mv * W + win * 512, mv * W + win * 512 + 512),
                                  in_=psf(f"qp0{L}", 0, 512))
                mk.S(vev, [f"qp0{L}"], [vf], tag="vev")
        if subseg < 2 or subseg in (15, 16, 17, 18, 19):
            return None
        PA.free(f"qp0{L}")
        PA.free(f"qp1{L}")
        PA.free(f"qp2{L}")
        PA.free(f"qp3{L}")
        PA.free(f"qst{L}")
        A.free(f"rt1{L}")
        A.free(f"rt2{L}")
        A.free(f"sqs{L}")
        A.free(wq)
        A.free(hb)
        for nm in ("tqA", "tqB", "tkA", "tkB"):
            A.free(nm + L, [nm + L + "_c"])

        for tg in (f"qsr{L}", f"ksr{L}"):
            def l1(e, tg=tg):
                return e.activation(out=AR[0:16, A.off(tg):A.off(tg) + 2048],
                                    in_=AR[0:16, A.off(tg):A.off(tg) + 2048],
                                    func=AF.Ln, bias=CONST[0:16, 2:3])
            mk.S(l1, [tg, "const"], [tg], tag="rsln")
            def l2(e, tg=tg):
                return e.activation(out=AR[0:16, A.off(tg):A.off(tg) + 2048],
                                    in_=AR[0:16, A.off(tg):A.off(tg) + 2048],
                                    func=AF.Exp, scale=-0.5)
            mk.S(l2, [tg], [tg], tag="rsex")
            def l3(e, tg=tg):
                return e.tensor_scalar_min(out=AR[0:16, A.off(tg):A.off(tg) + 2048],
                                           in0=AR[0:16, A.off(tg):A.off(tg) + 2048],
                                           scalar1=1e12)
            mk.V(l3, [tg], [tg], tag="rsmin")
        A.alloc(f"rsq{L}", 256)
        PA.alloc(f"rqt{L}", 512)
        for tc in range(16):
            def tr(e, tc=tc):
                return e.transpose(psf(f"rqt{L}", 0, 16)[:, :],
                                   AR[0:16, A.off(f"qsr{L}") + tc * 128:A.off(f"qsr{L}") + tc * 128 + 128],
                                   IDENTF[0:16, 0:16])
            mk.T(tr, [f"qsr{L}", "identf"], [f"rqt{L}"], tag="rstr")
            def ev(e, tc=tc):
                return e.tensor_scalar_mul(out=arf(f"rsq{L}", tc * 16, tc * 16 + 16),
                                           in0=psf(f"rqt{L}", 0, 16), scalar1=8.0)
            mk.V(ev, [f"rqt{L}"], [f"rsq{L}"], tag="rsev")
        PA.free(f"rqt{L}")
        A.free(f"qsr{L}")
        PA.alloc(f"kb{L}", 512)
        for mc in range(4):
            for win in range(4):
                def bm(e, mc=mc, win=win):
                    return e.matmul(psf(f"kb{L}", 0, 512),
                                    EPAIR[:, mc * 128:mc * 128 + 128],
                                    AR[0:16, A.off(f"ksr{L}") + win * 512:A.off(f"ksr{L}") + win * 512 + 512],
                                    start=True, stop=True)
                mk.T(bm, [f"ksr{L}", "epair"], [f"kb{L}"], tag="kbm")
                def km(e, mc=mc, win=win):
                    return e.tensor_mul(
                        out=arb(kr, mc * 2560 + 512 + win * 512, mc * 2560 + 512 + win * 512 + 512),
                        in0=arb(kr, mc * 2560 + 512 + win * 512, mc * 2560 + 512 + win * 512 + 512),
                        in1=psf(f"kb{L}", 0, 512))
                mk.V(km, [f"kb{L}"], [kr], tag="kbmul")
        PA.free(f"kb{L}")
        A.free(f"ksr{L}")
        if subseg < 3:
            return None
        vt = f"vt{L}"
        A.alloc(vt, 5120)
        PA.alloc(f"tpv{L}", 512)
        tpvb = None
        for vc in range(4):
            for tc in range(16):
                def tr(e, vc=vc, tc=tc):
                    pb = PS.bitcast(BF16)
                    off = PA.addr[f"tpv{L}"][0]
                    return e.transpose(pb[:, 2 * off:2 * off + 128],
                                       arb(vf, vc * W + tc * 128, vc * W + tc * 128 + 128),
                                       IDENT)
                mk.T(tr, [vf, "ident"], [f"tpv{L}"], tag="vtr")
                def ev(e, vc=vc, tc=tc):
                    pb = PS.bitcast(BF16)
                    off = PA.addr[f"tpv{L}"][0]
                    return e.tensor_copy(
                        out=arb(vt, (4 + tc) * 512 + vc * 128, (4 + tc) * 512 + vc * 128 + 128),
                        in_=pb[:, 2 * off:2 * off + 128])
                mk.V(ev, [f"tpv{L}"], [vt], tag="vtev")
        PA.free(f"tpv{L}")
        A.free(vf)
        if subseg < 4:
            return None
        def he1(e):
            return e.dma_start(
                out=bass.AP(hal_i[l].tensor, 0, [[2048, 128], [512, 4], [1, 512]]),
                in_=bass.AP(ARB.tensor, None, None) if False else
                ARB[:, 2 * A.off(kr):2 * A.off(kr) + 10240].rearrange(
                    "p (m t) -> p m t", m=4)[:, :, 2048:2560])
        mk.dmaG(he1, f"ds_he1{L}", [kr], [f"hali{L}"], tag="halo")
        def he2(e):
            return e.dma_start(
                out=bass.AP(hal_i[l].tensor, 262144, [[2048, 128], [1, 2048]]),
                in_=arb(vt, 16 * 512, 20 * 512))
        mk.dmaG(he2, f"ds_he2{L}", [vt], [f"hali{L}"], tag="halo")
        em.cc(lambda e, l=l: e.collective_compute(
            "AllGather", ALU.bypass, replica_groups=[[0, 1], [2, 3], [4, 5], [6, 7]],
            ins=[hal_i[l].opt()], outs=[hal_o[l].opt()]),
            f"cs_hal{L}", [f"hali{L}"], [f"halo{L}"], tag="haloc")
        A.alloc(f"hrt{L}", 2048)
        def hr1(e):
            return e.dma_start(
                out=arb(f"hrt{L}", 0, 2048).rearrange("p (m t) -> p m t", m=4),
                in_=bass.AP(hal_o[l].tensor, 0, [[2048, 128], [512, 4], [1, 512]]))
        mk.dmaG(hr1, f"ds_hr1{L}", [f"halo{L}"], [f"hrt1{L}"], tag="halor")
        def hr2(e):
            return e.dma_start(
                out=arb(f"hrt{L}", 2048, 4096),
                in_=bass.AP(hal_o[l].tensor, 262144, [[2048, 128], [1, 2048]]))
        mk.dmaG(hr2, f"ds_hr2{L}", [f"halo{L}"], [f"hrt2{L}"], tag="halor")
        for mc in range(4):
            def hm1(e, mc=mc):
                return e.tensor_scalar_mul(out=arb(kr, mc * 2560, mc * 2560 + 512),
                                           in0=arb(f"hrt{L}", mc * 512, mc * 512 + 512),
                                           scalar1=SMASK[:, 0:1])
            mk.P(hm1, [f"hrt1{L}", "smask"], [kr], tag="hmk")
            def hm2(e, mc=mc):
                return e.tensor_scalar_mul(out=arb(vt, mc * 512, mc * 512 + 512),
                                           in0=arb(f"hrt{L}", 2048 + mc * 512, 2048 + mc * 512 + 512),
                                           scalar1=SMASK[:, 0:1])
            mk.P(hm2, [f"hrt2{L}", "smask"], [vt], tag="hmv")
        A.free(f"hrt{L}", [f"hrt1{L}", f"hrt2{L}"])
        if subseg < 5:
            return None
        ol = f"ol{L}"
        A.alloc(ol, 4096)
        A.alloc(f"aex{L}", 1024)
        A.alloc(f"atx{L}", 1024)
        A.alloc(f"oscb{L}", 512)
        PA.alloc(f"sc0{L}", 1024)
        PA.alloc(f"sc1{L}", 1024)
        PA.alloc(f"tp0{L}", 512)
        PA.alloc(f"tp1{L}", 512)
        PA.alloc(f"op{L}", 512)
        PA.alloc(f"rr{L}", 512)
        PSB = PS.bitcast(BF16)
        qroff = 2 * A.off(qr)
        kroff = 2 * A.off(kr)
        vtoff = 2 * A.off(vt)
        oloff = 2 * A.off(ol)
        aexoff = 2 * A.off(f"aex{L}")
        atxoff = 2 * A.off(f"atx{L}")
        for qt in range(16):
            for h in range(8):
                pr = (qt * 8 + h) % 2
                scn, tpn = f"sc{pr}{L}", f"tp{pr}{L}"
                tpoff = 2 * PA.addr[tpn][0]
                mc, pb_ = h // 2, (h % 2) * 64
                def smm1(e, qt=qt, mc=mc, pb_=pb_, scn=scn):
                    return e.matmul(psf(scn, 0, 512),
                                    ARB[pb_:pb_ + 64, qroff + mc * 2048 + qt * 128:
                                        qroff + mc * 2048 + qt * 128 + 128],
                                    ARB[pb_:pb_ + 64, kroff + mc * 2560 + qt * 128:
                                        kroff + mc * 2560 + qt * 128 + 512],
                                    start=True, stop=True)
                mk.T(smm1, [qr, kr], [scn], tag="smm1")
                def smm2(e, qt=qt, mc=mc, pb_=pb_, scn=scn):
                    return e.matmul(psf(scn, 512, 640),
                                    ARB[pb_:pb_ + 64, qroff + mc * 2048 + qt * 128:
                                        qroff + mc * 2048 + qt * 128 + 128],
                                    ARB[pb_:pb_ + 64, kroff + mc * 2560 + qt * 128 + 512:
                                        kroff + mc * 2560 + qt * 128 + 640],
                                    start=True, stop=True)
                mk.T(smm2, [qr, kr], [scn], tag="smm2")
                def tm1(e, scn=scn):
                    off = PA.addr[scn][0]
                    pv = bass.AP(PS.tensor, off, [[4096, 128], [512, 2], [1, 128]])
                    tv = bass.AP(TRI.tensor, 0, [[256, 128], [128, 2], [1, 128]])
                    return e.tensor_tensor(out=pv, in0=pv, in1=tv, op=ALU.add)
                mk.V(tm1, ["tri", scn], [scn], tag="tri1")
                if qt < 4:
                    def tm3(e, qt=qt, scn=scn):
                        return e.tensor_add(out=psf(scn, 0, 512), in0=psf(scn, 0, 512),
                                            in1=EDGE[:, qt * 512:qt * 512 + 512])
                    mk.V(tm3, ["edge", scn], [scn], tag="tri3")
                def nmx(e, scn=scn, h=h):
                    return e.reduce_max(out=SUMS[:, 16 + h:17 + h], in_=psf(scn, 0, 640),
                                        axis=AX.X, negate=True)
                mk.V(nmx, [scn], [f"s_nmx{h}"], tag="anmx")
                def bia(e, qt=qt, h=h):
                    return e.tensor_mul(out=SUMS[:, 24 + h:25 + h], in0=SUMS[:, 16 + h:17 + h],
                                        in1=arf(f"rsq{L}", qt * 16 + h, qt * 16 + h + 1))
                mk.V(bia, [f"s_nmx{h}", f"rsq{L}"], [f"s_bia{h}"], tag="abia")
                def ex_(e, qt=qt, h=h, pr=pr, scn=scn):
                    return e.activation(out=ARB[:, aexoff + pr * 640:aexoff + pr * 640 + 640],
                                        in_=psf(scn, 0, 640), func=AF.Exp,
                                        scale=arf(f"rsq{L}", qt * 16 + h, qt * 16 + h + 1),
                                        bias=SUMS[:, 24 + h:25 + h], accum_out=SUMS[:, h:h + 1])
                mk.S(ex_, [scn, f"s_bia{h}", f"rsq{L}"], [f"aex{pr}{L}", f"s_sum{h}"], tag="aexp")
                for kc in range(5):
                    def tr(e, kc=kc, pr=pr, tpoff=tpoff):
                        return e.transpose(PSB[:, tpoff + kc * 128:tpoff + kc * 128 + 128],
                                           ARB[:, aexoff + pr * 640 + kc * 128:
                                               aexoff + pr * 640 + kc * 128 + 128], IDENT)
                    mk.T(tr, [f"aex{pr}{L}", "ident"], [tpn], tag="atr")
                def aev(e, pr=pr, tpoff=tpoff):
                    return e.copy(out=ARB[:, atxoff + pr * 640:atxoff + pr * 640 + 640],
                                  in_=PSB[:, tpoff:tpoff + 640])
                mk.S(aev, [tpn], [f"atx{pr}{L}"], tag="aev")
                for kc in range(5):
                    st, sp = (kc == 0), (kc == 4)
                    def vm(e, kc=kc, qt=qt, h=h, pb_=pb_, st=st, sp=sp, pr=pr):
                        return e.matmul(psf(f"op{L}", 0, 128)[pb_:pb_ + 64, :],
                                        ARB[:, vtoff + (qt + kc) * 512 + h * 64:
                                            vtoff + (qt + kc) * 512 + h * 64 + 64],
                                        ARB[:, atxoff + pr * 640 + kc * 128:
                                            atxoff + pr * 640 + kc * 128 + 128],
                                        start=st, stop=sp)
                    mk.T(vm, [vt, f"atx{pr}{L}"], [f"op{L}"], tag="avm")
                def oev(e, h=h, pb_=pb_):
                    return e.copy(out=AR[pb_:pb_ + 64,
                                         A.off(f"oscb{L}") + (h // 2) * 128:
                                         A.off(f"oscb{L}") + (h // 2) * 128 + 128],
                                  in_=psf(f"op{L}", 0, 128)[pb_:pb_ + 64, :])
                mk.S(oev, [f"op{L}"], [f"oscb{h}"], tag="aoev")
            def rc_(e):
                return e.reciprocal(out=SUMS[:, 0:8], in_=SUMS[:, 0:8])
            mk.V(rc_, [f"s_sum{h2}" for h2 in range(8)], ["s_rec"], tag="arec")
            def tr8(e):
                return e.transpose(psf(f"rr{L}", 0, 128)[0:8, :], SUMS[:, 0:8], IDENTF)
            mk.T(tr8, ["s_rec", "identf"], [f"rr{L}"], tag="atr8")
            def rev(e):
                return e.tensor_copy(out=RST[:, :], in_=psf(f"rr{L}", 0, 128)[0:8, :])
            mk.V(rev, [f"rr{L}"], ["rstt"], tag="arev")
            for h in range(8):
                mc, pb_ = h // 2, (h % 2) * 64
                def ehm(e, h=h, pb_=pb_):
                    return e.matmul(psf(f"rr{L}", 128, 256)[pb_:pb_ + 64, :],
                                    EHSET[:, h * 64:h * 64 + 64], RST[:, :],
                                    start=True, stop=True)
                mk.T(ehm, ["rstt", "ehset"], [f"rr{L}b"], tag="aehm")
                def omu(e, h=h, pb_=pb_, qt=qt, mc=mc):
                    return e.tensor_mul(out=ARB[pb_:pb_ + 64, oloff + mc * 2048 + qt * 128:
                                                oloff + mc * 2048 + qt * 128 + 128],
                                        in0=AR[pb_:pb_ + 64,
                                               A.off(f"oscb{L}") + mc * 128:
                                               A.off(f"oscb{L}") + mc * 128 + 128],
                                        in1=psf(f"rr{L}", 128, 256)[pb_:pb_ + 64, :])
                mk.V(omu, [f"oscb{h}", f"rr{L}b"], [ol], tag="aomu")
        PA.free(f"sc0{L}")
        PA.free(f"sc1{L}")
        PA.free(f"tp0{L}")
        PA.free(f"tp1{L}")
        PA.free(f"op{L}")
        PA.free(f"rr{L}")
        if subseg < 6:
            return None
        A.free(f"aex{L}", [f"aex0{L}", f"aex1{L}"])
        A.free(f"atx{L}", [f"atx0{L}", f"atx1{L}"])
        A.free(f"oscb{L}", [f"oscb{h}" for h in range(8)])
        A.free(f"rsq{L}")
        A.free(qr)
        A.free(kr)
        A.free(vt)
        xs = f"xs{L}"
        A.alloc(xs, 8 * W)
        mk.dmaQ(lambda e: e.dma_start(out=arf(xs), in_=src_ap), f"ds_xs{L}",
                [src_rgn], [xs], tag="xsld")
        wow = f"wow{L}"
        A.alloc(wow, 2048)
        mk.dmaG(lambda e, l=l: e.dma_start(out=arb(wow), in_=woff_d[l]), "pb_wow" + L,
                [], [wow], tag="wowld")
        PA.alloc(f"wp0{L}", 512)
        PA.alloc(f"wp1{L}", 512)
        for mo in range(8):
            for win in range(4):
                pn = f"wp{win % 2}{L}"
                for k in range(4):
                    st, sp = (k == 0), (k == 3)
                    def f(e, k=k, mo=mo, win=win, st=st, sp=sp, pn=pn):
                        return e.matmul(psf(pn, 0, 512),
                                        arb(wow, k * CH + mo * 128, k * CH + mo * 128 + 128),
                                        arb(ol, k * W + win * 512, k * W + win * 512 + 512),
                                        start=st, stop=sp)
                    mk.T(f, [wow, ol], [pn], tag="womm2")
                def ev(e, mo=mo, win=win, pn=pn):
                    return e.tensor_add(out=arf(xs, mo * W + win * 512, mo * W + win * 512 + 512),
                                        in0=arf(xs, mo * W + win * 512, mo * W + win * 512 + 512),
                                        in1=psf(pn, 0, 512))
                mk.V(ev, [pn], [xs], tag="woev2")
        PA.free(f"wp0{L}")
        PA.free(f"wp1{L}")
        A.free(wow)
        A.free(ol)
        if subseg < 7:
            return None
        for c in range(8):
            def mb0(e, c=c):
                return e.tensor_copy(out=BTOK[:, 16 + c:17 + c],
                                     in_=arf(xs, c * W + 2047, c * W + 2048))
            mk.V(mb0, [xs], [f"mbx{L}"], tag="mtok")
        def mb1(e, l=l):
            return e.dma_start(out=bass.AP(tok_i[2 + l].tensor, 0, [[8, 128], [1, 8]]),
                               in_=BTOK[:, 16:24])
        mk.dmaG(mb1, f"ds_mbi{L}", [f"mbx{L}"], [f"mtoki{L}"], tag="mtok")
        em.cc(lambda e, l=l: e.collective_compute(
            "AllGather", ALU.bypass, replica_groups=[[0, 1], [2, 3], [4, 5], [6, 7]],
            ins=[tok_i[2 + l].opt()], outs=[tok_o[2 + l].opt()]),
            f"cs_mtok{L}", [f"mtoki{L}"], [f"mtoko{L}"], tag="mtok")
        def mb2(e, l=l):
            return e.dma_start(out=BTOK[:, 24:32],
                               in_=bass.AP(tok_o[2 + l].tensor, 0, [[8, 128], [1, 8]]))
        mk.dmaG(mb2, f"ds_mbo{L}", [f"mtoko{L}"], [f"mbr{L}"], tag="mtok")
        mcast = f"mcast{L}"
        A.alloc(mcast, 8192)
        for c in range(8):
            if c < 4:
                def cst(e, c=c):
                    return e.tensor_copy(out=arb(mcast, c * W, (c + 1) * W),
                                         in_=arf(xs, c * W, (c + 1) * W))
                mk.P(cst, [xs], [mcast], tag="mcst")
            else:
                def cst(e, c=c):
                    return e.tensor_copy(out=arb(mcast, c * W + 1, (c + 1) * W),
                                         in_=arf(xs, c * W, (c + 1) * W - 1))
                mk.P(cst, [xs], [mcast], tag="mcst")
                def cs0(e, c=c):
                    return e.tensor_tensor(out=arb(mcast, c * W, c * W + 1),
                                           in0=BTOK[:, 24 + c:25 + c], in1=SMASK[:, 0:1],
                                           op=ALU.mult)
                mk.V(cs0, [f"mbr{L}", "smask"], [mcast], tag="mcs0")
        ln_stats(None, W, WINS_W, fold2=False, tag=f"mst{L}", cast_name=mcast)
        hbm = f"hbm{L}"
        A.alloc(hbm, 8192)
        A.alloc(f"mt{L}", 512)
        vg = VLNG0 if l == 0 else VLNG1
        vb = VLNB0 if l == 0 else VLNB1
        PA.alloc(f"mb1{L}", 512)
        PA.alloc(f"mb2{L}", 512)
        for win in range(4):
            lo = win * 512
            def b1f(e, lo=lo):
                return e.matmul(psf(f"mb1{L}", 0, 512), ONESR,
                                arf(f"mst{L}sB", lo, lo + 512)[0:1, :], start=True, stop=True)
            mk.T(b1f, [f"mst{L}_r1", "onesr"], [f"mb1{L}"], tag="mb1")
            def b2f(e, lo=lo):
                return e.matmul(psf(f"mb2{L}", 0, 512), ONESR,
                                arf(f"mst{L}sD", lo, lo + 512)[0:1, :], start=True, stop=True)
            mk.T(b2f, [f"mst{L}_mr1", "onesr"], [f"mb2{L}"], tag="mb2")
            for c in range(8):
                sh = 1 if c >= 4 else 0
                llo = lo - sh
                if c >= 4 and win == 0:
                    rlo, rwid, wlo = 0, 511, 1
                else:
                    rlo, rwid, wlo = llo, 512, lo
                def m1f(e, c=c, rlo=rlo, rwid=rwid, wlo=wlo, win=win):
                    return e.tensor_mul(out=arf(f"mt{L}", 0, rwid),
                                        in0=arf(xs, c * W + rlo, c * W + rlo + rwid),
                                        in1=psf(f"mb1{L}", 512 - rwid, 512))
                mk.V(m1f, [xs, f"mb1{L}"], [f"mt{L}"], tag="m1f")
                def m2f(e, c=c, rwid=rwid):
                    return e.tensor_sub(out=arf(f"mt{L}", 0, rwid),
                                        in0=arf(f"mt{L}", 0, rwid),
                                        in1=psf(f"mb2{L}", 512 - rwid, 512))
                mk.V(m2f, [f"mb2{L}"], [f"mt{L}"], tag="m2f")
                def m3f(e, c=c, rwid=rwid, wlo=wlo, vg=vg, vb=vb):
                    return e.tensor_scalar(out=arb(hbm, c * W + wlo, c * W + wlo + rwid),
                                           in0=arf(f"mt{L}", 0, rwid),
                                           scalar1=VEC[:, vg + c:vg + c + 1],
                                           scalar2=VEC[:, vb + c:vb + c + 1],
                                           op0=ALU.mult, op1=ALU.add)
                mk.V(m3f, [f"mt{L}", "vec"], [hbm], tag="m3f")
                if c >= 4 and win == 0:
                    def c1f(e, c=c):
                        return e.tensor_tensor(out=SUMS[:, 10:11], in0=BTOK[:, 24 + c:25 + c],
                                               in1=SMASK[:, 0:1], op=ALU.mult)
                    mk.V(c1f, [f"mbr{L}", "smask"], ["s_c1"], tag="c1f")
                    def c2f(e, c=c):
                        return e.tensor_mul(out=SUMS[:, 10:11], in0=SUMS[:, 10:11],
                                            in1=psf(f"mb1{L}", 0, 1))
                    mk.V(c2f, ["s_c1", f"mb1{L}"], ["s_c1"], tag="c2f")
                    def c3f(e, c=c):
                        return e.tensor_sub(out=SUMS[:, 10:11], in0=SUMS[:, 10:11],
                                            in1=psf(f"mb2{L}", 0, 1))
                    mk.V(c3f, ["s_c1", f"mb2{L}"], ["s_c1"], tag="c3f")
                    def c4f(e, c=c, vg=vg, vb=vb):
                        return e.tensor_scalar(out=arb(hbm, c * W, c * W + 1),
                                               in0=SUMS[:, 10:11],
                                               scalar1=VEC[:, vg + c:vg + c + 1],
                                               scalar2=VEC[:, vb + c:vb + c + 1],
                                               op0=ALU.mult, op1=ALU.add)
                    mk.V(c4f, ["s_c1", "vec"], [hbm], tag="c4f")
        PA.free(f"mb1{L}")
        PA.free(f"mb2{L}")
        A.free(f"mt{L}")
        A.free(mcast)
        A.free(f"mst{L}sB", [f"mst{L}_sq_r", f"mst{L}_r1"])
        A.free(f"mst{L}sD", [f"mst{L}_mr1"])
        spill_ap = s_xi if l == 0 else s_mid
        spill_rgn = f"sxp{L}"
        out_ap = s_mid if l == 0 else s_xi
        out_rgn = f"sxo{L}"
        mk.dmaQ(lambda e: e.dma_start(out=spill_ap, in_=arf(xs)), f"ds_sxp{L}",
                [xs], [spill_rgn], tag="mlpspill")
        A.free(xs)
        w2w = f"w2w{L}"
        A.alloc(w2w, 11264)
        mk.dmaG(lambda e, l=l: e.dma_start(out=arb(w2w), in_=w2f_d[l]), "pb_w2w" + L,
                [], [w2w], tag="w2ld")
        bcol = VB2F0 if l == 0 else VB2F1
        for wb in range(2):
            glu = f"glu{L}_{wb}"
            A.alloc(glu, 11264)
            A.alloc(f"gel{L}_{wb}0", 512)
            A.alloc(f"gel{L}_{wb}1", 512)
            PA.alloc(f"ap{L}{wb}0", 512)
            PA.alloc(f"gp{L}{wb}0", 512)
            PA.alloc(f"ap{L}{wb}1", 512)
            PA.alloc(f"gp{L}{wb}1", 512)
            if wb == 0:
                A.alloc(f"w1bA{L}", 1024)
                A.alloc(f"w1bB{L}", 1024)
            for mp in range(22):
                apn = f"ap{L}{wb}{mp % 2}"
                gpn = f"gp{L}{wb}{mp % 2}"
                w1b = (f"w1bA{L}" if mp % 2 == 0 else f"w1bB{L}")
                def wld(e, mp=mp, l=l, w1b=w1b):
                    return e.dma_start(
                        out=arb(w1b, 0, 2048).rearrange("p (k m) -> p k m", k=8),
                        in_=bass.AP(w1f_d[l].tensor, mp * 256,
                                    [[8 * 5632, 128], [5632, 8], [1, 256]]))
                mk.dmaG(wld, "pb_" + w1b, [], [w1b], tag="w1ld")
                for sub in range(2):
                    lo = wb * 1024 + sub * 512
                    for k in range(8):
                        st, sp = (k == 0), (k == 7)
                        def fa(e, k=k, lo=lo, st=st, sp=sp, w1b=w1b, apn=apn):
                            return e.matmul(psf(apn, 0, 512),
                                            arb(w1b, k * 256, k * 256 + 128),
                                            arb(hbm, k * W + lo, k * W + lo + 512),
                                            start=st, stop=sp)
                        mk.T(fa, [w1b, hbm], [apn], tag="w1a")
                        def fg(e, k=k, lo=lo, st=st, sp=sp, w1b=w1b, gpn=gpn):
                            return e.matmul(psf(gpn, 0, 512),
                                            arb(w1b, k * 256 + 128, k * 256 + 256),
                                            arb(hbm, k * W + lo, k * W + lo + 512),
                                            start=st, stop=sp)
                        mk.T(fg, [w1b, hbm], [gpn], tag="w1g")
                    def gev(e, mp=mp, wb=wb, l=l, gpn=gpn):
                        return e.activation(out=arf(f"gel{L}_{wb}" + str(mp % 2)),
                                            in_=psf(gpn, 0, 512), func=AF.Gelu,
                                            bias=B1[:, l * 44 + 2 * mp + 1:l * 44 + 2 * mp + 2])
                    mk.S(gev, [gpn, "b1"], [f"gel{L}_{wb}" + str(mp % 2)], tag="gelu")
                    def glf(e, mp=mp, wb=wb, sub=sub, l=l, glu=glu, apn=apn):
                        return e.scalar_tensor_tensor(
                            out=arb(glu, mp * 1024 + sub * 512, mp * 1024 + sub * 512 + 512),
                            in0=psf(apn, 0, 512),
                            scalar=B1[:, l * 44 + 2 * mp:l * 44 + 2 * mp + 1],
                            in1=arf(f"gel{L}_{wb}" + str(mp % 2)), op0=ALU.add, op1=ALU.mult)
                    mk.V(glf, [apn, "b1", f"gel{L}_{wb}" + str(mp % 2)], [glu], tag="glu")
            for pnm in (f"ap{L}{wb}0", f"gp{L}{wb}0", f"ap{L}{wb}1", f"gp{L}{wb}1"):
                PA.free(pnm)
            A.free(f"gel{L}_{wb}0")
            A.free(f"gel{L}_{wb}1")
            for mo in range(8):
                PA.alloc(f"xp{L}{wb}{mo}", 512)
            for sub in range(2):
                lo = wb * 1024 + sub * 512
                xw = f"xw{L}{wb}{sub}"
                A.alloc(xw, 4096)
                def xwl(e, lo=lo, xw=xw, spill_ap=spill_ap):
                    return e.dma_start(
                        out=AR[:, A.off(xw):A.off(xw) + 4096].rearrange(
                            "p (c t) -> p c t", c=8),
                        in_=bass.AP(spill_ap.tensor, lo, [[8 * W, 128], [W, 8], [1, 512]]))
                mk.dmaQ(xwl, f"ds_{xw}", [spill_rgn], [xw], tag="xwl")
                for mo in range(8):
                    for kp in range(22):
                        st, sp = (kp == 0), (kp == 21)
                        def f2(e, kp=kp, mo=mo, st=st, sp=sp, glu=glu, sub=sub):
                            return e.matmul(psf(f"xp{L}{wb}{mo}", 0, 512),
                                            arb(w2w, kp * CH + mo * 128, kp * CH + mo * 128 + 128),
                                            arb(glu, kp * 1024 + sub * 512, kp * 1024 + sub * 512 + 512),
                                            start=st, stop=sp)
                        mk.T(f2, [w2w, glu], [f"xp{L}{wb}{mo}"], tag="w2mm")
                    def ev2(e, mo=mo, bcol=bcol, xw=xw):
                        return e.scalar_tensor_tensor(
                            out=arf(xw, mo * 512, mo * 512 + 512),
                            in0=psf(f"xp{L}{wb}{mo}", 0, 512),
                            scalar=VEC[:, bcol + mo:bcol + mo + 1],
                            in1=arf(xw, mo * 512, mo * 512 + 512),
                            op0=ALU.add, op1=ALU.add)
                    mk.V(ev2, [f"xp{L}{wb}{mo}", "vec", xw], [xw], tag="w2ev")
                if wb == 1 and sub == 1:
                    for c in range(8):
                        def xbs2(e, c=c, xw=xw, l=l):
                            return e.tensor_copy(out=XBND[:, 8 + c:9 + c],
                                                 in_=arf(xw, c * 512 + 511, c * 512 + 512))
                        if l == 0:
                            mk.V(xbs2, [xw], ["xbnd1"], tag="xbs2")
                def xws(e, lo=lo, xw=xw, out_ap=out_ap):
                    return e.dma_start(
                        out=bass.AP(out_ap.tensor, lo, [[8 * W, 128], [W, 8], [1, 512]]),
                        in_=AR[:, A.off(xw):A.off(xw) + 4096].rearrange(
                            "p (c t) -> p c t", c=8))
                mk.dmaQ(xws, f"ds_{xw}s", [xw], [out_rgn], tag="xws")
                A.free(xw)
            for mo in range(8):
                PA.free(f"xp{L}{wb}{mo}")
            A.free(glu)
            if wb == 1:
                A.free(f"w1bA{L}")
                A.free(f"w1bB{L}")
        A.free(w2w)
        A.free(hbm)
        return out_ap, out_rgn

    r0 = ffn_layer(0, s_xi, "s_xi2")
    if r0 is None:
        return finish(nc, mk, A, PA)
    o0_ap, o0_rgn = r0
    if "xl1o" in mk.debug:
        A.alloc("xl1odbg", 8 * W)
        mk.dmaQ(lambda e: e.dma_start(out=arf("xl1odbg"), in_=o0_ap),
                "ds_xl1odbg", [o0_rgn], ["xl1odbg"], tag="xl1od")
        mk.dbg("xl1o", lambda: arf("xl1odbg"), [128, 8 * W], ["xl1odbg"])
        A.free("xl1odbg")
    if seg < 6:
        return finish(nc, mk, A, PA)
    o1_ap, o1_rgn = ffn_layer(1, o0_ap, o0_rgn)
    xs1 = "xsF"
    A.alloc(xs1, 8 * W)
    mk.dmaQ(lambda e: e.dma_start(out=arf(xs1), in_=o1_ap), "ds_xsF",
            [o1_rgn], [xs1], tag="xsF")
    ln_stats(xs1, W, WINS_W, fold2=False, tag="fin")
    A.alloc("x1f", 8 * W)
    mk.dmaQ(lambda e: e.dma_start(out=arf("x1f"), in_=s_x1), "ds_x1f",
            ["s_x1"], ["x1f"], tag="x1f")
    PA.alloc("fb1", 512)
    PA.alloc("fb2", 512)
    for win in range(4):
        lo = win * 512
        def b1f(e, lo=lo):
            return e.matmul(psf("fb1", 0, 512), ONESR,
                            arf("finsB", lo, lo + 512)[0:1, :], start=True, stop=True)
        mk.T(b1f, ["fin_r1", "onesr"], ["fb1"], tag="fb1")
        def b2f(e, lo=lo):
            return e.matmul(psf("fb2", 0, 512), ONESR,
                            arf("finsD", lo, lo + 512)[0:1, :], start=True, stop=True)
        mk.T(b2f, ["fin_mr1", "onesr"], ["fb2"], tag="fb2")
        for c in range(8):
            def f1(e, c=c, lo=lo):
                return e.tensor_mul(out=arf(xs1, c * W + lo, c * W + lo + 512),
                                    in0=arf(xs1, c * W + lo, c * W + lo + 512),
                                    in1=psf("fb1", 0, 512))
            mk.V(f1, ["fb1"], [xs1], tag="f1")
            def f2(e, c=c, lo=lo):
                return e.tensor_sub(out=arf(xs1, c * W + lo, c * W + lo + 512),
                                    in0=arf(xs1, c * W + lo, c * W + lo + 512),
                                    in1=psf("fb2", 0, 512))
            mk.V(f2, ["fb2"], [xs1], tag="f2")
            def f3(e, c=c, lo=lo):
                return e.tensor_add(out=arf(xs1, c * W + lo, c * W + lo + 512),
                                    in0=arf(xs1, c * W + lo, c * W + lo + 512),
                                    in1=arf("x1f", c * W + lo, c * W + lo + 512))
            mk.V(f3, ["x1f"], [xs1], tag="f3")
    PA.free("fb1")
    PA.free("fb2")
    mk.dmaQ(lambda e: e.dma_start(out=out_d, in_=arf(xs1)), "ds_out",
            [xs1], ["out"], tag="outdma")

    return finish(nc, mk, A, PA)

def finish(nc, mk, A, PA):
    em = mk.em
    if "out" not in em.last_w:
        allr0 = list(em.last_w.keys())
        mk.dmaQ(lambda e: e.dma_start(out=mk.out_d, in_=mk.AR[:, 0:8 * W]),
                "ds_out", allr0, ["out"], tag="dummyout")
    allr = list(em.last_w.keys())
    mk.P(lambda e: e.engine_nop(), allr, [], tag="fin")
    from contextlib import ExitStack
    ctx = ExitStack()
    with nc.Block() as block:
        em.build(nc, block, ctx)
    ctx.close()
    return nc, mk



def img_vec(v):
    return np.ascontiguousarray(v.reshape(8, 128).T)

def img_lhsT(Wdc):
    WT = Wdc.T
    kin, mout = WT.shape
    return np.ascontiguousarray(WT.reshape(kin // 128, 128, mout).transpose(1, 0, 2).reshape(128, -1))

def stage(I):
    x = np.asarray(I["L0_attn_x"])
    f32 = np.float32
    common = {}
    common["wk"] = img_lhsT(I["L0_attn_Wk"])
    common["wv"] = img_lhsT(I["L0_attn_Wv"])
    common["wr"] = img_lhsT(I["L0_attn_Wr"])
    common["wo"] = img_lhsT(I["L0_attn_Wo"])
    common["fk"] = img_lhsT(I["L0_attn_Fk"])
    common["fv"] = img_lhsT(I["L0_attn_Fv"])
    common["fr"] = img_lhsT(I["L0_attn_Fr"])
    vecs = np.zeros((128, 136), f32)
    cols = [("L0_attn_tm_k", 0), ("L0_attn_tm_v", 8), ("L0_attn_tm_r", 16),
            ("L0_attn_ln1_g", 24), ("L0_attn_ln1_b", 32), ("L0_attn_ln2_g", 40),
            ("L0_attn_ln2_b", 48), ("L0_attn_fm_k", 56), ("L0_attn_fm_r", 64),
            ("L0_attn_time_decay", 72), ("L0_attn_time_first", 80)]
    for nm, c0 in cols:
        vecs[:, c0:c0 + 8] = img_vec(np.asarray(I[nm]))
    vecs[:, 88:96] = img_vec(np.asarray(I["L0_ffn_ln_g"][0]))
    vecs[:, 96:104] = img_vec(np.asarray(I["L0_ffn_ln_b"][0]))
    vecs[:, 104:112] = img_vec(np.asarray(I["L0_ffn_ln_g"][1]))
    vecs[:, 112:120] = img_vec(np.asarray(I["L0_ffn_ln_b"][1]))
    vecs[:, 120:128] = img_vec(np.asarray(I["L0_ffn_b2"][0]))
    vecs[:, 128:136] = img_vec(np.asarray(I["L0_ffn_b2"][1]))
    common["vecs"] = vecs
    common["iota1"] = np.broadcast_to(np.arange(1, TL + 1, dtype=f32), (128, TL)).copy()
    common["onesr"] = np.ones((1, 128), f32)
    common["onesc"] = np.ones((128, 1), f32)

    in_maps = []
    for c in range(8):
        b, h = c // 2, c % 2
        t0 = h * TL
        m = dict(common)
        xi = np.zeros((128, 8 * WH), f32)
        xs = x[b, t0:t0 + TL].T
        for ck in range(8):
            xi[:, ck * WH + 1:(ck + 1) * WH] = xs[ck * 128:(ck + 1) * 128]
            if h == 1:
                xi[:, ck * WH] = x[b, t0 - 1, ck * 128:(ck + 1) * 128]
        m["xin"] = xi
        m["smask"] = np.full((128, 1), float(h), f32)
        in_maps.append(m)
    return stage_ffn(I, in_maps)

def _img_from_WT(WT):
    kin, mout = WT.shape
    return np.ascontiguousarray(WT.reshape(kin // 128, 128, mout).transpose(1, 0, 2).reshape(128, -1))

def _swap64(a):
    return np.concatenate([a[32:64], a[0:32]])

def stage_ffn(I, in_maps):
    f32 = np.float32
    for l in range(2):
        wq = np.asarray(I["L0_ffn_wqkv"][l], f32)
        q, k, v = wq[:, :512], wq[:, 512:1024], wq[:, 1024:]
        sw = np.concatenate([np.arange(32, 64), np.arange(0, 32)])
        idx = (np.arange(512) // 64) * 64 + sw[np.arange(512) % 64]
        qrot = q[:, idx]
        krot = k[:, idx]
        ext = np.concatenate([q, k, v, qrot, krot], 1)
        w1 = np.asarray(I["L0_ffn_w1"][l], f32)
        a, g = w1[:, :2730], w1[:, 2730:]
        ap = np.zeros((1024, 2816), f32); ap[:, :2730] = a
        gp = np.zeros((1024, 2816), f32); gp[:, :2730] = g
        pairs = []
        for i in range(22):
            pairs.append(ap[:, i * 128:(i + 1) * 128])
            pairs.append(gp[:, i * 128:(i + 1) * 128])
        w1re = np.concatenate(pairs, 1)
        w2 = np.asarray(I["L0_ffn_w2"][l], f32)
        w2p = np.zeros((2816, 1024), f32); w2p[:2730] = w2
        for m in in_maps:
            m[f"wqkv{l}"] = _img_from_WT(ext)
            m[f"woff{l}"] = _img_from_WT(np.asarray(I["L0_ffn_wo"][l], f32))
            m[f"w1f{l}"] = _img_from_WT(w1re)
            m[f"w2f{l}"] = _img_from_WT(w2p)
    b1re = np.zeros((128, 88), f32)
    for l in range(2):
        b1 = np.asarray(I["L0_ffn_b1"][l], f32)
        ap = np.zeros(2816, f32); ap[:2730] = b1[:2730]
        gp = np.zeros(2816, f32); gp[:2730] = b1[2730:]
        for i in range(22):
            b1re[:, l * 44 + 2 * i] = ap[i * 128:(i + 1) * 128]
            b1re[:, l * 44 + 2 * i + 1] = gp[i * 128:(i + 1) * 128]
    qst = np.zeros((128, 8), f32)
    for l in range(2):
        qs = np.asarray(I["L0_ffn_q_scale"][l], f32)
        ks = np.asarray(I["L0_ffn_k_scale"][l], f32)
        d = np.arange(128) % 64
        qst[:, 4 * l + 0] = qs[d]
        qst[:, 4 * l + 1] = _swap64(qs)[d]
        qst[:, 4 * l + 2] = ks[d]
        qst[:, 4 * l + 3] = _swap64(ks)[d]
    tri = np.zeros((128, 256), f32)
    ii = np.arange(128)[:, None]; jj = np.arange(128)[None, :]
    tri[:, :128] = np.where(jj <= ii, -1e30, 0.0)
    tri[:, 128:] = np.where(jj > ii, -1e30, 0.0)
    ident = np.eye(128, dtype=f32)
    hseg = np.zeros((128, 128), f32)
    for m in range(8):
        hseg[:64, m * 16 + 2 * m] = 1.0
        hseg[64:, m * 16 + 2 * m + 1] = 1.0
    epair = np.zeros((16, 512), f32)
    for m in range(4):
        epair[2 * m, m * 128:m * 128 + 64] = 1.0
        epair[2 * m + 1, m * 128 + 64:m * 128 + 128] = 1.0
    ehset = np.zeros((8, 512), f32)
    for hh in range(8):
        ehset[hh, hh * 64:(hh + 1) * 64] = 1.0
    DH, WIN = 64, 512
    half = np.arange(0, DH, 2, dtype=np.float64)
    inv_freq = 1.0 / 10000.0 ** (half / DH)
    base = (half + 0.4 * DH) / (1.4 * DH)
    sign = np.where(np.arange(64) < 32, -1.0, 1.0)
    for c, m in enumerate(in_maps):
        h = c % 2
        t = np.arange(h * 2048, h * 2048 + 2048, dtype=np.float64)
        fr = t[:, None] * inv_freq[None, :]
        fr = np.concatenate([fr, fr], 1)
        power = (t - 2048.0) / (WIN // 2)
        sc = base[None, :] ** power[:, None]
        sc = np.concatenate([sc, sc], 1)
        cos, sin = np.cos(fr), np.sin(fr)
        Aq = (cos * sc).T
        Bq = (sign[:, None] * (sin * sc).T)
        Ak = (cos / sc).T
        Bk = (sign[:, None] * (sin / sc).T)
        tabs = np.zeros((128, 4 * 2048), f32)
        for i, Tb in enumerate((Aq, Bq, Ak, Bk)):
            tabs[:64, i * 2048:(i + 1) * 2048] = Tb
            tabs[64:, i * 2048:(i + 1) * 2048] = Tb
        m["tabs"] = tabs
        edge = np.zeros((128, 2048), f32)
        if h == 0:
            for qt in range(4):
                edge[:, qt * 512:qt * 512 + (512 - qt * 128)] = -1e30
        m["edge"] = edge
        m["qst"] = qst
        m["tri"] = tri
        m["ident"] = ident
        m["hseg"] = hseg
        m["epair"] = epair
        m["ehset"] = ehset
        m["b1re"] = b1re
    return in_maps


_NC_CACHE = {}

def kernel(**inputs):
    in_maps = stage(inputs)
    if "nc" not in _NC_CACHE:
        _NC_CACHE["nc"] = build(debug=(), seg=99)
    nc, mkk = _NC_CACHE["nc"]
    res = run_bass_kernel_spmd(nc, in_maps, core_ids=list(range(8)))
    out = np.empty((4, 4096, 1024), np.float32)
    for c in range(8):
        b, h = c // 2, c % 2
        img = np.asarray(res.results[c]["out"], np.float32)
        fm = img.reshape(128, 8, 2048).transpose(1, 0, 2).reshape(1024, 2048)
        out[b, h * 2048:(h + 1) * 2048, :] = fm.T
    return out


# version 10 (final)
# speedup vs baseline: 1.0150x; 1.0150x over previous
import numpy as np
import concourse.bass as bass
import concourse.mybir as mybir
from concourse.bass_utils import run_bass_kernel_spmd

from contextlib import ExitStack

COMPUTE_STREAMS = ("T", "V", "S", "P")
QUEUE_STREAMS = ("Q",)
ALL_STREAMS = COMPUTE_STREAMS + QUEUE_STREAMS


class Op:
    __slots__ = ("stream", "fn", "reads", "writes", "dsem", "cc",
                 "idx", "need_inc", "waits", "tag")

    def __init__(self, stream, fn, reads, writes, dsem=None, cc=False, tag=""):
        self.stream = stream
        self.fn = fn
        self.reads = list(reads)
        self.writes = list(writes)
        self.dsem = dsem
        self.cc = cc
        self.idx = -1
        self.need_inc = False
        self.waits = []
        self.tag = tag


class Em:
    def __init__(self):
        self.ops = {s: [] for s in ALL_STREAMS}
        self.last_w = {}
        self.readers = {}
        self.dsem_count = {}

    def op(self, stream, fn, reads=(), writes=(), tag=""):
        o = Op(stream, fn, reads, writes, tag=tag)
        o.idx = len(self.ops[stream])
        self.ops[stream].append(o)
        tok = ("c", stream, o.idx)
        self._dep(o, tok)
        return tok

    def dma(self, qstream, fn, dsem, reads=(), writes=(), tag=""):
        if qstream == "Q":
            dsem = "pb_q%d" % len(self.ops["Q"])
        o = Op(qstream, fn, reads, writes, dsem=dsem, tag=tag)
        o.idx = len(self.ops[qstream])
        self.ops[qstream].append(o)
        self.dsem_count[dsem] = self.dsem_count.get(dsem, 0) + 1
        tok = ("d", dsem, self.dsem_count[dsem], qstream, o.idx)
        self._dep(o, tok)
        return tok

    def cc(self, fn, dsem, reads=(), writes=(), tag=""):
        o = Op("P", fn, reads, writes, dsem=dsem, cc=True, tag=tag)
        o.idx = len(self.ops["P"])
        self.ops["P"].append(o)
        self.dsem_count[dsem] = self.dsem_count.get(dsem, 0) + 1
        tok = ("x", dsem, self.dsem_count[dsem], "P", o.idx)
        self._dep(o, tok)
        return tok

    def inherit(self, new_region, old_regions):
        toks = []
        for r in old_regions:
            t = self.last_w.get(r)
            if t is not None:
                toks.append(t)
            toks.extend(self.readers.get(r, ()))
        if toks:
            self.readers.setdefault(new_region, []).extend(toks)

    def _dep(self, o, tok):
        for r in o.reads:
            t = self.last_w.get(r)
            if t is not None:
                o.waits.append(t)
            self.readers.setdefault(r, []).append(tok)
        for w in o.writes:
            t = self.last_w.get(w)
            if t is not None:
                o.waits.append(t)
            for rt in self.readers.get(w, ()):
                o.waits.append(rt)
            self.readers[w] = []
            self.last_w[w] = tok
        seen = set()
        uw = []
        for t in o.waits:
            if t[0] == "c" and t[1] == o.stream:
                if t[2] == o.idx:
                    continue
                if o.stream == "T":
                    continue
            if t in seen:
                continue
            seen.add(t)
            uw.append(t)
        o.waits = uw
        for t in o.waits:
            if t[0] == "c":
                self.ops[t[1]][t[2]].need_inc = True

    def build(self, nc, block, ctx):
        inc_no = {}
        for s in COMPUTE_STREAMS:
            n = 0
            for o in self.ops[s]:
                if o.need_inc:
                    n += 1
                    inc_no[(s, o.idx)] = n
        nxt = {}
        for s in COMPUTE_STREAMS:
            lst = self.ops[s]
            nf = None
            arr = [None] * len(lst)
            for i in range(len(lst) - 1, -1, -1):
                if lst[i].need_inc:
                    nf = i
                arr[i] = nf
            nxt[s] = arr

        qcnt = {}
        qtot = {}
        for s in ("Q", "P"):
            k = 0
            for o in self.ops[s]:
                if o.dsem is not None and not o.cc and not o.dsem.startswith("pb_"):
                    k += 1
                    qcnt[(s, o.idx)] = k
            qtot[s] = k

        def resolve(tok):
            if tok[0] == "c":
                _, s, i = tok
                j = nxt[s][i]
                assert j is not None, f"no inc for {tok}"
                return ("sem_" + s, inc_no[(s, j)])
            if tok[0] == "d":
                _, dsem, cnt, q, idx = tok
                if dsem.startswith("pb_"):
                    return (dsem, 16 * cnt)
                return ("dq_" + q, 16 * qcnt[(q, idx)])
            return (tok[1], tok[2])

        sems = {}
        for s in COMPUTE_STREAMS:
            sems["sem_" + s] = ctx.enter_context(nc.semaphore("sem_" + s))
        for s in ("Q", "P"):
            sems["dq_" + s] = ctx.enter_context(nc.semaphore("dq_" + s))
        for d in self.dsem_count:
            if d.startswith("cs_") or d.startswith("pb_"):
                sems[d] = ctx.enter_context(nc.semaphore(d))

        resolved = {s: [[resolve(t) for t in o.waits] for o in self.ops[s]]
                    for s in self.ops}
        knowA = {s: [] for s in self.ops}
        for s in self.ops:
            k = {}
            for o in self.ops[s]:
                for sem, v in resolved[s][o.idx]:
                    if k.get(sem, 0) < v:
                        k[sem] = v
                knowA[s].append(dict(k))

        def know_at(tok):
            if tok[0] == "c":
                _, s, i = tok
                return knowA[s][nxt[s][i]]
            _, dsem, cnt, s, i = tok
            return knowA[s][i]

        final_waits = {s: [None] * len(self.ops[s]) for s in self.ops}
        for s in self.ops:
            k = {}
            for o in self.ops[s]:
                ws = []
                for t, rv in zip(o.waits, resolved[s][o.idx]):
                    sem, v = rv
                    if k.get(sem, 0) >= v:
                        continue
                    ws.append((sem, v))
                    k[sem] = v
                    for ksem, kv in know_at(t).items():
                        if k.get(ksem, 0) < kv:
                            k[ksem] = kv
                final_waits[s][o.idx] = ws

        def section(stream):
            def go(eng):
                for o in self.ops[stream]:
                    for sem, v in final_waits[stream][o.idx]:
                        eng.wait_ge(sems[sem], v)
                    if o.dsem is not None and not o.cc and not o.dsem.startswith("pb_"):
                        k = qcnt[(stream, o.idx)]
                        if k > 1:
                            eng.wait_ge(sems["dq_" + stream], 16 * (k - 1))
                    inst = o.fn(eng)
                    if o.cc:
                        inst.then_inc(sems[o.dsem], 1)
                    elif o.dsem is not None and o.dsem.startswith("pb_"):
                        inst.then_inc(sems[o.dsem], 16)
                    elif o.dsem is not None:
                        inst.then_inc(sems["dq_" + stream], 16)
                    elif o.need_inc:
                        inst.then_inc(sems["sem_" + stream], 1)
            return go

        @block.sync
        def _(eng):
            section("Q")(eng)

        @block.vector
        def _(eng):
            section("V")(eng)

        @block.scalar
        def _(eng):
            section("S")(eng)

        @block.gpsimd
        def _(eng):
            section("P")(eng)

        @block.tensor
        def _(eng):
            section("T")(eng)
        return sems

    def stats(self):
        r = {}
        for s in self.ops:
            n = len(self.ops[s])
            ninc = sum(1 for o in self.ops[s] if o.need_inc or o.dsem)
            r[s] = (n, ninc)
        return r


class Arena:

    def __init__(self, em, total):
        self.em = em
        self.total = total
        self.live = {}
        self.addr = {}
        self.freed = []

    def alloc(self, name, size, align=16):
        req = size
        size = (size + align - 1) // align * align
        taken = sorted([(o, s) for (o, s) in self.live.values()])
        prev = 0
        best = None
        for (o, s) in taken + [(self.total, 0)]:
            if o - prev >= size:
                best = prev
                break
            prev = max(prev, o + s)
        assert best is not None, f"arena OOM for {name} size {size}: live={self.live}"
        off = best
        olds = []
        keep = []
        for (fo, fs, fn_) in self.freed:
            if fo < off + size and off < fo + fs:
                olds.append(fn_)
                if fo < off:
                    keep.append((fo, off - fo, fn_))
                if fo + fs > off + size:
                    keep.append((off + size, fo + fs - (off + size), fn_))
            else:
                keep.append((fo, fs, fn_))
        self.freed = keep
        if olds:
            self.em.inherit(name, olds)
        assert name not in self.addr, f"arena name reused: {name}"
        self.live[name] = (off, size)
        self.addr[name] = (off, req)
        return off

    def free(self, name, aliases=()):
        off, size = self.live.pop(name)
        self.freed.append((off, size, name))
        for a in aliases:
            self.freed.append((off, size, a))

    def off(self, name):
        return self.addr[name][0]


F32 = mybir.dt.float32
BF16 = mybir.dt.bfloat16
FP16 = mybir.dt.float16
AF = mybir.ActivationFunctionType
ALU = mybir.AluOpType
AX = mybir.AxisListType


CH = 1024
TL = 2048
W = TL
WH = TL + 1
EPS = 1e-5
ARW = 49664
WINS = [(0, 1), (1, 513), (513, 1025), (1025, 1537), (1537, 2049)]
WINS_W = [(0, 512), (512, 1024), (1024, 1536), (1536, 2048)]

VTMK, VTMV, VTMR, VG1, VB1, VG2, VB2, VFMK, VFMR, VTD, VTF = [8 * i for i in range(11)]
VLNG0, VLNB0, VLNG1, VLNB1, VB2F0, VB2F1 = [88 + 8 * i for i in range(6)]
NVEC = 136


SC_NK0, SC_EU, SC_LAM, SC_W, SC_TMP = 0, 8, 16, 24, 32


class MK:
    def __init__(self, nc, debug=()):
        self.nc = nc
        self.em = Em()
        self.debug = list(debug)
        self.dbg_shapes = {}
        self._names = set()

    def din(self, name, shape, dtype=F32):
        return self.nc.dram_tensor(name, list(shape), dtype, kind="ExternalInput").ap()

    def dout(self, name, shape, dtype=F32):
        return self.nc.dram_tensor(name, list(shape), dtype, kind="ExternalOutput").ap()

    def dint(self, name, shape, dtype=F32):
        return self.nc.dram_tensor(name, list(shape), dtype).ap()

    def T(self, fn, r, w, tag=""):
        return self.em.op("T", fn, r, w, tag=tag)

    def V(self, fn, r, w, tag=""):
        return self.em.op("V", fn, r, w, tag=tag)

    def S(self, fn, r, w, tag=""):
        return self.em.op("S", fn, r, w, tag=tag)

    def P(self, fn, r, w, tag=""):
        return self.em.op("P", fn, r, w, tag=tag)

    def dmaQ(self, fn, dsem, r, w, tag=""):
        return self.em.dma("Q", fn, dsem, r, w, tag=tag)

    def dmaG(self, fn, dsem, r, w, tag=""):
        return self.em.dma("P", fn, dsem, r, w, tag=tag)

    def dbg(self, name, ap_fn, shape, rgns, dtype=F32):
        if name not in self.debug:
            return
        d = self.dout("dbg_" + name, shape, dtype)
        self.dmaQ(lambda e: e.dma_start(out=d, in_=ap_fn()), "ds_dbg_" + name,
                  rgns, [], tag="dbg_" + name)
        self.dbg_shapes[name] = (shape, dtype)


def build(debug=(), seg=99, subseg=99):
    nc = bass.Bass("TRN2", target_bir_lowering=False)
    mk = MK(nc, debug=debug)
    em = mk.em

    xin = mk.din("xin", [128, 8 * WH])
    wk_d = mk.din("wk", [128, 8 * CH])
    wv_d = mk.din("wv", [128, 8 * CH])
    wr_d = mk.din("wr", [128, 8 * CH])
    wo_d = mk.din("wo", [128, 8 * CH])
    fk_d = mk.din("fk", [128, 8 * 4096])
    fv_d = mk.din("fv", [128, 32 * CH])
    fr_d = mk.din("fr", [128, 8 * CH])
    vecs_d = mk.din("vecs", [128, NVEC])
    tabs_d = mk.din("tabs", [128, 4 * 2048])
    qs_d = mk.din("qst", [128, 8])
    tri_d = mk.din("tri", [128, 256])
    edge_d = mk.din("edge", [128, 2048])
    ident_d = mk.din("ident", [128, 128])
    hseg_d = mk.din("hseg", [128, 128])
    epair_d = mk.din("epair", [16, 512])
    ehset_d = mk.din("ehset", [8, 512])
    b1_d = mk.din("b1re", [128, 88])
    wqkv_d = [mk.din(f"wqkv{l}", [128, 8 * 2560]) for l in range(2)]
    woff_d = [mk.din(f"woff{l}", [128, 4 * 1024]) for l in range(2)]
    w1f_d = [mk.din(f"w1f{l}", [128, 8 * 5632]) for l in range(2)]
    w2f_d = [mk.din(f"w2f{l}", [128, 22 * 1024]) for l in range(2)]
    iota_d = mk.din("iota1", [128, TL])
    onesr_d = mk.din("onesr", [1, 128])
    onesc_d = mk.din("onesc", [128, 1])
    smask_d = mk.din("smask", [128, 1])
    out_d = mk.dout("out", [128, 8 * W])
    mk.out_d = out_d

    agi = [mk.dint(f"agi{i}", [1, 384]) for i in range(2)]
    ago = [mk.dint(f"ago{i}", [2, 384]) for i in range(2)]
    aggi = mk.dint("aggi", [1, 1024])
    aggo = mk.dint("aggo", [2, 1024])
    tok_i = [mk.dint(f"toki{j}", [1, 1024]) for j in range(4)]
    tok_o = [mk.dint(f"toko{j}", [2, 1024]) for j in range(4)]
    hal_i = [mk.dint(f"hali{l}", [1, 128 * 4096], BF16) for l in range(2)]
    hal_o = [mk.dint(f"halo{l}", [2, 128 * 4096], BF16) for l in range(2)]

    s_xi = mk.dint("s_xi", [128, 8 * W])
    s_mid = mk.dint("s_mid", [128, 8 * W])
    s_x1 = mk.dint("s_x1", [128, 8 * W])

    AR_h = nc.alloc_sbuf_tensor("AR", [128, ARW], F32)
    AR = AR_h.ap()
    ARB = AR.bitcast(BF16)
    ARH = AR.bitcast(FP16)
    VEC_h = nc.alloc_sbuf_tensor("VEC", [128, NVEC], F32)
    VEC = VEC_h.ap()
    STATE_h = nc.alloc_sbuf_tensor("STATE", [128, 64], F32)
    STATE = STATE_h.ap()
    AGS_h = nc.alloc_sbuf_tensor("AGS", [128, 64], F32)
    AGS = AGS_h.ap()
    ONESR_h = nc.alloc_sbuf_tensor("ONESR", [1, 128], F32)
    ONESR = ONESR_h.ap()
    ONESB_h = nc.alloc_sbuf_tensor("ONESB", [128, 1], BF16)
    ONESB = ONESB_h.ap()
    SMASK_h = nc.alloc_sbuf_tensor("SMASK", [128, 1], F32)
    SMASK = SMASK_h.ap()
    CONST_h = nc.alloc_sbuf_tensor("CONST", [128, 8], F32)
    CONST = CONST_h.ap()
    TRI_h = nc.alloc_sbuf_tensor("TRI", [128, 256], BF16)
    TRI = TRI_h.ap()
    EDGE_h = nc.alloc_sbuf_tensor("EDGE", [128, 2048], BF16)
    EDGE = EDGE_h.ap()
    IDENT_h = nc.alloc_sbuf_tensor("IDENT", [128, 128], BF16)
    IDENT = IDENT_h.ap()
    IDENTF_h = nc.alloc_sbuf_tensor("IDENTF", [128, 128], F32)
    IDENTF = IDENTF_h.ap()
    HSEG_h = nc.alloc_sbuf_tensor("HSEG", [128, 128], BF16)
    HSEG = HSEG_h.ap()
    EPAIR_h = nc.alloc_sbuf_tensor("EPAIR", [16, 512], F32)
    EPAIR = EPAIR_h.ap()
    EHSET_h = nc.alloc_sbuf_tensor("EHSET", [8, 512], F32)
    EHSET = EHSET_h.ap()
    QS_h = nc.alloc_sbuf_tensor("QS", [128, 8], F32)
    QS = QS_h.ap()
    SUMS_h = nc.alloc_sbuf_tensor("SUMS", [128, 32], F32)
    SUMS = SUMS_h.ap()
    B1_h = nc.alloc_sbuf_tensor("B1", [128, 88], F32)
    B1 = B1_h.ap()
    RST_h = nc.alloc_sbuf_tensor("RST", [8, 128], F32)
    RST = RST_h.ap()
    BTOK_h = nc.alloc_sbuf_tensor("BTOK", [128, 32], F32)
    XBND_h = nc.alloc_sbuf_tensor("XBND", [128, 16], F32)
    XBND = XBND_h.ap()
    BTOK = BTOK_h.ap()

    PS_h = nc.alloc_psum_tensor("PS", [128, 4096], F32)
    PS = PS_h.ap()
    mk.AR = AR

    A = Arena(em, ARW)
    PA = Arena(em, 4096)

    def arf(name, lo=0, hi=None):
        off, size = A.addr[name]
        if hi is None:
            hi = size
        return AR[:, off + lo: off + hi]

    def arb(name, lo=0, hi=None):
        off, size = A.addr[name]
        if hi is None:
            hi = 2 * size
        return ARB[:, 2 * off + lo: 2 * off + hi]

    def arh(name, lo=0, hi=None):
        off, size = A.addr[name]
        if hi is None:
            hi = 2 * size
        return ARH[:, 2 * off + lo: 2 * off + hi]

    def psf(name, lo=0, hi=None):
        off, size = PA.addr[name]
        if hi is None:
            hi = size
        return PS[:, off + lo: off + hi]

    mk.dmaQ(lambda e: e.dma_start(out=VEC_h[:, :], in_=vecs_d), "ds_vec", [], ["vec"])
    
    mk.dmaQ(lambda e: e.dma_start(out=ONESR_h[:, :], in_=onesr_d), "ds_onesr", [], ["onesr"])
    mk.dmaQ(lambda e: e.dma_start(out=SMASK_h[:, :], in_=smask_d), "ds_smask", [], ["smask"])
    mk.dmaG(lambda e: e.dma_start(out=ONESB_h[:, :], in_=onesc_d), "dsg", [], ["onesb"])

    mk.dmaG(lambda e: e.dma_start(out=TRI_h[:, :], in_=tri_d), "ds_tri", [], ["tri"])
    mk.dmaG(lambda e: e.dma_start(out=EDGE_h[:, :], in_=edge_d), "ds_edge", [], ["edge"])
    mk.dmaG(lambda e: e.dma_start(out=IDENT_h[:, :], in_=ident_d), "ds_ident", [], ["ident"])
    mk.dmaQ(lambda e: e.dma_start(out=IDENTF_h[:, :], in_=ident_d), "ds_identf", [], ["identf"])
    mk.dmaG(lambda e: e.dma_start(out=HSEG_h[:, :], in_=hseg_d), "ds_hseg", [], ["hseg"])
    mk.dmaQ(lambda e: e.dma_start(out=EPAIR_h[:, :], in_=epair_d), "ds_epair", [], ["epair"])
    mk.dmaQ(lambda e: e.dma_start(out=EHSET_h[:, :], in_=ehset_d), "ds_ehset", [], ["ehset"])
    mk.dmaQ(lambda e: e.dma_start(out=QS_h[:, :], in_=qs_d), "ds_qs", [], ["qs"])
    mk.dmaQ(lambda e: e.dma_start(out=B1_h[:, :], in_=b1_d), "ds_b1", [], ["b1"])

    for ci, cv in ((0, EPS), (1, EPS + EPS * EPS), (2, 1e-30), (3, 1.0 + EPS), (4, 1.0)):
        def cf(e, ci=ci, cv=cv):
            return e.memset(CONST[:, ci:ci + 1], cv)
        mk.V(cf, [], ["const"], tag="const")

    mk.S(lambda e: e.activation(out=STATE[:, SC_W:SC_W + 8], in_=VEC[:, VTD:VTD + 8],
                                func=AF.Exp), ["vec"], ["st_w"])
    mk.V(lambda e: e.tensor_scalar_mul(out=STATE[:, SC_W:SC_W + 8],
                                       in0=STATE[:, SC_W:SC_W + 8], scalar1=-1.0),
         ["st_w"], ["st_w"])
    mk.S(lambda e: e.activation(out=STATE[:, SC_LAM:SC_LAM + 8],
                                in_=STATE[:, SC_W:SC_W + 8], func=AF.Exp),
         ["st_w"], ["st_lam"])
    mk.S(lambda e: e.activation(out=STATE[:, SC_EU:SC_EU + 8], in_=VEC[:, VTF:VTF + 8],
                                func=AF.Exp), ["vec"], ["st_eu"])

    def ln_stats(src_name, iw, wins, fold2=False, tag="ln", cast_name=None):
        n = wins[-1][1]
        sA, sB, sC, sD = (tag + "sA", tag + "sB", tag + "sC", tag + "sD")
        A.alloc(sA, n)
        A.alloc(sB, n)
        A.alloc(sC, n)
        A.alloc(sD, n)
        def row(rn, lo=0, hi=None):
            return arf({"skrA": sA, "skrB": sB, "skrC": sC, "skrD": sD}[rn], lo, hi)[0:1, :]
        if cast_name is None:
            A.alloc(tag + "_cast", (8 * iw + 1) // 2)
            rc = tag + "_cast"
            mk.P(lambda e: e.tensor_copy(out=arb(rc, 0, 8 * iw), in_=arf(src_name)),
                 [src_name], [rc], tag=tag + "cast")
        else:
            rc = cast_name
        A.alloc(tag + "_sq", (8 * iw + 1) // 2)
        rs = tag + "_sq"
        mk.S(lambda e: e.activation(out=arb(rs, 0, 8 * iw), in_=arb(rc, 0, 8 * iw),
                                    func=AF.Square), [rc], [rs], tag=tag + "sq")
        PA.alloc(tag + "_psum", 512)
        PA.alloc(tag + "_psq", 512)
        for (lo, hi) in wins:
            wl = hi - lo
            for c in range(8):
                st, sp = (c == 0), (c == 7)
                def f1(e, c=c, lo=lo, hi=hi, st=st, sp=sp, wl=wl):
                    return e.matmul(psf(tag + "_psum", 0, wl)[0:1, :], ONESB,
                                    arb(rc, c * iw + lo, c * iw + hi),
                                    start=st, stop=sp)
                mk.T(f1, [rc, "onesb"], [tag + "_psum"], tag=tag + "smm")
                def f2(e, c=c, lo=lo, hi=hi, st=st, sp=sp, wl=wl):
                    return e.matmul(psf(tag + "_psq", 0, wl)[0:1, :], ONESB,
                                    arb(rs, c * iw + lo, c * iw + hi),
                                    start=st, stop=sp)
                mk.T(f2, [rs, "onesb"], [tag + "_psq"], tag=tag + "qmm")
            def e1(e, lo=lo, wl=hi - lo):
                return e.tensor_copy(out=row("skrA", lo, lo + wl),
                                     in_=psf(tag + "_psum", 0, wl)[0:1, :])
            mk.V(e1, [tag + "_psum"], [tag + "_sum"], tag=tag + "sev")
            def e2(e, lo=lo, wl=hi - lo):
                return e.tensor_copy(out=row("skrB", lo, lo + wl),
                                     in_=psf(tag + "_psq", 0, wl)[0:1, :])
            mk.V(e2, [tag + "_psq"], [tag + "_sq_r"], tag=tag + "qev")
        PA.free(tag + "_psum")
        PA.free(tag + "_psq")
        if cast_name is None:
            A.free(rc)
        A.free(rs)
        mk.V(lambda e: e.tensor_scalar_mul(out=row("skrA"), in0=row("skrA"),
                                           scalar1=1.0 / CH), [tag + "_sum"], [tag + "_sum"])
        mk.V(lambda e: e.tensor_scalar_mul(out=row("skrB"), in0=row("skrB"),
                                           scalar1=1.0 / CH), [tag + "_sq_r"], [tag + "_sq_r"])
        mk.S(lambda e: e.activation(out=row("skrD"), in_=row("skrA"), func=AF.Square),
             [tag + "_sum"], [tag + "_mr1"])
        mk.V(lambda e: e.tensor_sub(out=row("skrB"), in0=row("skrB"), in1=row("skrD")),
             [tag + "_sq_r", tag + "_mr1"], [tag + "_sq_r"])
        if fold2:
            mk.S(lambda e: e.activation(out=row("skrC"), in_=row("skrB"), func=AF.Ln,
                                        scale=CONST[0:1, 3:4], bias=CONST[0:1, 1:2]),
                 [tag + "_sq_r", "const"], [tag + "_r2"])
            mk.S(lambda e: e.activation(out=row("skrC"), in_=row("skrC"), func=AF.Exp,
                                        scale=-0.5), [tag + "_r2"], [tag + "_r2"])
        mk.S(lambda e: e.activation(out=row("skrB"), in_=row("skrB"), func=AF.Ln,
                                    bias=CONST[0:1, 0:1]), [tag + "_sq_r", "const"], [tag + "_r1"])
        mk.S(lambda e: e.activation(out=row("skrB"), in_=row("skrB"), func=AF.Exp,
                                    scale=-0.5), [tag + "_r1"], [tag + "_r1"])
        mk.V(lambda e: e.tensor_mul(out=row("skrD"), in0=row("skrA"), in1=row("skrB")),
             [tag + "_sum", tag + "_r1"], [tag + "_mr1"])
        if fold2:
            mk.V(lambda e: e.tensor_mul(out=row("skrA"), in0=row("skrA"), in1=row("skrC")),
                 [tag + "_sum", tag + "_r2"], [tag + "_mr2"])
        else:
            A.free(sA, [tag + "_sum", tag + "_mr2"])
            A.free(sC, [tag + "_r2"])

    def bcast_row(tag, which, lo, wl, pname):
        rn = {"R1": tag + "sB", "R2": tag + "sC", "MR1": tag + "sD", "MR2": tag + "sA"}[which]
        rk = {"R1": tag + "_r1", "R2": tag + "_r2", "MR1": tag + "_mr1", "MR2": tag + "_mr2"}[which]
        def f(e):
            return e.matmul(psf(pname, 0, wl), ONESR,
                            arf(rn, lo, lo + wl)[0:1, :], start=True, stop=True)
        mk.T(f, [rk, "onesr"], [pname], tag="bcast")

    A.alloc("x0", 8 * WH)
    mk.dmaQ(lambda e: e.dma_start(out=arf("x0"), in_=xin), "ds_x0", [], ["x0"])
    ln_stats("x0", WH, WINS, fold2=True, tag="ln1")
    A.alloc("h", 8 * WH)
    A.alloc("xiw", 8 * 512)
    p1 = PA.alloc("bc1", 512)
    p2 = PA.alloc("bc2", 512)
    p3 = PA.alloc("bc3", 512)
    p4 = PA.alloc("bc4", 512)
    for wi, (lo, hi) in enumerate(WINS):
        wl = hi - lo
        bcast_row("ln1", "R1", lo, wl, "bc1")
        bcast_row("ln1", "MR1", lo, wl, "bc2")
        bcast_row("ln1", "R2", lo, wl, "bc3")
        bcast_row("ln1", "MR2", lo, wl, "bc4")
        for c in range(8):
            if lo >= 1:
                def a1(e, c=c, lo=lo, wl=wl):
                    return e.tensor_mul(out=arf("xiw", c * 512, c * 512 + wl),
                                        in0=arf("x0", c * WH + lo, c * WH + lo + wl),
                                        in1=psf("bc1", 0, wl))
                mk.V(a1, ["x0", "bc1"], ["xiw"], tag="xiap1")
                def a2(e, c=c, wl=wl):
                    return e.tensor_sub(out=arf("xiw", c * 512, c * 512 + wl),
                                        in0=arf("xiw", c * 512, c * 512 + wl),
                                        in1=psf("bc2", 0, wl))
                mk.V(a2, ["bc2"], ["xiw"], tag="xiap2")
            def a3(e, c=c, lo=lo, wl=wl):
                return e.tensor_mul(out=arf("h", c * WH + lo, c * WH + lo + wl),
                                    in0=arf("x0", c * WH + lo, c * WH + lo + wl),
                                    in1=psf("bc3", 0, wl))
            mk.V(a3, ["x0", "bc3"], ["h"], tag="hap1")
            def a4(e, c=c, lo=lo, wl=wl):
                return e.tensor_sub(out=arf("h", c * WH + lo, c * WH + lo + wl),
                                    in0=arf("h", c * WH + lo, c * WH + lo + wl),
                                    in1=psf("bc4", 0, wl))
            mk.V(a4, ["bc4"], ["h"], tag="hap2")
            def a5(e, c=c, lo=lo, wl=wl):
                return e.tensor_scalar(out=arf("h", c * WH + lo, c * WH + lo + wl),
                                       in0=arf("h", c * WH + lo, c * WH + lo + wl),
                                       scalar1=VEC[:, VG1 + c:VG1 + c + 1],
                                       scalar2=VEC[:, VB1 + c:VB1 + c + 1],
                                       op0=ALU.mult, op1=ALU.add)
            mk.P(a5, ["vec"], ["h"], tag="hap3")
        if lo >= 1:
            def sp(e, lo=lo, wl=wl):
                pat = [[8 * W, 128]] + [[W, 8], [1, wl]]
                return e.dma_start(
                    out=bass.AP(s_xi.tensor, lo - 1, pat),
                    in_=AR[:, A.off("xiw"):A.off("xiw") + 8 * 512].rearrange(
                        "p (c t) -> p c t", c=8)[:, :, 0:wl])
            mk.dmaQ(sp, "ds_sxi", ["xiw"], ["s_xi"], tag="spillxi")
    for nm in ("bc1", "bc2", "bc3", "bc4"):
        PA.free(nm)
    for c in range(8):
        def hz(e, c=c):
            return e.tensor_scalar_mul(out=arf("h", c * WH, c * WH + 1),
                                       in0=arf("h", c * WH, c * WH + 1),
                                       scalar1=SMASK[:, 0:1])
        mk.V(hz, ["smask"], ["h"], tag="hzero")
    A.free("x0")
    A.free("xiw")
    A.free("ln1sA", ["ln1_sum", "ln1_mr2"])
    A.free("ln1sB", ["ln1_sq_r", "ln1_r1"])
    A.free("ln1sC", ["ln1_r2"])
    A.free("ln1sD", ["ln1_mr1"])
    mk.dbg("h", lambda: arf("h"), [128, 8 * WH], ["h"])
    if seg < 2:
        return finish(nc, mk, A, PA)

    A.alloc("mxk", 8 * W // 2)
    A.alloc("mxv", 8 * W // 2)
    A.alloc("mxr", 8 * W // 2)
    A.alloc("dmix", W)
    for c in range(8):
        def d1(e, c=c):
            return e.tensor_sub(out=arf("dmix"),
                                in0=arf("h", c * WH + 1, c * WH + WH),
                                in1=arf("h", c * WH, c * WH + W))
        mk.V(d1, ["h"], ["dmix"], tag="dmix")
        for nm, vcol in (("mxr", VTMR), ("mxk", VTMK), ("mxv", VTMV)):
            def mx(e, c=c, nm=nm, vcol=vcol):
                return e.scalar_tensor_tensor(
                    out=arb(nm, c * W, c * W + W), in0=arf("dmix"),
                    scalar=VEC[:, vcol + c:vcol + c + 1],
                    in1=arf("h", c * WH, c * WH + W),
                    op0=ALU.mult, op1=ALU.add)
            mk.V(mx, ["dmix", "h", "vec"], [nm], tag=nm)
    A.free("dmix")
    A.free("h")
    mk.dbg("xk", lambda: arb("mxk"), [128, 8 * W], ["mxk"], BF16)

    A.alloc("wg0", 4096)
    A.alloc("wg1", 4096)
    mk.dmaG(lambda e: e.dma_start(out=arb("wg0"), in_=wr_d), "pb_wg0", [], ["wg0"])

    def mm_img(pname, plo, wname, m, rhs_name, rhs_iw, rlo, rhi, nk=8, tag="mm"):
        for k in range(nk):
            st, sp = (k == 0), (k == nk - 1)
            def f(e, k=k, st=st, sp=sp):
                return e.matmul(psf(pname, plo, plo + (rhi - rlo)),
                                arb(wname, k * CH + m * 128, k * CH + m * 128 + 128),
                                arb(rhs_name, k * rhs_iw + rlo, k * rhs_iw + rhi),
                                start=st, stop=sp)
            mk.T(f, [wname, rhs_name], [pname], tag=tag)

    A.alloc("sry", 8 * W // 2)
    PA.alloc("rp0", 512)
    PA.alloc("rp1", 512)
    for m in range(8):
        for wi, (lo, hi) in enumerate(WINS_W):
            pn = "rp%d" % (wi % 2)
            mm_img(pn, 0, "wg0", m, "mxr", W, lo, hi, tag="rmm")
            def sev(e, m=m, lo=lo, hi=hi, pn=pn):
                return e.activation(out=arh("sry", m * W + lo, m * W + hi),
                                    in_=psf(pn, 0, hi - lo), func=AF.Sigmoid)
            mk.S(sev, [pn], ["sry%d" % m], tag="srev")
    PA.free("rp0")
    PA.free("rp1")
    A.free("mxr")
    mk.dbg("sr", lambda: arh("sry"), [128, 8 * W], [f"sry{m}" for m in range(8)], FP16)

    mk.dmaG(lambda e: e.dma_start(out=arb("wg0"), in_=wk_d), "pb_wg0", [], ["wg0"])
    mk.dmaG(lambda e: e.dma_start(out=arb("wg1"), in_=wv_d), "pb_wg1", [], ["wg1"])

    for i in range(1):
        A.alloc(f"kv{i}", 2 * W)
        A.alloc(f"ab{i}", 2 * WH)
    A.alloc("nd0", 2 * W)
    A.alloc("nd1", 2 * W)
    PA.alloc("pwp", 2048)
    PA.alloc("kp0", 512)
    PA.alloc("kp1", 512)

    NSET = 1
    for m in range(8):
        i = m % NSET
        kv, ab, nd = f"kv{i}", f"ab{i}", f"nd{m % 2}"
        rk, rv = f"k{m}", f"v{m}"
        rab, rnd = f"ab{m}", f"nd{m}"
        if m >= NSET:
            em.inherit(rk, [f"k{m-NSET}"])
            em.inherit(rv, [f"v{m-NSET}"])
            em.inherit(rab, [f"ab{m-NSET}"])
        if m >= 2:
            em.inherit(rnd, [f"nd{m-2}"])
        for wi, (lo, hi) in enumerate(WINS_W):
            pn = "kp%d" % (wi % 2)
            mm_img(pn, 0, "wg0", m, "mxk", W, lo, hi, tag="kmm")
            def kev(e, lo=lo, hi=hi, pn=pn, kv=kv):
                return e.copy(out=arf(kv, lo, hi), in_=psf(pn, 0, hi - lo))
            mk.S(kev, [pn], [rk], tag="kev")
            mm_img(pn, 0, "wg1", m, "mxv", W, lo, hi, tag="vmm")
            def vev(e, lo=lo, hi=hi, pn=pn, kv=kv):
                return e.copy(out=arf(kv, W + lo, W + hi), in_=psf(pn, 0, hi - lo))
            mk.S(vev, [pn], [rv], tag="vev")
        def nmx(e, kv=kv, m=m):
            return e.reduce_max(out=STATE[:, SC_NK0 + m:SC_NK0 + m + 1],
                                in_=arf(kv, 0, W), axis=AX.X, negate=True)
        mk.V(nmx, [rk], [f"nk0_{m}"], tag="nk0")
        def ekf(e, kv=kv, m=m):
            return e.activation(out=arf(kv, 0, W), in_=arf(kv, 0, W), func=AF.Exp,
                                bias=STATE[:, SC_NK0 + m:SC_NK0 + m + 1])
        mk.S(ekf, [rk, f"nk0_{m}"], [rk], tag="ek")
        def ekvf(e, kv=kv):
            return e.tensor_mul(out=arf(kv, W, 2 * W), in0=arf(kv, W, 2 * W),
                                in1=arf(kv, 0, W))
        mk.V(ekvf, [rk, rv], [rv], tag="ekv")
        def z0(e, ab=ab):
            return e.memset(arf(ab, 0, 1), 0.0)
        mk.V(z0, [], [rab], tag="scz")
        def z1(e, ab=ab):
            return e.memset(arf(ab, WH, WH + 1), 0.0)
        mk.V(z1, [], [rab], tag="scz")
        lam_ap = bass.AP(STATE.tensor, SC_LAM + m, [[64, 128], [0, W]])
        def sca(e, ab=ab, kv=kv, lam_ap=lam_ap):
            return e.tensor_tensor_scan(out=arf(ab, 1, WH), data0=lam_ap,
                                        data1=arf(kv, W, 2 * W), initial=0.0,
                                        op0=ALU.mult, op1=ALU.add)
        mk.V(sca, [rv, "st_lam"], [rab], tag="scanA")
        def scb(e, ab=ab, kv=kv, lam_ap=lam_ap):
            return e.tensor_tensor_scan(out=arf(ab, WH + 1, 2 * WH), data0=lam_ap,
                                        data1=arf(kv, 0, W), initial=0.0,
                                        op0=ALU.mult, op1=ALU.add)
        mk.V(scb, [rk, "st_lam"], [rab], tag="scanB")
        ia = m % 2
        if m >= 2:
            em.inherit(f"agx{m}", [f"agx{m-2}"])
            em.inherit(f"agr2_{m}", [f"agr2_{m-2}"])
        def ex1(e, ab=ab, ia=ia):
            return e.tensor_copy(out=AGS[:, ia * 3:ia * 3 + 1], in_=arf(ab, W, WH))
        mk.V(ex1, [rab], [f"agx{m}"], tag="exA")
        def ex2(e, ab=ab, ia=ia):
            return e.tensor_copy(out=AGS[:, ia * 3 + 1:ia * 3 + 2], in_=arf(ab, WH + W, 2 * WH))
        mk.V(ex2, [rab], [f"agx{m}"], tag="exB")
        def ex3(e, ia=ia, m=m):
            return e.tensor_copy(out=AGS[:, ia * 3 + 2:ia * 3 + 3],
                                 in_=STATE[:, SC_NK0 + m:SC_NK0 + m + 1])
        mk.V(ex3, [f"nk0_{m}"], [f"agx{m}"], tag="exK")
        def nml(e, nd=nd, kv=kv, ab=ab, m=m):
            return e.scalar_tensor_tensor(out=arf(nd, 0, W), in0=arf(kv, W, 2 * W),
                                          scalar=STATE[:, SC_EU + m:SC_EU + m + 1],
                                          in1=arf(ab, 0, W), op0=ALU.mult, op1=ALU.add)
        mk.V(nml, [rv, rab, "st_eu"], [rnd], tag="num")
        def dnl(e, nd=nd, kv=kv, ab=ab, m=m):
            return e.scalar_tensor_tensor(out=arf(nd, W, 2 * W), in0=arf(kv, 0, W),
                                          scalar=STATE[:, SC_EU + m:SC_EU + m + 1],
                                          in1=arf(ab, WH, WH + W), op0=ALU.mult, op1=ALU.add)
        mk.V(dnl, [rk, rab, "st_eu"], [rnd], tag="den")
        ia = m % 2
        if m >= 2:
            em.inherit(f"agi_{m}", [f"agi_{m-2}"])
            em.inherit(f"ago_{m}", [f"ago_{m-2}"])
            em.inherit(f"agr_{m}", [f"agr_{m-2}"])
        def dout_f(e, ia=ia):
            return e.dma_start(out=bass.AP(agi[ia].tensor, 0, [[3, 128], [1, 3]]),
                               in_=AGS[:, ia * 3:ia * 3 + 3])
        mk.dmaG(dout_f, f"ds_agi{ia}", [f"agx{m}"], [f"agi_{m}"], tag="agdma")
        def ccf(e, ia=ia):
            return e.collective_compute(
                "AllGather", ALU.bypass,
                replica_groups=[[0, 1], [2, 3], [4, 5], [6, 7]],
                ins=[agi[ia].opt()], outs=[ago[ia].opt()])
        em.cc(ccf, f"cs_ag{ia}", [f"agi_{m}"], [f"ago_{m}"], tag="agcc")
        def din_f(e, ia=ia):
            return e.dma_start(out=AGS[:, 32 + ia * 3:32 + ia * 3 + 3],
                               in_=bass.AP(ago[ia].tensor, 0, [[3, 128], [1, 3]]))
        mk.dmaG(din_f, f"ds_ago{ia}", [f"ago_{m}"], [f"agr2_{m}"], tag="agback")
        def sp1(e, ia=ia, m=m):
            return e.tensor_sub(out=STATE[:, SC_TMP + 2:SC_TMP + 3],
                                in0=STATE[:, SC_NK0 + m:SC_NK0 + m + 1],
                                in1=AGS[:, 32 + ia * 3 + 2:32 + ia * 3 + 3])
        mk.V(sp1, [f"agr2_{m}", f"nk0_{m}"], ["st_f"], tag="stf1")
        mk.S(lambda e: e.activation(out=STATE[:, SC_TMP + 2:SC_TMP + 3],
                                    in_=STATE[:, SC_TMP + 2:SC_TMP + 3], func=AF.Exp),
             ["st_f"], ["st_f"], tag="stf2")
        mk.V(lambda e: e.tensor_mul(out=STATE[:, SC_TMP + 2:SC_TMP + 3],
                                    in0=STATE[:, SC_TMP + 2:SC_TMP + 3],
                                    in1=SMASK[:, 0:1]),
             ["st_f", "smask"], ["st_f"], tag="stf3")
        def spA(e, ia=ia):
            return e.tensor_mul(out=STATE[:, SC_TMP:SC_TMP + 1],
                                in0=AGS[:, 32 + ia * 3:32 + ia * 3 + 1],
                                in1=STATE[:, SC_TMP + 2:SC_TMP + 3])
        mk.V(spA, [f"agr2_{m}", "st_f"], ["st_ab"], tag="stA")
        def spB(e, ia=ia):
            return e.tensor_mul(out=STATE[:, SC_TMP + 1:SC_TMP + 2],
                                in0=AGS[:, 32 + ia * 3 + 1:32 + ia * 3 + 2],
                                in1=STATE[:, SC_TMP + 2:SC_TMP + 3])
        mk.V(spB, [f"agr2_{m}", "st_f"], ["st_ab"], tag="stB")
        lam_b2 = bass.AP(STATE.tensor, SC_LAM + m, [[64, 128], [0, W]])
        one_b = bass.AP(CONST.tensor, 4, [[8, 128], [0, W]])
        def pwf(e, m=m, lam_b2=lam_b2, one_b=one_b):
            return e.tensor_tensor_scan(out=psf("pwp", 0, 2048), data0=lam_b2,
                                        data1=one_b, initial=1.0,
                                        op0=ALU.mult, op1=ALU.mult)
        mk.V(pwf, ["st_lam", "const"], ["pw"], tag="pw")
        def c0(e, nd=nd):
            return e.tensor_add(out=arf(nd, 0, 1), in0=arf(nd, 0, 1),
                                in1=STATE[:, SC_TMP:SC_TMP + 1])
        mk.V(c0, ["st_ab"], [rnd], tag="corr")
        def c1(e, nd=nd):
            return e.scalar_tensor_tensor(out=arf(nd, 1, W), in0=psf("pwp", 0, W - 1),
                                          scalar=STATE[:, SC_TMP:SC_TMP + 1],
                                          in1=arf(nd, 1, W), op0=ALU.mult, op1=ALU.add)
        mk.V(c1, ["pw", "st_ab"], [rnd], tag="corr")
        def c2(e, nd=nd):
            return e.tensor_add(out=arf(nd, W, W + 1), in0=arf(nd, W, W + 1),
                                in1=STATE[:, SC_TMP + 1:SC_TMP + 2])
        mk.V(c2, ["st_ab"], [rnd], tag="corr")
        def c3(e, nd=nd):
            return e.scalar_tensor_tensor(out=arf(nd, W + 1, 2 * W), in0=psf("pwp", 0, W - 1),
                                          scalar=STATE[:, SC_TMP + 1:SC_TMP + 2],
                                          in1=arf(nd, W + 1, 2 * W), op0=ALU.mult, op1=ALU.add)
        mk.V(c3, ["pw", "st_ab"], [rnd], tag="corr")
        def rc_(e, nd=nd):
            return e.reciprocal(out=arf(nd, W, 2 * W), in_=arf(nd, W, 2 * W))
        mk.V(rc_, [rnd], [rnd], tag="recip")
        def ym(e, nd=nd):
            return e.tensor_mul(out=arf(nd, 0, W), in0=arf(nd, 0, W),
                                in1=arf(nd, W, 2 * W))
        mk.V(ym, [rnd], [rnd], tag="ymul")
        def ys(e, nd=nd, m=m):
            return e.tensor_mul(out=arb("sry", m * W, (m + 1) * W), in0=arf(nd, 0, W),
                                in1=arh("sry", m * W, (m + 1) * W))
        mk.V(ys, [rnd], [f"y{m}", f"sry{m}"], tag="ysr")
    for i in range(1):
        A.free(f"kv{i}", [f"k{m}" for m in range(8)] + [f"v{m}" for m in range(8)])
        A.free(f"ab{i}", [f"ab{m}" for m in range(8)])
    A.free("nd0", [f"nd{m}" for m in range(0, 8, 2)])
    A.free("nd1", [f"nd{m}" for m in range(1, 8, 2)])
    PA.free("pwp")
    A.free("mxk")
    A.free("mxv")
    PA.free("kp0")
    PA.free("kp1")
    mk.dbg("wkvy", lambda: arb("sry"), [128, 8 * W],
           [f"y{m}" for m in range(8)], BF16)
    if seg < 3:
        return finish(nc, mk, A, PA)

    mk.dmaG(lambda e: e.dma_start(out=arb("wg0"), in_=wo_d), "pb_wg0", [], ["wg0"])
    A.alloc("xi2", 8 * W)
    mk.dmaQ(lambda e: e.dma_start(out=arf("xi2"), in_=s_xi.tensor.ap()), "ds_sxi2",
            ["s_xi"], ["xi2"], tag="xilod")
    PA.alloc("wop0", 512)
    PA.alloc("wop1", 512)
    yrd = [f"y{m}" for m in range(8)]
    for mo in range(8):
        for wi, (lo, hi) in enumerate(WINS_W):
            pn = "wop%d" % (wi % 2)
            for k in range(8):
                st, sp = (k == 0), (k == 7)
                def f(e, k=k, mo=mo, lo=lo, hi=hi, st=st, sp=sp, pn=pn):
                    return e.matmul(psf(pn, 0, hi - lo),
                                    arb("wg0", k * CH + mo * 128, k * CH + mo * 128 + 128),
                                    arb("sry", k * W + lo, k * W + hi),
                                    start=st, stop=sp)
                mk.T(f, ["wg0"] + yrd, [pn], tag="womm")
            def ev(e, mo=mo, lo=lo, hi=hi, pn=pn):
                return e.tensor_add(out=arf("xi2", mo * W + lo, mo * W + hi),
                                    in0=arf("xi2", mo * W + lo, mo * W + hi),
                                    in1=psf(pn, 0, hi - lo))
            mk.V(ev, [pn, "xi2"], ["xi2"], tag="woev")
    PA.free("wop0")
    PA.free("wop1")
    A.free("sry", [f"sry{m}" for m in range(8)] + [f"y{m}" for m in range(8)])
    A.free("wg0")
    A.free("wg1")
    mk.dbg("xmid", lambda: arf("xi2"), [128, 8 * W], ["xi2"])
    mk.dmaQ(lambda e: e.dma_start(out=s_mid.tensor.ap(), in_=arf("xi2")), "ds_smid",
            ["xi2"], ["s_mid"], tag="midspill")
    ln_stats("xi2", W, WINS_W, fold2=False, tag="ln2")
    if seg < 4:
        return finish(nc, mk, A, PA)

    def stats_win(srcname, off, wl, tag):
        A.alloc(tag + "B", wl)
        A.alloc(tag + "D", wl)
        cb = tag + "_c"
        A.alloc(cb, (8 * wl + 1) // 2)
        def cst(e):
            return e.tensor_copy(out=arb(cb, 0, 8 * wl),
                                 in_=arf(srcname, off, off + 8 * wl))
        mk.P(cst, [srcname], [cb], tag=tag + "c")
        PA.alloc(tag + "_ps", 512)
        PA.alloc(tag + "_pq", 512)
        sqb = tag + "_s"
        A.alloc(sqb, (8 * wl + 1) // 2)
        def sqf(e):
            return e.activation(out=arb(sqb, 0, 8 * wl), in_=arb(cb, 0, 8 * wl),
                                func=AF.Square)
        mk.S(sqf, [cb], [sqb], tag=tag + "q")
        for c in range(8):
            st, sp = (c == 0), (c == 7)
            def f1(e, c=c, st=st, sp=sp):
                return e.matmul(psf(tag + "_ps", 0, wl)[0:1, :], ONESB,
                                arb(cb, c * wl, (c + 1) * wl), start=st, stop=sp)
            mk.T(f1, [cb, "onesb"], [tag + "_ps"], tag=tag + "m1")
            def f2(e, c=c, st=st, sp=sp):
                return e.matmul(psf(tag + "_pq", 0, wl)[0:1, :], ONESB,
                                arb(sqb, c * wl, (c + 1) * wl), start=st, stop=sp)
            mk.T(f2, [sqb, "onesb"], [tag + "_pq"], tag=tag + "m2")
        def rw(nm, lo=0, hi=None):
            return arf(nm, lo, hi)[0:1, :]
        def e1(e):
            return e.tensor_scalar_mul(out=rw(tag + "D"), in0=psf(tag + "_ps", 0, wl)[0:1, :],
                                       scalar1=1.0 / CH)
        mk.V(e1, [tag + "_ps"], [tag + "_mean"], tag=tag + "e1")
        def e2(e):
            return e.tensor_scalar_mul(out=rw(tag + "B"), in0=psf(tag + "_pq", 0, wl)[0:1, :],
                                       scalar1=1.0 / CH)
        mk.V(e2, [tag + "_pq"], [tag + "_ex2"], tag=tag + "e2")
        PA.free(tag + "_ps")
        PA.free(tag + "_pq")
        A.free(cb)
        A.free(sqb)
        A.alloc(tag + "T", wl)
        def e3(e):
            return e.activation(out=rw(tag + "T"), in_=rw(tag + "D"), func=AF.Square)
        mk.S(e3, [tag + "_mean"], [tag + "_msq"], tag=tag + "e3")
        def e4(e):
            return e.tensor_sub(out=rw(tag + "B"), in0=rw(tag + "B"), in1=rw(tag + "T"))
        mk.V(e4, [tag + "_ex2", tag + "_msq"], [tag + "_var"], tag=tag + "e4")
        def e5(e):
            return e.activation(out=rw(tag + "B"), in_=rw(tag + "B"), func=AF.Ln,
                                bias=CONST[0:1, 0:1])
        mk.S(e5, [tag + "_var", "const"], [tag + "_r"], tag=tag + "e5")
        def e6(e):
            return e.activation(out=rw(tag + "B"), in_=rw(tag + "B"), func=AF.Exp,
                                scale=-0.5)
        mk.S(e6, [tag + "_r"], [tag + "_r"], tag=tag + "e6")
        def e7(e):
            return e.tensor_mul(out=rw(tag + "D"), in0=rw(tag + "D"), in1=rw(tag + "B"))
        mk.V(e7, [tag + "_mean", tag + "_r"], [tag + "_mr"], tag=tag + "e7")
        A.free(tag + "T", [tag + "_msq"])
        return tag + "B", tag + "D", tag + "_r", tag + "_mr"

    def bc2(tag, pn1, pn2, wl, lo=0):
        def f1(e):
            return e.matmul(psf(pn1, 0, wl), ONESR, arf(tag + "B", lo, lo + wl)[0:1, :],
                            start=True, stop=True)
        mk.T(f1, [tag + "_r", "onesr"], [pn1], tag="bc2a")
        def f2(e):
            return e.matmul(psf(pn2, 0, wl), ONESR, arf(tag + "D", lo, lo + wl)[0:1, :],
                            start=True, stop=True)
        mk.T(f2, [tag + "_mr", "onesr"], [pn2], tag="bc2b")

    PA.alloc("gbt1", 512)
    PA.alloc("gbt2", 512)
    def gb1(e):
        return e.matmul(psf("gbt1", 0, 1), ONESR, arf("ln2sB", 2047, 2048)[0:1, :],
                        start=True, stop=True)
    mk.T(gb1, ["ln2_r1", "onesr"], ["gbt1"], tag="gbt")
    def gb2(e):
        return e.matmul(psf("gbt2", 0, 1), ONESR, arf("ln2sD", 2047, 2048)[0:1, :],
                        start=True, stop=True)
    mk.T(gb2, ["ln2_mr1", "onesr"], ["gbt2"], tag="gbt")
    for c in range(8):
        def g1(e, c=c):
            return e.tensor_mul(out=BTOK[:, c:c + 1],
                                in0=arf("xi2", c * W + 2047, c * W + 2048),
                                in1=psf("gbt1", 0, 1))
        mk.V(g1, ["xi2", "gbt1"], ["btok"], tag="gbt")
        def g2(e, c=c):
            return e.tensor_sub(out=BTOK[:, c:c + 1], in0=BTOK[:, c:c + 1],
                                in1=psf("gbt2", 0, 1))
        mk.V(g2, ["gbt2"], ["btok"], tag="gbt")
        def g3(e, c=c):
            return e.tensor_scalar(out=BTOK[:, c:c + 1], in0=BTOK[:, c:c + 1],
                                   scalar1=VEC[:, VG2 + c:VG2 + c + 1],
                                   scalar2=VEC[:, VB2 + c:VB2 + c + 1],
                                   op0=ALU.mult, op1=ALU.add)
        mk.V(g3, ["vec"], ["btok"], tag="gbt")
    PA.free("gbt1")
    PA.free("gbt2")
    mk.dmaG(lambda e: e.dma_start(out=bass.AP(aggi.tensor, 0, [[8, 128], [1, 8]]),
                                  in_=BTOK[:, 0:8]), "ds_aggi", ["btok"], ["aggi"], tag="gtokd")
    em.cc(lambda e: e.collective_compute("AllGather", ALU.bypass,
                                         replica_groups=[[0, 1], [2, 3], [4, 5], [6, 7]],
                                         ins=[aggi.opt()], outs=[aggo.opt()]),
          "cs_gtok", ["aggi"], ["aggo"], tag="gtokc")
    mk.dmaG(lambda e: e.dma_start(out=BTOK[:, 8:16],
                                  in_=bass.AP(aggo.tensor, 0, [[8, 128], [1, 8]])),
            "ds_aggo", ["aggo"], ["btokr"], tag="gtokr")
    A.free("xi2")

    A.alloc("fvw", 16384)
    A.alloc("frw", 4096)
    mk.dmaG(lambda e: e.dma_start(out=arb("fvw"), in_=fv_d), "pb_fvw", [], ["fvw"])
    mk.dmaG(lambda e: e.dma_start(out=arb("frw"), in_=fr_d), "pb_frw", [], ["frw"])

    for wi, (lo, hi) in enumerate(WINS_W):
        wl = 512
        xmw, gw = f"xmw{wi}", f"gw{wi}"
        A.alloc(xmw, 8 * wl)
        A.alloc(gw, 8 * 513)
        def xml(e, lo=lo, xmw=xmw):
            return e.dma_start(
                out=AR[:, A.off(xmw):A.off(xmw) + 8 * 512].rearrange(
                    "p (c t) -> p c t", c=8),
                in_=bass.AP(s_mid.tensor, lo, [[8 * W, 128], [W, 8], [1, 512]]))
        mk.dmaQ(xml, f"ds_{xmw}", ["s_mid"], [xmw], tag="xml")
        PA.alloc(f"gwb1_{wi}", 512)
        PA.alloc(f"gwb2_{wi}", 512)
        def b1f(e, lo=lo, wi=wi):
            return e.matmul(psf(f"gwb1_{wi}", 0, 512), ONESR,
                            arf("ln2sB", lo, lo + 512)[0:1, :], start=True, stop=True)
        mk.T(b1f, ["ln2_r1", "onesr"], [f"gwb1_{wi}"], tag="gwb")
        def b2f(e, lo=lo, wi=wi):
            return e.matmul(psf(f"gwb2_{wi}", 0, 512), ONESR,
                            arf("ln2sD", lo, lo + 512)[0:1, :], start=True, stop=True)
        mk.T(b2f, ["ln2_mr1", "onesr"], [f"gwb2_{wi}"], tag="gwb")
        for c in range(8):
            def a1(e, c=c, xmw=xmw, gw=gw, wi=wi):
                return e.tensor_mul(out=arf(gw, c * 513 + 1, c * 513 + 513),
                                    in0=arf(xmw, c * 512, c * 512 + 512),
                                    in1=psf(f"gwb1_{wi}", 0, 512))
            mk.V(a1, [xmw, f"gwb1_{wi}"], [gw], tag="gap1")
            def a2(e, c=c, gw=gw, wi=wi):
                return e.tensor_sub(out=arf(gw, c * 513 + 1, c * 513 + 513),
                                    in0=arf(gw, c * 513 + 1, c * 513 + 513),
                                    in1=psf(f"gwb2_{wi}", 0, 512))
            mk.V(a2, [f"gwb2_{wi}"], [gw], tag="gap2")
            def a3(e, c=c, gw=gw):
                return e.tensor_scalar(out=arf(gw, c * 513 + 1, c * 513 + 513),
                                       in0=arf(gw, c * 513 + 1, c * 513 + 513),
                                       scalar1=VEC[:, VG2 + c:VG2 + c + 1],
                                       scalar2=VEC[:, VB2 + c:VB2 + c + 1],
                                       op0=ALU.mult, op1=ALU.add)
            mk.P(a3, ["vec"], [gw], tag="gap3")
            if wi == 0:
                def a4(e, c=c, gw=gw):
                    return e.tensor_mul(out=arf(gw, c * 513, c * 513 + 1),
                                        in0=BTOK[:, 8 + c:9 + c], in1=SMASK[:, 0:1])
                mk.V(a4, ["btokr", "smask"], [gw], tag="gbnd")
            else:
                def a4(e, c=c, gw=gw):
                    return e.tensor_copy(out=arf(gw, c * 513, c * 513 + 1),
                                         in_=BTOK[:, 24 + c:25 + c])
                mk.V(a4, ["gsave"], [gw], tag="gbnd")
        PA.free(f"gwb1_{wi}")
        PA.free(f"gwb2_{wi}")
        for c in range(8):
            def gsv(e, c=c, gw=gw):
                return e.tensor_copy(out=BTOK[:, 24 + c:25 + c],
                                     in_=arf(gw, c * 513 + 512, c * 513 + 513))
            mk.V(gsv, [gw], ["gsave"], tag="gsv")
        gkw, grw, dgw = f"gkw{wi}", f"grw{wi}", f"dgw{wi}"
        A.alloc(gkw, 8 * wl // 2)
        A.alloc(grw, 8 * wl // 2)
        A.alloc(dgw, wl)
        for c in range(8):
            def d1(e, c=c, gw=gw, dgw=dgw):
                return e.tensor_sub(out=arf(dgw),
                                    in0=arf(gw, c * 513 + 1, c * 513 + 513),
                                    in1=arf(gw, c * 513, c * 513 + 512))
            mk.V(d1, [gw], [dgw], tag="dg")
            for nm, vcol in ((gkw, VFMK), (grw, VFMR)):
                def mx(e, c=c, nm=nm, vcol=vcol, gw=gw, dgw=dgw):
                    return e.scalar_tensor_tensor(
                        out=arb(nm, c * 512, c * 512 + 512), in0=arf(dgw),
                        scalar=VEC[:, vcol + c:vcol + c + 1],
                        in1=arf(gw, c * 513, c * 513 + 512),
                        op0=ALU.mult, op1=ALU.add)
                mk.V(mx, [dgw, gw, "vec"], [nm], tag="gmx")
        A.free(dgw)
        A.free(gw)
        srw = f"srw{wi}"
        A.alloc(srw, 8 * wl)
        PA.alloc(f"frp_{wi}_0", 512)
        PA.alloc(f"frp_{wi}_1", 512)
        for mo in range(8):
            pp = 512 * (mo % 2)
            for k in range(8):
                st, sp = (k == 0), (k == 7)
                def f(e, k=k, mo=mo, st=st, sp=sp, wi=wi, grw=grw, pp=pp):
                    return e.matmul(psf(f"frp_{wi}_{pp // 512}", 0, 512),
                                    arb("frw", k * CH + mo * 128, k * CH + mo * 128 + 128),
                                    arb(grw, k * 512, k * 512 + 512), start=st, stop=sp)
                mk.T(f, ["frw", grw], [f"frp_{wi}_{mo%2}"], tag="frmm")
            def sev(e, mo=mo, wi=wi, srw=srw, pp=pp):
                return e.activation(out=arf(srw, mo * 512, mo * 512 + 512),
                                    in_=psf(f"frp_{wi}_{pp // 512}", 0, 512), func=AF.Sigmoid)
            mk.S(sev, [f"frp_{wi}_{mo%2}"], [srw], tag="srev2")
        PA.free(f"frp_{wi}_0")
        PA.free(f"frp_{wi}_1")
        A.free(grw)
        kfb, kfw = f"kfb{wi}", f"kfw{wi}"
        A.alloc(kfb, 32 * 512 // 2)
        A.alloc(kfw, 512)
        PA.alloc(f"fkp_{wi}_0", 512)
        PA.alloc(f"fkp_{wi}_1", 512)
        if wi == 0:
            A.alloc("fkgA", 4096)
        for grp in range(4):
            fg = "fkgA"
            def ld(e, grp=grp, fg=fg):
                return e.dma_start(
                    out=ARB[:, 2 * A.off(fg):2 * A.off(fg) + 8192].rearrange(
                        "p (k m) -> p k m", k=8),
                    in_=bass.AP(fk_d.tensor, grp * 1024,
                                [[8 * 4096, 128], [4096, 8], [1, 1024]]))
            mk.dmaG(ld, "pb_" + fg, [], [fg], tag="fkld")
            for fi in range(8):
                fch = grp * 8 + fi
                pp = 512 * (fch % 2)
                for k in range(8):
                    st, sp = (k == 0), (k == 7)
                    def f(e, k=k, fi=fi, st=st, sp=sp, wi=wi, gkw=gkw, pp=pp, fg=fg):
                        return e.matmul(psf(f"fkp_{wi}_{pp // 512}", 0, 512),
                                        arb(fg, k * 1024 + fi * 128, k * 1024 + fi * 128 + 128),
                                        arb(gkw, k * 512, k * 512 + 512), start=st, stop=sp)
                    mk.T(f, [fg, gkw], [f"fkp_{wi}_{fch%2}"], tag="fkmm")
                def rev(e, wi=wi, kfw=kfw, fch=fch, pp=pp):
                    return e.activation(out=arf(kfw),
                                        in_=psf(f"fkp_{wi}_{pp // 512}", 0, 512), func=AF.Relu)
                mk.S(rev, [f"fkp_{wi}_{fch%2}"], [f"kfr{wi}"], tag="fkrelu")
                def sqv(e, fch=fch, kfb=kfb, kfw=kfw):
                    return e.tensor_mul(out=arb(kfb, fch * 512, fch * 512 + 512),
                                        in0=arf(kfw), in1=arf(kfw))
                mk.P(sqv, [f"kfr{wi}"], [kfb], tag="fksq")
        if wi == 3:
            A.free("fkgA")
        PA.free(f"fkp_{wi}_0")
        PA.free(f"fkp_{wi}_1")
        A.free(kfw, [f"kfr{wi}"])
        A.free(gkw)
        PA.alloc(f"kvp_{wi}_0", 512)
        PA.alloc(f"kvp_{wi}_1", 512)
        A.alloc(f"t5_{wi}", 512)
        for mo in range(8):
            pp = 512 * (mo % 2)
            for fch in range(32):
                st, sp = (fch == 0), (fch == 31)
                def f(e, fch=fch, mo=mo, st=st, sp=sp, wi=wi, kfb=kfb, pp=pp):
                    return e.matmul(psf(f"kvp_{wi}_{pp // 512}", 0, 512),
                                    arb("fvw", fch * CH + mo * 128, fch * CH + mo * 128 + 128),
                                    arb(kfb, fch * 512, fch * 512 + 512), start=st, stop=sp)
                mk.T(f, ["fvw", kfb], [f"kvp_{wi}_{mo%2}"], tag="fvmm")
            def cmb(e, mo=mo, wi=wi, srw=srw, pp=pp):
                return e.tensor_mul(out=arf(f"t5_{wi}"),
                                    in0=arf(srw, mo * 512, mo * 512 + 512),
                                    in1=psf(f"kvp_{wi}_{pp // 512}", 0, 512))
            mk.V(cmb, [srw, f"kvp_{wi}_{mo%2}"], [f"t5w_{wi}"], tag="cmb1")
            def cmb2(e, mo=mo, wi=wi, xmw=xmw):
                return e.tensor_add(out=arf(xmw, mo * 512, mo * 512 + 512),
                                    in0=arf(xmw, mo * 512, mo * 512 + 512),
                                    in1=arf(f"t5_{wi}"))
            mk.V(cmb2, [f"t5w_{wi}"], [xmw], tag="cmb2")
        PA.free(f"kvp_{wi}_0")
        PA.free(f"kvp_{wi}_1")
        A.free(f"t5_{wi}", [f"t5w_{wi}"])
        A.free(kfb)
        A.free(srw)
        stats_win(xmw, 0, 512, f"sw{wi}")
        x0w = f"x0w{wi}"
        A.alloc(x0w, 8 * wl)
        def x0l(e, lo=lo, x0w=x0w):
            return e.dma_start(
                out=AR[:, A.off(x0w):A.off(x0w) + 8 * 512].rearrange(
                    "p (c t) -> p c t", c=8),
                in_=bass.AP(xin.tensor, lo + 1, [[8 * WH, 128], [WH, 8], [1, 512]]))
        mk.dmaQ(x0l, f"ds_{x0w}", [], [x0w], tag="x0l")
        PA.alloc(f"swb1_{wi}", 512)
        PA.alloc(f"swb2_{wi}", 512)
        bc2(f"sw{wi}", f"swb1_{wi}", f"swb2_{wi}", 512)
        for c in range(8):
            def s1f(e, c=c, xmw=xmw, wi=wi):
                return e.tensor_mul(out=arf(xmw, c * 512, c * 512 + 512),
                                    in0=arf(xmw, c * 512, c * 512 + 512),
                                    in1=psf(f"swb1_{wi}", 0, 512))
            mk.V(s1f, [f"swb1_{wi}"], [xmw], tag="x1a")
            def s2f(e, c=c, xmw=xmw, wi=wi):
                return e.tensor_sub(out=arf(xmw, c * 512, c * 512 + 512),
                                    in0=arf(xmw, c * 512, c * 512 + 512),
                                    in1=psf(f"swb2_{wi}", 0, 512))
            mk.V(s2f, [f"swb2_{wi}"], [xmw], tag="x1b")
            def s3f(e, c=c, xmw=xmw, x0w=x0w):
                return e.tensor_add(out=arf(xmw, c * 512, c * 512 + 512),
                                    in0=arf(xmw, c * 512, c * 512 + 512),
                                    in1=arf(x0w, c * 512, c * 512 + 512))
            mk.V(s3f, [x0w], [xmw], tag="x1c")
        PA.free(f"swb1_{wi}")
        PA.free(f"swb2_{wi}")
        A.free(x0w)
        def x1s(e, lo=lo, xmw=xmw):
            return e.dma_start(
                out=bass.AP(s_x1.tensor, lo, [[8 * W, 128], [W, 8], [1, 512]]),
                in_=AR[:, A.off(xmw):A.off(xmw) + 8 * 512].rearrange(
                    "p (c t) -> p c t", c=8))
        mk.dmaQ(x1s, f"ds_x1s{wi}", [xmw], ["s_x1"], tag="x1s")
        stats_win(xmw, 0, 512, f"pw{wi}")
        PA.alloc(f"pwb1_{wi}", 512)
        PA.alloc(f"pwb2_{wi}", 512)
        bc2(f"pw{wi}", f"pwb1_{wi}", f"pwb2_{wi}", 512)
        for c in range(8):
            def p1f(e, c=c, xmw=xmw, wi=wi):
                return e.tensor_mul(out=arf(xmw, c * 512, c * 512 + 512),
                                    in0=arf(xmw, c * 512, c * 512 + 512),
                                    in1=psf(f"pwb1_{wi}", 0, 512))
            mk.V(p1f, [f"pwb1_{wi}"], [xmw], tag="pn1")
            def p2f(e, c=c, xmw=xmw, wi=wi):
                return e.tensor_sub(out=arf(xmw, c * 512, c * 512 + 512),
                                    in0=arf(xmw, c * 512, c * 512 + 512),
                                    in1=psf(f"pwb2_{wi}", 0, 512))
            mk.V(p2f, [f"pwb2_{wi}"], [xmw], tag="pn2")
        PA.free(f"pwb1_{wi}")
        PA.free(f"pwb2_{wi}")
        if wi == 3:
            for c in range(8):
                def xbs(e, c=c, xmw=xmw):
                    return e.tensor_copy(out=XBND[:, c:c + 1],
                                         in_=arf(xmw, c * 512 + 511, c * 512 + 512))
                mk.V(xbs, [xmw], ["xbnd0"], tag="xbs")
        def xis(e, lo=lo, xmw=xmw):
            return e.dma_start(
                out=bass.AP(s_xi.tensor, lo, [[8 * W, 128], [W, 8], [1, 512]]),
                in_=AR[:, A.off(xmw):A.off(xmw) + 8 * 512].rearrange(
                    "p (c t) -> p c t", c=8))
        mk.dmaQ(xis, f"ds_xis{wi}", [xmw], ["s_xi2"], tag="xis")
        A.free(f"sw{wi}B", [f"sw{wi}_ex2", f"sw{wi}_var", f"sw{wi}_r"])
        A.free(f"sw{wi}D", [f"sw{wi}_mean", f"sw{wi}_mr"])
        A.free(f"pw{wi}B", [f"pw{wi}_ex2", f"pw{wi}_var", f"pw{wi}_r"])
        A.free(f"pw{wi}D", [f"pw{wi}_mean", f"pw{wi}_mr"])
        A.free(xmw)
    A.free("fvw")
    A.free("frw")
    A.free("ln2sB", ["ln2_sq_r", "ln2_r1"])
    A.free("ln2sD", ["ln2_mr1"])
    if "xin2" in mk.debug:
        A.alloc("xin2dbg", 8 * W)
        mk.dmaQ(lambda e: e.dma_start(out=arf("xin2dbg"), in_=s_xi),
                "ds_xin2dbg", ["s_xi2"], ["xin2dbg"], tag="xd")
        mk.dbg("xin2", lambda: arf("xin2dbg"), [128, 8 * W], ["xin2dbg"])
        A.free("xin2dbg")
    if "x1" in mk.debug:
        A.alloc("x1dbg", 8 * W)
        mk.dmaQ(lambda e: e.dma_start(out=arf("x1dbg"), in_=s_x1),
                "ds_x1dbg", ["s_x1"], ["x1dbg"], tag="xd1")
        mk.dbg("x1", lambda: arf("x1dbg"), [128, 8 * W], ["x1dbg"])
        A.free("x1dbg")
    if seg < 5:
        return finish(nc, mk, A, PA)

    def ffn_layer(l, src_ap, src_rgn):
        L = f"L{l}"
        QSC = 4 * l
        def bt1(e, l=l):
            return e.dma_start(out=bass.AP(tok_i[l].tensor, 0, [[8, 128], [1, 8]]),
                               in_=XBND[:, 8 * l:8 * l + 8])
        mk.dmaG(bt1, f"ds_bti{L}", [f"xbnd{l}"], [f"toki{L}"], tag="btok")
        em.cc(lambda e, l=l: e.collective_compute(
            "AllGather", ALU.bypass, replica_groups=[[0, 1], [2, 3], [4, 5], [6, 7]],
            ins=[tok_i[l].opt()], outs=[tok_o[l].opt()]),
            f"cs_tok{L}", [f"toki{L}"], [f"toko{L}"], tag="btok")
        def bt2(e, l=l):
            return e.dma_start(out=BTOK[:, 8:16],
                               in_=bass.AP(tok_o[l].tensor, 0, [[8, 128], [1, 8]]))
        mk.dmaG(bt2, f"ds_bto{L}", [f"toko{L}"], [f"btr{L}"], tag="btok")
        hb = f"hb{L}"
        A.alloc(hb, 8192)
        for c in range(8):
            if c < 4:
                def hc(e, c=c, hb=hb):
                    return e.dma_start(out=arb(hb, c * W, (c + 1) * W),
                                       in_=bass.AP(src_ap.tensor, c * W, [[8 * W, 128], [1, W]]))
            else:
                def hc(e, c=c, hb=hb):
                    return e.dma_start(out=arb(hb, c * W + 1, (c + 1) * W),
                                       in_=bass.AP(src_ap.tensor, c * W, [[8 * W, 128], [1, W - 1]]))
            mk.dmaG(hc, "pb_hb" + L, [src_rgn], [hb], tag="hbld")
        for c in range(4, 8):
            def hc0(e, c=c, hb=hb):
                return e.tensor_tensor(out=arb(hb, c * W, c * W + 1),
                                       in0=BTOK[:, 8 + c:9 + c], in1=SMASK[:, 0:1],
                                       op=ALU.mult)
            mk.V(hc0, [f"btr{L}", "smask"], [hb], tag="hb0")
        if subseg == 11:
            return None
        for i, nm in enumerate(("tqA", "tqB", "tkA", "tkB")):
            A.alloc(nm + L, 2048)
            def tl(e, i=i, nm=nm):
                return e.dma_start(out=arf(nm + L),
                                   in_=bass.AP(tabs_d.tensor, i * 2048, [[8192, 128], [1, 2048]]))
            mk.dmaQ(tl, f"ds_{nm}{L}", [], [nm + L + "_c"], tag="tab")
            def tp(e, i=i, nm=nm):
                return e.tensor_scalar_mul(out=arf(nm + L), in0=arf(nm + L),
                                           scalar1=QS[:, QSC + i:QSC + i + 1])
            mk.V(tp, [nm + L + "_c", "qs"], [nm + L], tag="tabp")
        if subseg == 12:
            return None
        wq = f"wq{L}"
        A.alloc(wq, 10240)
        mk.dmaG(lambda e, l=l: e.dma_start(out=arb(wq), in_=wqkv_d[l]), "pb_wq" + L,
                [], [wq], tag="wql")
        qr, kr, vf = f"qr{L}", f"kr{L}", f"vf{L}"
        A.alloc(qr, 4096)
        A.alloc(kr, 5120)
        A.alloc(vf, 4096)
        A.alloc(f"rt1{L}", 512)
        A.alloc(f"rt2{L}", 512)
        A.alloc(f"sqs{L}", 256)
        A.alloc(f"qsr{L}", 2048)
        A.alloc(f"ksr{L}", 2048)
        PA.alloc(f"qp0{L}", 512)
        PA.alloc(f"qp1{L}", 512)
        PA.alloc(f"qp2{L}", 512)
        PA.alloc(f"qp3{L}", 512)
        PA.alloc(f"qst{L}", 512)

        def qkv_mm(mc, win, pn):
            lo = win * 512
            for k in range(8):
                st, sp = (k == 0), (k == 7)
                def f(e, k=k, mc=mc, lo=lo, st=st, sp=sp, pn=pn):
                    return e.matmul(psf(pn, 0, 512),
                                    arb(wq, k * 2560 + mc * 128, k * 2560 + mc * 128 + 128),
                                    arb(hb, k * W + lo, k * W + lo + 512),
                                    start=st, stop=sp)
                mk.T(f, [wq, hb], [pn], tag="qkvmm")

        if subseg == 13:
            return None
        for win in range(4):
            if subseg in (14, 15, 16, 17, 18, 19) and win > 0:
                break
            lo = win * 512
            for mq in range(4):
                qp0n = f"qp{2 * (mq % 2)}{L}"
                qp1n = f"qp{2 * (mq % 2) + 1}{L}"
                qkv_mm(mq, win, qp0n)
                if subseg == 15:
                    def pe_(e, mq=mq, win=win, qp0n=qp0n):
                        return e.copy(out=arf(f"rt1{L}"), in_=psf(qp0n, 0, 512))
                    mk.S(pe_, [qp0n], [f"rt1{L}"], tag="dbgev")
                    continue
                qkv_mm(12 + mq, win, qp1n)
                if subseg == 16:
                    def pe_(e, mq=mq, win=win, qp0n=qp0n):
                        return e.copy(out=arf(f"rt1{L}"), in_=psf(qp0n, 0, 512))
                    mk.S(pe_, [qp0n], [f"rt1{L}"], tag="dbgev")
                    def pe2_(e, mq=mq, win=win, qp1n=qp1n):
                        return e.copy(out=arf(f"rt2{L}"), in_=psf(qp1n, 0, 512))
                    mk.S(pe2_, [qp1n], [f"rt2{L}"], tag="dbgev2")
                    continue
                def sq_(e, win=win, mq=mq, qp0n=qp0n):
                    return e.activation(out=arb(f"sqs{L}", 0, 512), in_=psf(qp0n, 0, 512),
                                        func=AF.Square)
                if subseg != 18:
                    mk.S(sq_, [qp0n], [f"sqs{L}"], tag="sq")
                if subseg == 19:
                    continue
                if subseg not in (17, 18):
                    def hmm(e, mq=mq, win=win):
                        return e.matmul(psf(f"qst{L}", 0, 512)[0:16, :],
                                        HSEG[:, mq * 16:mq * 16 + 16], arb(f"sqs{L}", 0, 512),
                                        start=(mq == 0), stop=(mq == 3))
                    mk.T(hmm, [f"sqs{L}", "hseg"], [f"qst{L}"], tag="hmm")
                def r1(e, mq=mq, win=win, qp0n=qp0n):
                    return e.tensor_mul(out=arf(f"rt1{L}"), in0=psf(qp0n, 0, 512),
                                        in1=arf("tqA" + L, win * 512, win * 512 + 512))
                mk.V(r1, [qp0n, "tqA" + L, f"sqs{L}"], [f"rt1{L}"], tag="rot1")
                def r2(e, mq=mq, win=win, qp1n=qp1n):
                    return e.tensor_mul(out=arf(f"rt2{L}"), in0=psf(qp1n, 0, 512),
                                        in1=arf("tqB" + L, win * 512, win * 512 + 512))
                mk.V(r2, [qp1n, "tqB" + L], [f"rt2{L}"], tag="rot2")
                def r3(e, mq=mq, win=win):
                    return e.tensor_add(out=arb(qr, mq * W + win * 512, mq * W + win * 512 + 512),
                                        in0=arf(f"rt1{L}"), in1=arf(f"rt2{L}"))
                mk.V(r3, [f"rt1{L}", f"rt2{L}"], [qr], tag="rot3")
            if subseg not in (17, 18, 19):
                def qev(e, win=win):
                    return e.tensor_copy(out=AR[0:16, A.off(f"qsr{L}") + win * 512:A.off(f"qsr{L}") + win * 512 + 512],
                                         in_=psf(f"qst{L}", 0, 512)[0:16, :])
                mk.V(qev, [f"qst{L}"], [f"qsr{L}"], tag="qev")
            if subseg in (15, 16, 17, 18, 19):
                continue
            for mk_ in range(4):
                qkv_mm(4 + mk_, win, f"qp0{L}")
                qkv_mm(16 + mk_, win, f"qp1{L}")
                def sq_(e, mk_=mk_, win=win):
                    return e.activation(out=arb(f"sqs{L}", 0, 512), in_=psf(f"qp0{L}", 0, 512),
                                        func=AF.Square)
                mk.S(sq_, [f"qp0{L}"], [f"sqs{L}"], tag="sqk")
                def hmm(e, mk_=mk_, win=win):
                    return e.matmul(psf(f"qst{L}", 0, 512)[0:16, :],
                                    HSEG[:, mk_ * 16:mk_ * 16 + 16], arb(f"sqs{L}", 0, 512),
                                    start=(mk_ == 0), stop=(mk_ == 3))
                mk.T(hmm, [f"sqs{L}", "hseg"], [f"qst{L}"], tag="hmmk")
                def r1(e, mk_=mk_, win=win):
                    return e.tensor_mul(out=arf(f"rt1{L}"), in0=psf(f"qp0{L}", 0, 512),
                                        in1=arf("tkA" + L, win * 512, win * 512 + 512))
                mk.V(r1, [f"qp0{L}", "tkA" + L, f"sqs{L}"], [f"rt1{L}"], tag="krot1")
                def r2(e, mk_=mk_, win=win):
                    return e.tensor_mul(out=arf(f"rt2{L}"), in0=psf(f"qp1{L}", 0, 512),
                                        in1=arf("tkB" + L, win * 512, win * 512 + 512))
                mk.V(r2, [f"qp1{L}", "tkB" + L], [f"rt2{L}"], tag="krot2")
                def r3(e, mk_=mk_, win=win):
                    return e.tensor_add(
                        out=arb(kr, mk_ * 2560 + 512 + win * 512,
                                mk_ * 2560 + 512 + win * 512 + 512),
                        in0=arf(f"rt1{L}"), in1=arf(f"rt2{L}"))
                mk.V(r3, [f"rt1{L}", f"rt2{L}"], [kr], tag="krot3")
            def kev(e, win=win):
                return e.tensor_copy(out=AR[0:16, A.off(f"ksr{L}") + win * 512:A.off(f"ksr{L}") + win * 512 + 512],
                                     in_=psf(f"qst{L}", 0, 512)[0:16, :])
            mk.V(kev, [f"qst{L}"], [f"ksr{L}"], tag="kev")
            for mv in range(4):
                qkv_mm(8 + mv, win, f"qp0{L}")
                def vev(e, mv=mv, win=win):
                    return e.copy(out=arb(vf, mv * W + win * 512, mv * W + win * 512 + 512),
                                  in_=psf(f"qp0{L}", 0, 512))
                mk.S(vev, [f"qp0{L}"], [vf], tag="vev")
        if subseg < 2 or subseg in (15, 16, 17, 18, 19):
            return None
        PA.free(f"qp0{L}")
        PA.free(f"qp1{L}")
        PA.free(f"qp2{L}")
        PA.free(f"qp3{L}")
        PA.free(f"qst{L}")
        A.free(f"rt1{L}")
        A.free(f"rt2{L}")
        A.free(f"sqs{L}")
        A.free(wq)
        A.free(hb)
        for nm in ("tqA", "tqB", "tkA", "tkB"):
            A.free(nm + L, [nm + L + "_c"])

        for tg in (f"qsr{L}", f"ksr{L}"):
            def l1(e, tg=tg):
                return e.activation(out=AR[0:16, A.off(tg):A.off(tg) + 2048],
                                    in_=AR[0:16, A.off(tg):A.off(tg) + 2048],
                                    func=AF.Ln, bias=CONST[0:16, 2:3])
            mk.S(l1, [tg, "const"], [tg], tag="rsln")
            def l2(e, tg=tg):
                return e.activation(out=AR[0:16, A.off(tg):A.off(tg) + 2048],
                                    in_=AR[0:16, A.off(tg):A.off(tg) + 2048],
                                    func=AF.Exp, scale=-0.5)
            mk.S(l2, [tg], [tg], tag="rsex")
            def l3(e, tg=tg):
                return e.tensor_scalar_min(out=AR[0:16, A.off(tg):A.off(tg) + 2048],
                                           in0=AR[0:16, A.off(tg):A.off(tg) + 2048],
                                           scalar1=1e12)
            mk.V(l3, [tg], [tg], tag="rsmin")
        A.alloc(f"rsq{L}", 256)
        PA.alloc(f"rqt{L}", 512)
        for tc in range(16):
            def tr(e, tc=tc):
                return e.transpose(psf(f"rqt{L}", 0, 16)[:, :],
                                   AR[0:16, A.off(f"qsr{L}") + tc * 128:A.off(f"qsr{L}") + tc * 128 + 128],
                                   IDENTF[0:16, 0:16])
            mk.T(tr, [f"qsr{L}", "identf"], [f"rqt{L}"], tag="rstr")
            def ev(e, tc=tc):
                return e.tensor_scalar_mul(out=arf(f"rsq{L}", tc * 16, tc * 16 + 16),
                                           in0=psf(f"rqt{L}", 0, 16), scalar1=8.0)
            mk.V(ev, [f"rqt{L}"], [f"rsq{L}"], tag="rsev")
        PA.free(f"rqt{L}")
        A.free(f"qsr{L}")
        PA.alloc(f"kb{L}", 512)
        for mc in range(4):
            for win in range(4):
                def bm(e, mc=mc, win=win):
                    return e.matmul(psf(f"kb{L}", 0, 512),
                                    EPAIR[:, mc * 128:mc * 128 + 128],
                                    AR[0:16, A.off(f"ksr{L}") + win * 512:A.off(f"ksr{L}") + win * 512 + 512],
                                    start=True, stop=True)
                mk.T(bm, [f"ksr{L}", "epair"], [f"kb{L}"], tag="kbm")
                def km(e, mc=mc, win=win):
                    return e.tensor_mul(
                        out=arb(kr, mc * 2560 + 512 + win * 512, mc * 2560 + 512 + win * 512 + 512),
                        in0=arb(kr, mc * 2560 + 512 + win * 512, mc * 2560 + 512 + win * 512 + 512),
                        in1=psf(f"kb{L}", 0, 512))
                mk.V(km, [f"kb{L}"], [kr], tag="kbmul")
        PA.free(f"kb{L}")
        A.free(f"ksr{L}")
        if subseg < 3:
            return None
        vt = f"vt{L}"
        A.alloc(vt, 5120)
        PA.alloc(f"tpv{L}", 512)
        tpvb = None
        for vc in range(4):
            for tc in range(16):
                def tr(e, vc=vc, tc=tc):
                    pb = PS.bitcast(BF16)
                    off = PA.addr[f"tpv{L}"][0]
                    return e.transpose(pb[:, 2 * off:2 * off + 128],
                                       arb(vf, vc * W + tc * 128, vc * W + tc * 128 + 128),
                                       IDENT)
                mk.T(tr, [vf, "ident"], [f"tpv{L}"], tag="vtr")
                def ev(e, vc=vc, tc=tc):
                    pb = PS.bitcast(BF16)
                    off = PA.addr[f"tpv{L}"][0]
                    return e.tensor_copy(
                        out=arb(vt, (4 + tc) * 512 + vc * 128, (4 + tc) * 512 + vc * 128 + 128),
                        in_=pb[:, 2 * off:2 * off + 128])
                mk.V(ev, [f"tpv{L}"], [vt], tag="vtev")
        PA.free(f"tpv{L}")
        A.free(vf)
        if subseg < 4:
            return None
        def he1(e):
            return e.dma_start(
                out=bass.AP(hal_i[l].tensor, 0, [[2048, 128], [512, 4], [1, 512]]),
                in_=bass.AP(ARB.tensor, None, None) if False else
                ARB[:, 2 * A.off(kr):2 * A.off(kr) + 10240].rearrange(
                    "p (m t) -> p m t", m=4)[:, :, 2048:2560])
        mk.dmaG(he1, f"ds_he1{L}", [kr], [f"hali{L}"], tag="halo")
        def he2(e):
            return e.dma_start(
                out=bass.AP(hal_i[l].tensor, 262144, [[2048, 128], [1, 2048]]),
                in_=arb(vt, 16 * 512, 20 * 512))
        mk.dmaG(he2, f"ds_he2{L}", [vt], [f"hali{L}"], tag="halo")
        em.cc(lambda e, l=l: e.collective_compute(
            "AllGather", ALU.bypass, replica_groups=[[0, 1], [2, 3], [4, 5], [6, 7]],
            ins=[hal_i[l].opt()], outs=[hal_o[l].opt()]),
            f"cs_hal{L}", [f"hali{L}"], [f"halo{L}"], tag="haloc")
        A.alloc(f"hrt{L}", 2048)
        def hr1(e):
            return e.dma_start(
                out=arb(f"hrt{L}", 0, 2048).rearrange("p (m t) -> p m t", m=4),
                in_=bass.AP(hal_o[l].tensor, 0, [[2048, 128], [512, 4], [1, 512]]))
        mk.dmaG(hr1, f"ds_hr1{L}", [f"halo{L}"], [f"hrt1{L}"], tag="halor")
        def hr2(e):
            return e.dma_start(
                out=arb(f"hrt{L}", 2048, 4096),
                in_=bass.AP(hal_o[l].tensor, 262144, [[2048, 128], [1, 2048]]))
        mk.dmaG(hr2, f"ds_hr2{L}", [f"halo{L}"], [f"hrt2{L}"], tag="halor")
        for mc in range(4):
            def hm1(e, mc=mc):
                return e.tensor_scalar_mul(out=arb(kr, mc * 2560, mc * 2560 + 512),
                                           in0=arb(f"hrt{L}", mc * 512, mc * 512 + 512),
                                           scalar1=SMASK[:, 0:1])
            mk.P(hm1, [f"hrt1{L}", "smask"], [kr], tag="hmk")
            def hm2(e, mc=mc):
                return e.tensor_scalar_mul(out=arb(vt, mc * 512, mc * 512 + 512),
                                           in0=arb(f"hrt{L}", 2048 + mc * 512, 2048 + mc * 512 + 512),
                                           scalar1=SMASK[:, 0:1])
            mk.P(hm2, [f"hrt2{L}", "smask"], [vt], tag="hmv")
        A.free(f"hrt{L}", [f"hrt1{L}", f"hrt2{L}"])
        if subseg < 5:
            return None
        ol = f"ol{L}"
        A.alloc(ol, 4096)
        A.alloc(f"aex{L}", 1024)
        A.alloc(f"atx{L}", 1024)
        A.alloc(f"oscb{L}", 512)
        PA.alloc(f"sc0{L}", 1024)
        PA.alloc(f"sc1{L}", 1024)
        PA.alloc(f"tp0{L}", 512)
        PA.alloc(f"tp1{L}", 512)
        PA.alloc(f"op{L}", 512)
        PA.alloc(f"rr{L}", 512)
        PSB = PS.bitcast(BF16)
        qroff = 2 * A.off(qr)
        kroff = 2 * A.off(kr)
        vtoff = 2 * A.off(vt)
        oloff = 2 * A.off(ol)
        aexoff = 2 * A.off(f"aex{L}")
        atxoff = 2 * A.off(f"atx{L}")
        for qt in range(16):
            for h in range(8):
                pr = (qt * 8 + h) % 2
                scn, tpn = f"sc{pr}{L}", f"tp{pr}{L}"
                tpoff = 2 * PA.addr[tpn][0]
                mc, pb_ = h // 2, (h % 2) * 64
                def smm1(e, qt=qt, mc=mc, pb_=pb_, scn=scn):
                    return e.matmul(psf(scn, 0, 512),
                                    ARB[pb_:pb_ + 64, qroff + mc * 2048 + qt * 128:
                                        qroff + mc * 2048 + qt * 128 + 128],
                                    ARB[pb_:pb_ + 64, kroff + mc * 2560 + qt * 128:
                                        kroff + mc * 2560 + qt * 128 + 512],
                                    start=True, stop=True)
                mk.T(smm1, [qr, kr], [scn], tag="smm1")
                def smm2(e, qt=qt, mc=mc, pb_=pb_, scn=scn):
                    return e.matmul(psf(scn, 512, 640),
                                    ARB[pb_:pb_ + 64, qroff + mc * 2048 + qt * 128:
                                        qroff + mc * 2048 + qt * 128 + 128],
                                    ARB[pb_:pb_ + 64, kroff + mc * 2560 + qt * 128 + 512:
                                        kroff + mc * 2560 + qt * 128 + 640],
                                    start=True, stop=True)
                mk.T(smm2, [qr, kr], [scn], tag="smm2")
                def tm1(e, scn=scn):
                    off = PA.addr[scn][0]
                    pv = bass.AP(PS.tensor, off, [[4096, 128], [512, 2], [1, 128]])
                    tv = bass.AP(TRI.tensor, 0, [[256, 128], [128, 2], [1, 128]])
                    return e.tensor_tensor(out=pv, in0=pv, in1=tv, op=ALU.add)
                mk.V(tm1, ["tri", scn], [scn], tag="tri1")
                if qt < 4:
                    def tm3(e, qt=qt, scn=scn):
                        return e.tensor_add(out=psf(scn, 0, 512), in0=psf(scn, 0, 512),
                                            in1=EDGE[:, qt * 512:qt * 512 + 512])
                    mk.V(tm3, ["edge", scn], [scn], tag="tri3")
                def nmx(e, scn=scn, h=h):
                    return e.reduce_max(out=SUMS[:, 16 + h:17 + h], in_=psf(scn, 0, 640),
                                        axis=AX.X, negate=True)
                mk.V(nmx, [scn], [f"s_nmx{h}"], tag="anmx")
                def bia(e, qt=qt, h=h):
                    return e.tensor_mul(out=SUMS[:, 24 + h:25 + h], in0=SUMS[:, 16 + h:17 + h],
                                        in1=arf(f"rsq{L}", qt * 16 + h, qt * 16 + h + 1))
                mk.V(bia, [f"s_nmx{h}", f"rsq{L}"], [f"s_bia{h}"], tag="abia")
                def ex_(e, qt=qt, h=h, pr=pr, scn=scn):
                    return e.activation(out=ARB[:, aexoff + pr * 640:aexoff + pr * 640 + 640],
                                        in_=psf(scn, 0, 640), func=AF.Exp,
                                        scale=arf(f"rsq{L}", qt * 16 + h, qt * 16 + h + 1),
                                        bias=SUMS[:, 24 + h:25 + h], accum_out=SUMS[:, h:h + 1])
                mk.S(ex_, [scn, f"s_bia{h}", f"rsq{L}"], [f"aex{pr}{L}", f"s_sum{h}"], tag="aexp")
                for kc in range(5):
                    def tr(e, kc=kc, pr=pr, tpoff=tpoff):
                        return e.transpose(PSB[:, tpoff + kc * 128:tpoff + kc * 128 + 128],
                                           ARB[:, aexoff + pr * 640 + kc * 128:
                                               aexoff + pr * 640 + kc * 128 + 128], IDENT)
                    mk.T(tr, [f"aex{pr}{L}", "ident"], [tpn], tag="atr")
                def aev(e, pr=pr, tpoff=tpoff):
                    return e.copy(out=ARB[:, atxoff + pr * 640:atxoff + pr * 640 + 640],
                                  in_=PSB[:, tpoff:tpoff + 640])
                mk.S(aev, [tpn], [f"atx{pr}{L}"], tag="aev")
                for kc in range(5):
                    st, sp = (kc == 0), (kc == 4)
                    def vm(e, kc=kc, qt=qt, h=h, pb_=pb_, st=st, sp=sp, pr=pr):
                        return e.matmul(psf(f"op{L}", 0, 128)[pb_:pb_ + 64, :],
                                        ARB[:, vtoff + (qt + kc) * 512 + h * 64:
                                            vtoff + (qt + kc) * 512 + h * 64 + 64],
                                        ARB[:, atxoff + pr * 640 + kc * 128:
                                            atxoff + pr * 640 + kc * 128 + 128],
                                        start=st, stop=sp)
                    mk.T(vm, [vt, f"atx{pr}{L}"], [f"op{L}"], tag="avm")
                def oev(e, h=h, pb_=pb_):
                    return e.copy(out=AR[pb_:pb_ + 64,
                                         A.off(f"oscb{L}") + (h // 2) * 128:
                                         A.off(f"oscb{L}") + (h // 2) * 128 + 128],
                                  in_=psf(f"op{L}", 0, 128)[pb_:pb_ + 64, :])
                mk.S(oev, [f"op{L}"], [f"oscb{h}"], tag="aoev")
            def rc_(e):
                return e.reciprocal(out=SUMS[:, 0:8], in_=SUMS[:, 0:8])
            mk.V(rc_, [f"s_sum{h2}" for h2 in range(8)], ["s_rec"], tag="arec")
            def tr8(e):
                return e.transpose(psf(f"rr{L}", 0, 128)[0:8, :], SUMS[:, 0:8], IDENTF)
            mk.T(tr8, ["s_rec", "identf"], [f"rr{L}"], tag="atr8")
            def rev(e):
                return e.tensor_copy(out=RST[:, :], in_=psf(f"rr{L}", 0, 128)[0:8, :])
            mk.V(rev, [f"rr{L}"], ["rstt"], tag="arev")
            for h in range(8):
                mc, pb_ = h // 2, (h % 2) * 64
                def ehm(e, h=h, pb_=pb_):
                    return e.matmul(psf(f"rr{L}", 128, 256)[pb_:pb_ + 64, :],
                                    EHSET[:, h * 64:h * 64 + 64], RST[:, :],
                                    start=True, stop=True)
                mk.T(ehm, ["rstt", "ehset"], [f"rr{L}b"], tag="aehm")
                def omu(e, h=h, pb_=pb_, qt=qt, mc=mc):
                    return e.tensor_mul(out=ARB[pb_:pb_ + 64, oloff + mc * 2048 + qt * 128:
                                                oloff + mc * 2048 + qt * 128 + 128],
                                        in0=AR[pb_:pb_ + 64,
                                               A.off(f"oscb{L}") + mc * 128:
                                               A.off(f"oscb{L}") + mc * 128 + 128],
                                        in1=psf(f"rr{L}", 128, 256)[pb_:pb_ + 64, :])
                mk.V(omu, [f"oscb{h}", f"rr{L}b"], [ol], tag="aomu")
        PA.free(f"sc0{L}")
        PA.free(f"sc1{L}")
        PA.free(f"tp0{L}")
        PA.free(f"tp1{L}")
        PA.free(f"op{L}")
        PA.free(f"rr{L}")
        if subseg < 6:
            return None
        A.free(f"aex{L}", [f"aex0{L}", f"aex1{L}"])
        A.free(f"atx{L}", [f"atx0{L}", f"atx1{L}"])
        A.free(f"oscb{L}", [f"oscb{h}" for h in range(8)])
        A.free(f"rsq{L}")
        A.free(qr)
        A.free(kr)
        A.free(vt)
        xs = f"xs{L}"
        A.alloc(xs, 8 * W)
        mk.dmaQ(lambda e: e.dma_start(out=arf(xs), in_=src_ap), f"ds_xs{L}",
                [src_rgn], [xs], tag="xsld")
        wow = f"wow{L}"
        A.alloc(wow, 2048)
        mk.dmaG(lambda e, l=l: e.dma_start(out=arb(wow), in_=woff_d[l]), "pb_wow" + L,
                [], [wow], tag="wowld")
        PA.alloc(f"wp0{L}", 512)
        PA.alloc(f"wp1{L}", 512)
        for mo in range(8):
            for win in range(4):
                pn = f"wp{win % 2}{L}"
                for k in range(4):
                    st, sp = (k == 0), (k == 3)
                    def f(e, k=k, mo=mo, win=win, st=st, sp=sp, pn=pn):
                        return e.matmul(psf(pn, 0, 512),
                                        arb(wow, k * CH + mo * 128, k * CH + mo * 128 + 128),
                                        arb(ol, k * W + win * 512, k * W + win * 512 + 512),
                                        start=st, stop=sp)
                    mk.T(f, [wow, ol], [pn], tag="womm2")
                def ev(e, mo=mo, win=win, pn=pn):
                    return e.tensor_add(out=arf(xs, mo * W + win * 512, mo * W + win * 512 + 512),
                                        in0=arf(xs, mo * W + win * 512, mo * W + win * 512 + 512),
                                        in1=psf(pn, 0, 512))
                mk.V(ev, [pn], [xs], tag="woev2")
        PA.free(f"wp0{L}")
        PA.free(f"wp1{L}")
        A.free(wow)
        A.free(ol)
        if subseg < 7:
            return None
        for c in range(8):
            def mb0(e, c=c):
                return e.tensor_copy(out=BTOK[:, 16 + c:17 + c],
                                     in_=arf(xs, c * W + 2047, c * W + 2048))
            mk.V(mb0, [xs], [f"mbx{L}"], tag="mtok")
        def mb1(e, l=l):
            return e.dma_start(out=bass.AP(tok_i[2 + l].tensor, 0, [[8, 128], [1, 8]]),
                               in_=BTOK[:, 16:24])
        mk.dmaG(mb1, f"ds_mbi{L}", [f"mbx{L}"], [f"mtoki{L}"], tag="mtok")
        em.cc(lambda e, l=l: e.collective_compute(
            "AllGather", ALU.bypass, replica_groups=[[0, 1], [2, 3], [4, 5], [6, 7]],
            ins=[tok_i[2 + l].opt()], outs=[tok_o[2 + l].opt()]),
            f"cs_mtok{L}", [f"mtoki{L}"], [f"mtoko{L}"], tag="mtok")
        def mb2(e, l=l):
            return e.dma_start(out=BTOK[:, 24:32],
                               in_=bass.AP(tok_o[2 + l].tensor, 0, [[8, 128], [1, 8]]))
        mk.dmaG(mb2, f"ds_mbo{L}", [f"mtoko{L}"], [f"mbr{L}"], tag="mtok")
        mcast = f"mcast{L}"
        A.alloc(mcast, 8192)
        for c in range(8):
            if c < 4:
                def cst(e, c=c):
                    return e.tensor_copy(out=arb(mcast, c * W, (c + 1) * W),
                                         in_=arf(xs, c * W, (c + 1) * W))
                mk.P(cst, [xs], [mcast], tag="mcst")
            else:
                def cst(e, c=c):
                    return e.tensor_copy(out=arb(mcast, c * W + 1, (c + 1) * W),
                                         in_=arf(xs, c * W, (c + 1) * W - 1))
                mk.P(cst, [xs], [mcast], tag="mcst")
                def cs0(e, c=c):
                    return e.tensor_tensor(out=arb(mcast, c * W, c * W + 1),
                                           in0=BTOK[:, 24 + c:25 + c], in1=SMASK[:, 0:1],
                                           op=ALU.mult)
                mk.V(cs0, [f"mbr{L}", "smask"], [mcast], tag="mcs0")
        ln_stats(None, W, WINS_W, fold2=False, tag=f"mst{L}", cast_name=mcast)
        hbm = f"hbm{L}"
        A.alloc(hbm, 8192)
        A.alloc(f"mt{L}", 512)
        vg = VLNG0 if l == 0 else VLNG1
        vb = VLNB0 if l == 0 else VLNB1
        PA.alloc(f"mb1{L}", 512)
        PA.alloc(f"mb2{L}", 512)
        for win in range(4):
            lo = win * 512
            def b1f(e, lo=lo):
                return e.matmul(psf(f"mb1{L}", 0, 512), ONESR,
                                arf(f"mst{L}sB", lo, lo + 512)[0:1, :], start=True, stop=True)
            mk.T(b1f, [f"mst{L}_r1", "onesr"], [f"mb1{L}"], tag="mb1")
            def b2f(e, lo=lo):
                return e.matmul(psf(f"mb2{L}", 0, 512), ONESR,
                                arf(f"mst{L}sD", lo, lo + 512)[0:1, :], start=True, stop=True)
            mk.T(b2f, [f"mst{L}_mr1", "onesr"], [f"mb2{L}"], tag="mb2")
            for c in range(8):
                sh = 1 if c >= 4 else 0
                llo = lo - sh
                if c >= 4 and win == 0:
                    rlo, rwid, wlo = 0, 511, 1
                else:
                    rlo, rwid, wlo = llo, 512, lo
                def m1f(e, c=c, rlo=rlo, rwid=rwid, wlo=wlo, win=win):
                    return e.tensor_mul(out=arf(f"mt{L}", 0, rwid),
                                        in0=arf(xs, c * W + rlo, c * W + rlo + rwid),
                                        in1=psf(f"mb1{L}", 512 - rwid, 512))
                mk.V(m1f, [xs, f"mb1{L}"], [f"mt{L}"], tag="m1f")
                def m2f(e, c=c, rwid=rwid):
                    return e.tensor_sub(out=arf(f"mt{L}", 0, rwid),
                                        in0=arf(f"mt{L}", 0, rwid),
                                        in1=psf(f"mb2{L}", 512 - rwid, 512))
                mk.V(m2f, [f"mb2{L}"], [f"mt{L}"], tag="m2f")
                def m3f(e, c=c, rwid=rwid, wlo=wlo, vg=vg, vb=vb):
                    return e.tensor_scalar(out=arb(hbm, c * W + wlo, c * W + wlo + rwid),
                                           in0=arf(f"mt{L}", 0, rwid),
                                           scalar1=VEC[:, vg + c:vg + c + 1],
                                           scalar2=VEC[:, vb + c:vb + c + 1],
                                           op0=ALU.mult, op1=ALU.add)
                mk.V(m3f, [f"mt{L}", "vec"], [hbm], tag="m3f")
                if c >= 4 and win == 0:
                    def c1f(e, c=c):
                        return e.tensor_tensor(out=SUMS[:, 10:11], in0=BTOK[:, 24 + c:25 + c],
                                               in1=SMASK[:, 0:1], op=ALU.mult)
                    mk.V(c1f, [f"mbr{L}", "smask"], ["s_c1"], tag="c1f")
                    def c2f(e, c=c):
                        return e.tensor_mul(out=SUMS[:, 10:11], in0=SUMS[:, 10:11],
                                            in1=psf(f"mb1{L}", 0, 1))
                    mk.V(c2f, ["s_c1", f"mb1{L}"], ["s_c1"], tag="c2f")
                    def c3f(e, c=c):
                        return e.tensor_sub(out=SUMS[:, 10:11], in0=SUMS[:, 10:11],
                                            in1=psf(f"mb2{L}", 0, 1))
                    mk.V(c3f, ["s_c1", f"mb2{L}"], ["s_c1"], tag="c3f")
                    def c4f(e, c=c, vg=vg, vb=vb):
                        return e.tensor_scalar(out=arb(hbm, c * W, c * W + 1),
                                               in0=SUMS[:, 10:11],
                                               scalar1=VEC[:, vg + c:vg + c + 1],
                                               scalar2=VEC[:, vb + c:vb + c + 1],
                                               op0=ALU.mult, op1=ALU.add)
                    mk.V(c4f, ["s_c1", "vec"], [hbm], tag="c4f")
        PA.free(f"mb1{L}")
        PA.free(f"mb2{L}")
        A.free(f"mt{L}")
        A.free(mcast)
        A.free(f"mst{L}sB", [f"mst{L}_sq_r", f"mst{L}_r1"])
        A.free(f"mst{L}sD", [f"mst{L}_mr1"])
        spill_ap = s_xi if l == 0 else s_mid
        spill_rgn = f"sxp{L}"
        out_ap = s_mid if l == 0 else s_xi
        out_rgn = f"sxo{L}"
        mk.dmaQ(lambda e: e.dma_start(out=spill_ap, in_=arf(xs)), f"ds_sxp{L}",
                [xs], [spill_rgn], tag="mlpspill")
        A.free(xs)
        w2w = f"w2w{L}"
        A.alloc(w2w, 11264)
        mk.dmaG(lambda e, l=l: e.dma_start(out=arb(w2w), in_=w2f_d[l]), "pb_w2w" + L,
                [], [w2w], tag="w2ld")
        bcol = VB2F0 if l == 0 else VB2F1
        for wb in range(2):
            glu = f"glu{L}_{wb}"
            A.alloc(glu, 11264)
            A.alloc(f"gel{L}_{wb}0", 512)
            A.alloc(f"gel{L}_{wb}1", 512)
            PA.alloc(f"ap{L}{wb}0", 512)
            PA.alloc(f"gp{L}{wb}0", 512)
            PA.alloc(f"ap{L}{wb}1", 512)
            PA.alloc(f"gp{L}{wb}1", 512)
            if wb == 0:
                A.alloc(f"w1bA{L}", 1024)
                A.alloc(f"w1bB{L}", 1024)
            for mp in range(22):
                apn = f"ap{L}{wb}{mp % 2}"
                gpn = f"gp{L}{wb}{mp % 2}"
                w1b = (f"w1bA{L}" if mp % 2 == 0 else f"w1bB{L}")
                def wld(e, mp=mp, l=l, w1b=w1b):
                    return e.dma_start(
                        out=arb(w1b, 0, 2048).rearrange("p (k m) -> p k m", k=8),
                        in_=bass.AP(w1f_d[l].tensor, mp * 256,
                                    [[8 * 5632, 128], [5632, 8], [1, 256]]))
                mk.dmaG(wld, "pb_" + w1b, [], [w1b], tag="w1ld")
                for sub in range(2):
                    lo = wb * 1024 + sub * 512
                    for k in range(8):
                        st, sp = (k == 0), (k == 7)
                        def fa(e, k=k, lo=lo, st=st, sp=sp, w1b=w1b, apn=apn):
                            return e.matmul(psf(apn, 0, 512),
                                            arb(w1b, k * 256, k * 256 + 128),
                                            arb(hbm, k * W + lo, k * W + lo + 512),
                                            start=st, stop=sp)
                        mk.T(fa, [w1b, hbm], [apn], tag="w1a")
                        def fg(e, k=k, lo=lo, st=st, sp=sp, w1b=w1b, gpn=gpn):
                            return e.matmul(psf(gpn, 0, 512),
                                            arb(w1b, k * 256 + 128, k * 256 + 256),
                                            arb(hbm, k * W + lo, k * W + lo + 512),
                                            start=st, stop=sp)
                        mk.T(fg, [w1b, hbm], [gpn], tag="w1g")
                    def gev(e, mp=mp, wb=wb, l=l, gpn=gpn):
                        return e.activation(out=arf(f"gel{L}_{wb}" + str(mp % 2)),
                                            in_=psf(gpn, 0, 512), func=AF.Gelu,
                                            bias=B1[:, l * 44 + 2 * mp + 1:l * 44 + 2 * mp + 2])
                    mk.S(gev, [gpn, "b1"], [f"gel{L}_{wb}" + str(mp % 2)], tag="gelu")
                    def glf(e, mp=mp, wb=wb, sub=sub, l=l, glu=glu, apn=apn):
                        return e.scalar_tensor_tensor(
                            out=arb(glu, mp * 1024 + sub * 512, mp * 1024 + sub * 512 + 512),
                            in0=psf(apn, 0, 512),
                            scalar=B1[:, l * 44 + 2 * mp:l * 44 + 2 * mp + 1],
                            in1=arf(f"gel{L}_{wb}" + str(mp % 2)), op0=ALU.add, op1=ALU.mult)
                    mk.V(glf, [apn, "b1", f"gel{L}_{wb}" + str(mp % 2)], [glu], tag="glu")
            for pnm in (f"ap{L}{wb}0", f"gp{L}{wb}0", f"ap{L}{wb}1", f"gp{L}{wb}1"):
                PA.free(pnm)
            A.free(f"gel{L}_{wb}0")
            A.free(f"gel{L}_{wb}1")
            for mo in range(8):
                PA.alloc(f"xp{L}{wb}{mo}", 512)
            for sub in range(2):
                lo = wb * 1024 + sub * 512
                xw = f"xw{L}{wb}{sub}"
                A.alloc(xw, 4096)
                def xwl(e, lo=lo, xw=xw, spill_ap=spill_ap):
                    return e.dma_start(
                        out=AR[:, A.off(xw):A.off(xw) + 4096].rearrange(
                            "p (c t) -> p c t", c=8),
                        in_=bass.AP(spill_ap.tensor, lo, [[8 * W, 128], [W, 8], [1, 512]]))
                mk.dmaQ(xwl, f"ds_{xw}", [spill_rgn], [xw], tag="xwl")
                for mo in range(8):
                    for kp in range(22):
                        st, sp = (kp == 0), (kp == 21)
                        def f2(e, kp=kp, mo=mo, st=st, sp=sp, glu=glu, sub=sub):
                            return e.matmul(psf(f"xp{L}{wb}{mo}", 0, 512),
                                            arb(w2w, kp * CH + mo * 128, kp * CH + mo * 128 + 128),
                                            arb(glu, kp * 1024 + sub * 512, kp * 1024 + sub * 512 + 512),
                                            start=st, stop=sp)
                        mk.T(f2, [w2w, glu], [f"xp{L}{wb}{mo}"], tag="w2mm")
                    def ev2(e, mo=mo, bcol=bcol, xw=xw):
                        return e.scalar_tensor_tensor(
                            out=arf(xw, mo * 512, mo * 512 + 512),
                            in0=psf(f"xp{L}{wb}{mo}", 0, 512),
                            scalar=VEC[:, bcol + mo:bcol + mo + 1],
                            in1=arf(xw, mo * 512, mo * 512 + 512),
                            op0=ALU.add, op1=ALU.add)
                    mk.V(ev2, [f"xp{L}{wb}{mo}", "vec", xw], [xw], tag="w2ev")
                if wb == 1 and sub == 1:
                    for c in range(8):
                        def xbs2(e, c=c, xw=xw, l=l):
                            return e.tensor_copy(out=XBND[:, 8 + c:9 + c],
                                                 in_=arf(xw, c * 512 + 511, c * 512 + 512))
                        if l == 0:
                            mk.V(xbs2, [xw], ["xbnd1"], tag="xbs2")
                def xws(e, lo=lo, xw=xw, out_ap=out_ap):
                    return e.dma_start(
                        out=bass.AP(out_ap.tensor, lo, [[8 * W, 128], [W, 8], [1, 512]]),
                        in_=AR[:, A.off(xw):A.off(xw) + 4096].rearrange(
                            "p (c t) -> p c t", c=8))
                mk.dmaQ(xws, f"ds_{xw}s", [xw], [out_rgn], tag="xws")
                A.free(xw)
            for mo in range(8):
                PA.free(f"xp{L}{wb}{mo}")
            A.free(glu)
            if wb == 1:
                A.free(f"w1bA{L}")
                A.free(f"w1bB{L}")
        A.free(w2w)
        A.free(hbm)
        return out_ap, out_rgn

    r0 = ffn_layer(0, s_xi, "s_xi2")
    if r0 is None:
        return finish(nc, mk, A, PA)
    o0_ap, o0_rgn = r0
    if "xl1o" in mk.debug:
        A.alloc("xl1odbg", 8 * W)
        mk.dmaQ(lambda e: e.dma_start(out=arf("xl1odbg"), in_=o0_ap),
                "ds_xl1odbg", [o0_rgn], ["xl1odbg"], tag="xl1od")
        mk.dbg("xl1o", lambda: arf("xl1odbg"), [128, 8 * W], ["xl1odbg"])
        A.free("xl1odbg")
    if seg < 6:
        return finish(nc, mk, A, PA)
    o1_ap, o1_rgn = ffn_layer(1, o0_ap, o0_rgn)
    xs1 = "xsF"
    A.alloc(xs1, 8 * W)
    mk.dmaQ(lambda e: e.dma_start(out=arf(xs1), in_=o1_ap), "ds_xsF",
            [o1_rgn], [xs1], tag="xsF")
    ln_stats(xs1, W, WINS_W, fold2=False, tag="fin")
    A.alloc("x1f", 8 * W)
    mk.dmaQ(lambda e: e.dma_start(out=arf("x1f"), in_=s_x1), "ds_x1f",
            ["s_x1"], ["x1f"], tag="x1f")
    PA.alloc("fb1", 512)
    PA.alloc("fb2", 512)
    for win in range(4):
        lo = win * 512
        def b1f(e, lo=lo):
            return e.matmul(psf("fb1", 0, 512), ONESR,
                            arf("finsB", lo, lo + 512)[0:1, :], start=True, stop=True)
        mk.T(b1f, ["fin_r1", "onesr"], ["fb1"], tag="fb1")
        def b2f(e, lo=lo):
            return e.matmul(psf("fb2", 0, 512), ONESR,
                            arf("finsD", lo, lo + 512)[0:1, :], start=True, stop=True)
        mk.T(b2f, ["fin_mr1", "onesr"], ["fb2"], tag="fb2")
        for c in range(8):
            def f1(e, c=c, lo=lo):
                return e.tensor_mul(out=arf(xs1, c * W + lo, c * W + lo + 512),
                                    in0=arf(xs1, c * W + lo, c * W + lo + 512),
                                    in1=psf("fb1", 0, 512))
            mk.V(f1, ["fb1"], [xs1], tag="f1")
            def f2(e, c=c, lo=lo):
                return e.tensor_sub(out=arf(xs1, c * W + lo, c * W + lo + 512),
                                    in0=arf(xs1, c * W + lo, c * W + lo + 512),
                                    in1=psf("fb2", 0, 512))
            mk.V(f2, ["fb2"], [xs1], tag="f2")
            def f3(e, c=c, lo=lo):
                return e.tensor_add(out=arf(xs1, c * W + lo, c * W + lo + 512),
                                    in0=arf(xs1, c * W + lo, c * W + lo + 512),
                                    in1=arf("x1f", c * W + lo, c * W + lo + 512))
            mk.V(f3, ["x1f"], [xs1], tag="f3")
    PA.free("fb1")
    PA.free("fb2")
    mk.dmaQ(lambda e: e.dma_start(out=out_d, in_=arf(xs1)), "ds_out",
            [xs1], ["out"], tag="outdma")

    return finish(nc, mk, A, PA)

def finish(nc, mk, A, PA):
    em = mk.em
    if "out" not in em.last_w:
        allr0 = list(em.last_w.keys())
        mk.dmaQ(lambda e: e.dma_start(out=mk.out_d, in_=mk.AR[:, 0:8 * W]),
                "ds_out", allr0, ["out"], tag="dummyout")
    allr = list(em.last_w.keys())
    mk.P(lambda e: e.engine_nop(), allr, [], tag="fin")
    from contextlib import ExitStack
    ctx = ExitStack()
    with nc.Block() as block:
        em.build(nc, block, ctx)
    ctx.close()
    return nc, mk



def img_vec(v):
    return np.ascontiguousarray(v.reshape(8, 128).T)

def img_lhsT(Wdc):
    WT = Wdc.T
    kin, mout = WT.shape
    return np.ascontiguousarray(WT.reshape(kin // 128, 128, mout).transpose(1, 0, 2).reshape(128, -1))

def stage(I):
    x = np.asarray(I["L0_attn_x"])
    f32 = np.float32
    common = {}
    common["wk"] = img_lhsT(I["L0_attn_Wk"])
    common["wv"] = img_lhsT(I["L0_attn_Wv"])
    common["wr"] = img_lhsT(I["L0_attn_Wr"])
    common["wo"] = img_lhsT(I["L0_attn_Wo"])
    common["fk"] = img_lhsT(I["L0_attn_Fk"])
    common["fv"] = img_lhsT(I["L0_attn_Fv"])
    common["fr"] = img_lhsT(I["L0_attn_Fr"])
    vecs = np.zeros((128, 136), f32)
    cols = [("L0_attn_tm_k", 0), ("L0_attn_tm_v", 8), ("L0_attn_tm_r", 16),
            ("L0_attn_ln1_g", 24), ("L0_attn_ln1_b", 32), ("L0_attn_ln2_g", 40),
            ("L0_attn_ln2_b", 48), ("L0_attn_fm_k", 56), ("L0_attn_fm_r", 64),
            ("L0_attn_time_decay", 72), ("L0_attn_time_first", 80)]
    for nm, c0 in cols:
        vecs[:, c0:c0 + 8] = img_vec(np.asarray(I[nm]))
    vecs[:, 88:96] = img_vec(np.asarray(I["L0_ffn_ln_g"][0]))
    vecs[:, 96:104] = img_vec(np.asarray(I["L0_ffn_ln_b"][0]))
    vecs[:, 104:112] = img_vec(np.asarray(I["L0_ffn_ln_g"][1]))
    vecs[:, 112:120] = img_vec(np.asarray(I["L0_ffn_ln_b"][1]))
    vecs[:, 120:128] = img_vec(np.asarray(I["L0_ffn_b2"][0]))
    vecs[:, 128:136] = img_vec(np.asarray(I["L0_ffn_b2"][1]))
    common["vecs"] = vecs
    common["iota1"] = np.broadcast_to(np.arange(1, TL + 1, dtype=f32), (128, TL)).copy()
    common["onesr"] = np.ones((1, 128), f32)
    common["onesc"] = np.ones((128, 1), f32)

    in_maps = []
    for c in range(8):
        b, h = c // 2, c % 2
        t0 = h * TL
        m = dict(common)
        xi = np.zeros((128, 8 * WH), f32)
        xs = x[b, t0:t0 + TL].T
        for ck in range(8):
            xi[:, ck * WH + 1:(ck + 1) * WH] = xs[ck * 128:(ck + 1) * 128]
            if h == 1:
                xi[:, ck * WH] = x[b, t0 - 1, ck * 128:(ck + 1) * 128]
        m["xin"] = xi
        m["smask"] = np.full((128, 1), float(h), f32)
        in_maps.append(m)
    return stage_ffn(I, in_maps)

def _img_from_WT(WT):
    kin, mout = WT.shape
    return np.ascontiguousarray(WT.reshape(kin // 128, 128, mout).transpose(1, 0, 2).reshape(128, -1))

def _swap64(a):
    return np.concatenate([a[32:64], a[0:32]])

def stage_ffn(I, in_maps):
    f32 = np.float32
    for l in range(2):
        wq = np.asarray(I["L0_ffn_wqkv"][l], f32)
        q, k, v = wq[:, :512], wq[:, 512:1024], wq[:, 1024:]
        sw = np.concatenate([np.arange(32, 64), np.arange(0, 32)])
        idx = (np.arange(512) // 64) * 64 + sw[np.arange(512) % 64]
        qrot = q[:, idx]
        krot = k[:, idx]
        ext = np.concatenate([q, k, v, qrot, krot], 1)
        w1 = np.asarray(I["L0_ffn_w1"][l], f32)
        a, g = w1[:, :2730], w1[:, 2730:]
        ap = np.zeros((1024, 2816), f32); ap[:, :2730] = a
        gp = np.zeros((1024, 2816), f32); gp[:, :2730] = g
        pairs = []
        for i in range(22):
            pairs.append(ap[:, i * 128:(i + 1) * 128])
            pairs.append(gp[:, i * 128:(i + 1) * 128])
        w1re = np.concatenate(pairs, 1)
        w2 = np.asarray(I["L0_ffn_w2"][l], f32)
        w2p = np.zeros((2816, 1024), f32); w2p[:2730] = w2
        for m in in_maps:
            m[f"wqkv{l}"] = _img_from_WT(ext)
            m[f"woff{l}"] = _img_from_WT(np.asarray(I["L0_ffn_wo"][l], f32))
            m[f"w1f{l}"] = _img_from_WT(w1re)
            m[f"w2f{l}"] = _img_from_WT(w2p)
    b1re = np.zeros((128, 88), f32)
    for l in range(2):
        b1 = np.asarray(I["L0_ffn_b1"][l], f32)
        ap = np.zeros(2816, f32); ap[:2730] = b1[:2730]
        gp = np.zeros(2816, f32); gp[:2730] = b1[2730:]
        for i in range(22):
            b1re[:, l * 44 + 2 * i] = ap[i * 128:(i + 1) * 128]
            b1re[:, l * 44 + 2 * i + 1] = gp[i * 128:(i + 1) * 128]
    qst = np.zeros((128, 8), f32)
    for l in range(2):
        qs = np.asarray(I["L0_ffn_q_scale"][l], f32)
        ks = np.asarray(I["L0_ffn_k_scale"][l], f32)
        d = np.arange(128) % 64
        qst[:, 4 * l + 0] = qs[d]
        qst[:, 4 * l + 1] = _swap64(qs)[d]
        qst[:, 4 * l + 2] = ks[d]
        qst[:, 4 * l + 3] = _swap64(ks)[d]
    tri = np.zeros((128, 256), f32)
    ii = np.arange(128)[:, None]; jj = np.arange(128)[None, :]
    tri[:, :128] = np.where(jj <= ii, -1e30, 0.0)
    tri[:, 128:] = np.where(jj > ii, -1e30, 0.0)
    ident = np.eye(128, dtype=f32)
    hseg = np.zeros((128, 128), f32)
    for m in range(8):
        hseg[:64, m * 16 + 2 * m] = 1.0
        hseg[64:, m * 16 + 2 * m + 1] = 1.0
    epair = np.zeros((16, 512), f32)
    for m in range(4):
        epair[2 * m, m * 128:m * 128 + 64] = 1.0
        epair[2 * m + 1, m * 128 + 64:m * 128 + 128] = 1.0
    ehset = np.zeros((8, 512), f32)
    for hh in range(8):
        ehset[hh, hh * 64:(hh + 1) * 64] = 1.0
    DH, WIN = 64, 512
    half = np.arange(0, DH, 2, dtype=np.float64)
    inv_freq = 1.0 / 10000.0 ** (half / DH)
    base = (half + 0.4 * DH) / (1.4 * DH)
    sign = np.where(np.arange(64) < 32, -1.0, 1.0)
    for c, m in enumerate(in_maps):
        h = c % 2
        t = np.arange(h * 2048, h * 2048 + 2048, dtype=np.float64)
        fr = t[:, None] * inv_freq[None, :]
        fr = np.concatenate([fr, fr], 1)
        power = (t - 2048.0) / (WIN // 2)
        sc = base[None, :] ** power[:, None]
        sc = np.concatenate([sc, sc], 1)
        cos, sin = np.cos(fr), np.sin(fr)
        Aq = (cos * sc).T
        Bq = (sign[:, None] * (sin * sc).T)
        Ak = (cos / sc).T
        Bk = (sign[:, None] * (sin / sc).T)
        tabs = np.zeros((128, 4 * 2048), f32)
        for i, Tb in enumerate((Aq, Bq, Ak, Bk)):
            tabs[:64, i * 2048:(i + 1) * 2048] = Tb
            tabs[64:, i * 2048:(i + 1) * 2048] = Tb
        m["tabs"] = tabs
        edge = np.zeros((128, 2048), f32)
        if h == 0:
            for qt in range(4):
                edge[:, qt * 512:qt * 512 + (512 - qt * 128)] = -1e30
        m["edge"] = edge
        m["qst"] = qst
        m["tri"] = tri
        m["ident"] = ident
        m["hseg"] = hseg
        m["epair"] = epair
        m["ehset"] = ehset
        m["b1re"] = b1re
    return in_maps


_NC_CACHE = {}

def kernel(**inputs):
    in_maps = stage(inputs)
    if "nc" not in _NC_CACHE:
        _NC_CACHE["nc"] = build(debug=(), seg=99)
    nc, mkk = _NC_CACHE["nc"]
    res = run_bass_kernel_spmd(nc, in_maps, core_ids=list(range(8)))
    out = np.empty((4, 4096, 1024), np.float32)
    for c in range(8):
        b, h = c // 2, c % 2
        img = np.asarray(res.results[c]["out"], np.float32)
        fm = img.reshape(128, 8, 2048).transpose(1, 0, 2).reshape(1024, 2048)
        out[b, h * 2048:(h + 1) * 2048, :] = fm.T
    return out


# version 11
# speedup vs baseline: 1.0265x; 1.0114x over previous
import numpy as np
import concourse.bass as bass
import concourse.mybir as mybir
from concourse.bass_utils import run_bass_kernel_spmd

from contextlib import ExitStack

COMPUTE_STREAMS = ("T", "V", "S", "P")
QUEUE_STREAMS = ("Q",)
ALL_STREAMS = COMPUTE_STREAMS + QUEUE_STREAMS


class Op:
    __slots__ = ("stream", "fn", "reads", "writes", "dsem", "cc",
                 "idx", "need_inc", "waits", "tag")

    def __init__(self, stream, fn, reads, writes, dsem=None, cc=False, tag=""):
        self.stream = stream
        self.fn = fn
        self.reads = list(reads)
        self.writes = list(writes)
        self.dsem = dsem
        self.cc = cc
        self.idx = -1
        self.need_inc = False
        self.waits = []
        self.tag = tag


class Em:
    def __init__(self):
        self.ops = {s: [] for s in ALL_STREAMS}
        self.last_w = {}
        self.readers = {}
        self.dsem_count = {}

    def op(self, stream, fn, reads=(), writes=(), tag=""):
        o = Op(stream, fn, reads, writes, tag=tag)
        o.idx = len(self.ops[stream])
        self.ops[stream].append(o)
        tok = ("c", stream, o.idx)
        self._dep(o, tok)
        return tok

    def dma(self, qstream, fn, dsem, reads=(), writes=(), tag=""):
        if qstream == "Q":
            dsem = "pb_q%d" % len(self.ops["Q"])
        o = Op(qstream, fn, reads, writes, dsem=dsem, tag=tag)
        o.idx = len(self.ops[qstream])
        self.ops[qstream].append(o)
        self.dsem_count[dsem] = self.dsem_count.get(dsem, 0) + 1
        tok = ("d", dsem, self.dsem_count[dsem], qstream, o.idx)
        self._dep(o, tok)
        return tok

    def cc(self, fn, dsem, reads=(), writes=(), tag=""):
        o = Op("P", fn, reads, writes, dsem=dsem, cc=True, tag=tag)
        o.idx = len(self.ops["P"])
        self.ops["P"].append(o)
        self.dsem_count[dsem] = self.dsem_count.get(dsem, 0) + 1
        tok = ("x", dsem, self.dsem_count[dsem], "P", o.idx)
        self._dep(o, tok)
        return tok

    def inherit(self, new_region, old_regions):
        toks = []
        for r in old_regions:
            t = self.last_w.get(r)
            if t is not None:
                toks.append(t)
            toks.extend(self.readers.get(r, ()))
        if toks:
            self.readers.setdefault(new_region, []).extend(toks)

    def _dep(self, o, tok):
        for r in o.reads:
            t = self.last_w.get(r)
            if t is not None:
                o.waits.append(t)
            self.readers.setdefault(r, []).append(tok)
        for w in o.writes:
            t = self.last_w.get(w)
            if t is not None:
                o.waits.append(t)
            for rt in self.readers.get(w, ()):
                o.waits.append(rt)
            self.readers[w] = []
            self.last_w[w] = tok
        seen = set()
        uw = []
        for t in o.waits:
            if t[0] == "c" and t[1] == o.stream:
                if t[2] == o.idx:
                    continue
                if o.stream == "T":
                    continue
            if t in seen:
                continue
            seen.add(t)
            uw.append(t)
        o.waits = uw
        for t in o.waits:
            if t[0] == "c":
                self.ops[t[1]][t[2]].need_inc = True

    def build(self, nc, block, ctx):
        inc_no = {}
        for s in COMPUTE_STREAMS:
            n = 0
            for o in self.ops[s]:
                if o.need_inc:
                    n += 1
                    inc_no[(s, o.idx)] = n
        nxt = {}
        for s in COMPUTE_STREAMS:
            lst = self.ops[s]
            nf = None
            arr = [None] * len(lst)
            for i in range(len(lst) - 1, -1, -1):
                if lst[i].need_inc:
                    nf = i
                arr[i] = nf
            nxt[s] = arr

        qcnt = {}
        qtot = {}
        for s in ("Q", "P"):
            k = 0
            for o in self.ops[s]:
                if o.dsem is not None and not o.cc and not o.dsem.startswith("pb_"):
                    k += 1
                    qcnt[(s, o.idx)] = k
            qtot[s] = k

        def resolve(tok):
            if tok[0] == "c":
                _, s, i = tok
                j = nxt[s][i]
                assert j is not None, f"no inc for {tok}"
                return ("sem_" + s, inc_no[(s, j)])
            if tok[0] == "d":
                _, dsem, cnt, q, idx = tok
                if dsem.startswith("pb_"):
                    return (dsem, 16 * cnt)
                return ("dq_" + q, 16 * qcnt[(q, idx)])
            return (tok[1], tok[2])

        sems = {}
        for s in COMPUTE_STREAMS:
            sems["sem_" + s] = ctx.enter_context(nc.semaphore("sem_" + s))
        for s in ("Q", "P"):
            sems["dq_" + s] = ctx.enter_context(nc.semaphore("dq_" + s))
        for d in self.dsem_count:
            if d.startswith("cs_") or d.startswith("pb_"):
                sems[d] = ctx.enter_context(nc.semaphore(d))

        resolved = {s: [[resolve(t) for t in o.waits] for o in self.ops[s]]
                    for s in self.ops}
        knowA = {s: [] for s in self.ops}
        for s in self.ops:
            k = {}
            for o in self.ops[s]:
                for sem, v in resolved[s][o.idx]:
                    if k.get(sem, 0) < v:
                        k[sem] = v
                knowA[s].append(dict(k))

        def know_at(tok):
            if tok[0] == "c":
                _, s, i = tok
                return knowA[s][nxt[s][i]]
            _, dsem, cnt, s, i = tok
            return knowA[s][i]

        final_waits = {s: [None] * len(self.ops[s]) for s in self.ops}
        for s in self.ops:
            k = {}
            for o in self.ops[s]:
                ws = []
                for t, rv in zip(o.waits, resolved[s][o.idx]):
                    sem, v = rv
                    if k.get(sem, 0) >= v:
                        continue
                    ws.append((sem, v))
                    k[sem] = v
                    for ksem, kv in know_at(t).items():
                        if k.get(ksem, 0) < kv:
                            k[ksem] = kv
                final_waits[s][o.idx] = ws

        def section(stream):
            def go(eng):
                for o in self.ops[stream]:
                    for sem, v in final_waits[stream][o.idx]:
                        eng.wait_ge(sems[sem], v)
                    if o.dsem is not None and not o.cc and not o.dsem.startswith("pb_"):
                        k = qcnt[(stream, o.idx)]
                        if k > 1:
                            eng.wait_ge(sems["dq_" + stream], 16 * (k - 1))
                    inst = o.fn(eng)
                    if o.cc:
                        inst.then_inc(sems[o.dsem], 1)
                    elif o.dsem is not None and o.dsem.startswith("pb_"):
                        inst.then_inc(sems[o.dsem], 16)
                    elif o.dsem is not None:
                        inst.then_inc(sems["dq_" + stream], 16)
                    elif o.need_inc:
                        inst.then_inc(sems["sem_" + stream], 1)
            return go

        @block.sync
        def _(eng):
            section("Q")(eng)

        @block.vector
        def _(eng):
            section("V")(eng)

        @block.scalar
        def _(eng):
            section("S")(eng)

        @block.gpsimd
        def _(eng):
            section("P")(eng)

        @block.tensor
        def _(eng):
            section("T")(eng)
        return sems

    def stats(self):
        r = {}
        for s in self.ops:
            n = len(self.ops[s])
            ninc = sum(1 for o in self.ops[s] if o.need_inc or o.dsem)
            r[s] = (n, ninc)
        return r


class Arena:

    def __init__(self, em, total):
        self.em = em
        self.total = total
        self.live = {}
        self.addr = {}
        self.freed = []

    def alloc(self, name, size, align=16):
        req = size
        size = (size + align - 1) // align * align
        taken = sorted([(o, s) for (o, s) in self.live.values()])
        prev = 0
        best = None
        for (o, s) in taken + [(self.total, 0)]:
            if o - prev >= size:
                best = prev
                break
            prev = max(prev, o + s)
        assert best is not None, f"arena OOM for {name} size {size}: live={self.live}"
        off = best
        olds = []
        keep = []
        for (fo, fs, fn_) in self.freed:
            if fo < off + size and off < fo + fs:
                olds.append(fn_)
                if fo < off:
                    keep.append((fo, off - fo, fn_))
                if fo + fs > off + size:
                    keep.append((off + size, fo + fs - (off + size), fn_))
            else:
                keep.append((fo, fs, fn_))
        self.freed = keep
        if olds:
            self.em.inherit(name, olds)
        assert name not in self.addr, f"arena name reused: {name}"
        self.live[name] = (off, size)
        self.addr[name] = (off, req)
        return off

    def free(self, name, aliases=()):
        off, size = self.live.pop(name)
        self.freed.append((off, size, name))
        for a in aliases:
            self.freed.append((off, size, a))

    def off(self, name):
        return self.addr[name][0]


F32 = mybir.dt.float32
BF16 = mybir.dt.bfloat16
FP16 = mybir.dt.float16
AF = mybir.ActivationFunctionType
ALU = mybir.AluOpType
AX = mybir.AxisListType


CH = 1024
TL = 2048
W = TL
WH = TL + 1
EPS = 1e-5
ARW = 49664
WINS = [(0, 1), (1, 513), (513, 1025), (1025, 1537), (1537, 2049)]
WINS_W = [(0, 512), (512, 1024), (1024, 1536), (1536, 2048)]

VTMK, VTMV, VTMR, VG1, VB1, VG2, VB2, VFMK, VFMR, VTD, VTF = [8 * i for i in range(11)]
VLNG0, VLNB0, VLNG1, VLNB1, VB2F0, VB2F1 = [88 + 8 * i for i in range(6)]
NVEC = 136


SC_NK0, SC_EU, SC_LAM, SC_W, SC_TMP = 0, 8, 16, 24, 32


class MK:
    def __init__(self, nc, debug=()):
        self.nc = nc
        self.em = Em()
        self.debug = list(debug)
        self.dbg_shapes = {}
        self._names = set()

    def din(self, name, shape, dtype=F32):
        return self.nc.dram_tensor(name, list(shape), dtype, kind="ExternalInput").ap()

    def dout(self, name, shape, dtype=F32):
        return self.nc.dram_tensor(name, list(shape), dtype, kind="ExternalOutput").ap()

    def dint(self, name, shape, dtype=F32):
        return self.nc.dram_tensor(name, list(shape), dtype).ap()

    def T(self, fn, r, w, tag=""):
        return self.em.op("T", fn, r, w, tag=tag)

    def V(self, fn, r, w, tag=""):
        return self.em.op("V", fn, r, w, tag=tag)

    def S(self, fn, r, w, tag=""):
        return self.em.op("S", fn, r, w, tag=tag)

    def P(self, fn, r, w, tag=""):
        return self.em.op("P", fn, r, w, tag=tag)

    def dmaQ(self, fn, dsem, r, w, tag=""):
        return self.em.dma("Q", fn, dsem, r, w, tag=tag)

    def dmaG(self, fn, dsem, r, w, tag=""):
        return self.em.dma("P", fn, dsem, r, w, tag=tag)

    def dbg(self, name, ap_fn, shape, rgns, dtype=F32):
        if name not in self.debug:
            return
        d = self.dout("dbg_" + name, shape, dtype)
        self.dmaQ(lambda e: e.dma_start(out=d, in_=ap_fn()), "ds_dbg_" + name,
                  rgns, [], tag="dbg_" + name)
        self.dbg_shapes[name] = (shape, dtype)


def build(debug=(), seg=99, subseg=99):
    nc = bass.Bass("TRN2", target_bir_lowering=False)
    mk = MK(nc, debug=debug)
    em = mk.em

    xin = mk.din("xin", [128, 8 * WH])
    wk_d = mk.din("wk", [128, 8 * CH])
    wv_d = mk.din("wv", [128, 8 * CH])
    wr_d = mk.din("wr", [128, 8 * CH])
    wo_d = mk.din("wo", [128, 8 * CH])
    fk_d = mk.din("fk", [128, 8 * 4096])
    fv_d = mk.din("fv", [128, 32 * CH])
    fr_d = mk.din("fr", [128, 8 * CH])
    vecs_d = mk.din("vecs", [128, NVEC])
    tabs_d = mk.din("tabs", [128, 4 * 2048])
    qs_d = mk.din("qst", [128, 8])
    tri_d = mk.din("tri", [128, 256])
    edge_d = mk.din("edge", [128, 2048])
    ident_d = mk.din("ident", [128, 128])
    hseg_d = mk.din("hseg", [128, 128])
    epair_d = mk.din("epair", [16, 512])
    ehset_d = mk.din("ehset", [8, 512])
    b1_d = mk.din("b1re", [128, 88])
    wqkv_d = [mk.din(f"wqkv{l}", [128, 8 * 2560]) for l in range(2)]
    woff_d = [mk.din(f"woff{l}", [128, 4 * 1024]) for l in range(2)]
    w1f_d = [mk.din(f"w1f{l}", [128, 8 * 5632]) for l in range(2)]
    w2f_d = [mk.din(f"w2f{l}", [128, 22 * 1024]) for l in range(2)]
    iota_d = mk.din("iota1", [128, TL])
    onesr_d = mk.din("onesr", [1, 128])
    onesc_d = mk.din("onesc", [128, 1])
    smask_d = mk.din("smask", [128, 1])
    out_d = mk.dout("out", [128, 8 * W])
    mk.out_d = out_d

    agi = [mk.dint(f"agi{i}", [1, 384]) for i in range(2)]
    ago = [mk.dint(f"ago{i}", [2, 384]) for i in range(2)]
    aggi = mk.dint("aggi", [1, 1024])
    aggo = mk.dint("aggo", [2, 1024])
    tok_i = [mk.dint(f"toki{j}", [1, 1024]) for j in range(4)]
    tok_o = [mk.dint(f"toko{j}", [2, 1024]) for j in range(4)]
    hal_i = [mk.dint(f"hali{l}", [1, 128 * 4096], BF16) for l in range(2)]
    hal_o = [mk.dint(f"halo{l}", [2, 128 * 4096], BF16) for l in range(2)]

    s_xi = mk.dint("s_xi", [128, 8 * W])
    s_mid = mk.dint("s_mid", [128, 8 * W])
    s_x1 = mk.dint("s_x1", [128, 8 * W])

    AR_h = nc.alloc_sbuf_tensor("AR", [128, ARW], F32)
    AR = AR_h.ap()
    ARB = AR.bitcast(BF16)
    ARH = AR.bitcast(FP16)
    VEC_h = nc.alloc_sbuf_tensor("VEC", [128, NVEC], F32)
    VEC = VEC_h.ap()
    STATE_h = nc.alloc_sbuf_tensor("STATE", [128, 64], F32)
    STATE = STATE_h.ap()
    AGS_h = nc.alloc_sbuf_tensor("AGS", [128, 64], F32)
    AGS = AGS_h.ap()
    ONESR_h = nc.alloc_sbuf_tensor("ONESR", [1, 128], F32)
    ONESR = ONESR_h.ap()
    ONESB_h = nc.alloc_sbuf_tensor("ONESB", [128, 1], BF16)
    ONESB = ONESB_h.ap()
    SMASK_h = nc.alloc_sbuf_tensor("SMASK", [128, 1], F32)
    SMASK = SMASK_h.ap()
    CONST_h = nc.alloc_sbuf_tensor("CONST", [128, 8], F32)
    CONST = CONST_h.ap()
    TRI_h = nc.alloc_sbuf_tensor("TRI", [128, 256], BF16)
    TRI = TRI_h.ap()
    EDGE_h = nc.alloc_sbuf_tensor("EDGE", [128, 2048], BF16)
    EDGE = EDGE_h.ap()
    IDENT_h = nc.alloc_sbuf_tensor("IDENT", [128, 128], BF16)
    IDENT = IDENT_h.ap()
    IDENTF_h = nc.alloc_sbuf_tensor("IDENTF", [128, 128], F32)
    IDENTF = IDENTF_h.ap()
    HSEG_h = nc.alloc_sbuf_tensor("HSEG", [128, 128], BF16)
    HSEG = HSEG_h.ap()
    EPAIR_h = nc.alloc_sbuf_tensor("EPAIR", [16, 512], F32)
    EPAIR = EPAIR_h.ap()
    EHSET_h = nc.alloc_sbuf_tensor("EHSET", [8, 512], F32)
    EHSET = EHSET_h.ap()
    QS_h = nc.alloc_sbuf_tensor("QS", [128, 8], F32)
    QS = QS_h.ap()
    SUMS_h = nc.alloc_sbuf_tensor("SUMS", [128, 32], F32)
    SUMS = SUMS_h.ap()
    B1_h = nc.alloc_sbuf_tensor("B1", [128, 88], F32)
    B1 = B1_h.ap()
    RST_h = nc.alloc_sbuf_tensor("RST", [8, 128], F32)
    RST = RST_h.ap()
    BTOK_h = nc.alloc_sbuf_tensor("BTOK", [128, 32], F32)
    XBND_h = nc.alloc_sbuf_tensor("XBND", [128, 16], F32)
    XBND = XBND_h.ap()
    BTOK = BTOK_h.ap()

    PS_h = nc.alloc_psum_tensor("PS", [128, 4096], F32)
    PS = PS_h.ap()
    mk.AR = AR

    A = Arena(em, ARW)
    PA = Arena(em, 4096)

    def arf(name, lo=0, hi=None):
        off, size = A.addr[name]
        if hi is None:
            hi = size
        return AR[:, off + lo: off + hi]

    def arb(name, lo=0, hi=None):
        off, size = A.addr[name]
        if hi is None:
            hi = 2 * size
        return ARB[:, 2 * off + lo: 2 * off + hi]

    def arh(name, lo=0, hi=None):
        off, size = A.addr[name]
        if hi is None:
            hi = 2 * size
        return ARH[:, 2 * off + lo: 2 * off + hi]

    def psf(name, lo=0, hi=None):
        off, size = PA.addr[name]
        if hi is None:
            hi = size
        return PS[:, off + lo: off + hi]

    mk.dmaQ(lambda e: e.dma_start(out=VEC_h[:, :], in_=vecs_d), "ds_vec", [], ["vec"])
    
    mk.dmaQ(lambda e: e.dma_start(out=ONESR_h[:, :], in_=onesr_d), "ds_onesr", [], ["onesr"])
    mk.dmaQ(lambda e: e.dma_start(out=SMASK_h[:, :], in_=smask_d), "ds_smask", [], ["smask"])
    mk.dmaG(lambda e: e.dma_start(out=ONESB_h[:, :], in_=onesc_d), "dsg", [], ["onesb"])

    mk.dmaG(lambda e: e.dma_start(out=TRI_h[:, :], in_=tri_d), "ds_tri", [], ["tri"])
    mk.dmaG(lambda e: e.dma_start(out=EDGE_h[:, :], in_=edge_d), "ds_edge", [], ["edge"])
    mk.dmaG(lambda e: e.dma_start(out=IDENT_h[:, :], in_=ident_d), "ds_ident", [], ["ident"])
    mk.dmaQ(lambda e: e.dma_start(out=IDENTF_h[:, :], in_=ident_d), "ds_identf", [], ["identf"])
    mk.dmaG(lambda e: e.dma_start(out=HSEG_h[:, :], in_=hseg_d), "ds_hseg", [], ["hseg"])
    mk.dmaQ(lambda e: e.dma_start(out=EPAIR_h[:, :], in_=epair_d), "ds_epair", [], ["epair"])
    mk.dmaQ(lambda e: e.dma_start(out=EHSET_h[:, :], in_=ehset_d), "ds_ehset", [], ["ehset"])
    mk.dmaQ(lambda e: e.dma_start(out=QS_h[:, :], in_=qs_d), "ds_qs", [], ["qs"])
    mk.dmaQ(lambda e: e.dma_start(out=B1_h[:, :], in_=b1_d), "ds_b1", [], ["b1"])

    for ci, cv in ((0, EPS), (1, EPS + EPS * EPS), (2, 1e-30), (3, 1.0 + EPS), (4, 1.0)):
        def cf(e, ci=ci, cv=cv):
            return e.memset(CONST[:, ci:ci + 1], cv)
        mk.V(cf, [], ["const"], tag="const")

    mk.S(lambda e: e.activation(out=STATE[:, SC_W:SC_W + 8], in_=VEC[:, VTD:VTD + 8],
                                func=AF.Exp), ["vec"], ["st_w"])
    mk.V(lambda e: e.tensor_scalar_mul(out=STATE[:, SC_W:SC_W + 8],
                                       in0=STATE[:, SC_W:SC_W + 8], scalar1=-1.0),
         ["st_w"], ["st_w"])
    mk.S(lambda e: e.activation(out=STATE[:, SC_LAM:SC_LAM + 8],
                                in_=STATE[:, SC_W:SC_W + 8], func=AF.Exp),
         ["st_w"], ["st_lam"])
    mk.S(lambda e: e.activation(out=STATE[:, SC_EU:SC_EU + 8], in_=VEC[:, VTF:VTF + 8],
                                func=AF.Exp), ["vec"], ["st_eu"])

    def ln_stats(src_name, iw, wins, fold2=False, tag="ln", cast_name=None):
        n = wins[-1][1]
        sA, sB, sC, sD = (tag + "sA", tag + "sB", tag + "sC", tag + "sD")
        A.alloc(sA, n)
        A.alloc(sB, n)
        A.alloc(sC, n)
        A.alloc(sD, n)
        def row(rn, lo=0, hi=None):
            return arf({"skrA": sA, "skrB": sB, "skrC": sC, "skrD": sD}[rn], lo, hi)[0:1, :]
        if cast_name is None:
            A.alloc(tag + "_cast", (8 * iw + 1) // 2)
            rc = tag + "_cast"
            mk.P(lambda e: e.tensor_copy(out=arb(rc, 0, 8 * iw), in_=arf(src_name)),
                 [src_name], [rc], tag=tag + "cast")
        else:
            rc = cast_name
        A.alloc(tag + "_sq", (8 * iw + 1) // 2)
        rs = tag + "_sq"
        mk.S(lambda e: e.activation(out=arb(rs, 0, 8 * iw), in_=arb(rc, 0, 8 * iw),
                                    func=AF.Square), [rc], [rs], tag=tag + "sq")
        PA.alloc(tag + "_psum", 512)
        PA.alloc(tag + "_psq", 512)
        for (lo, hi) in wins:
            wl = hi - lo
            for c in range(8):
                st, sp = (c == 0), (c == 7)
                def f1(e, c=c, lo=lo, hi=hi, st=st, sp=sp, wl=wl):
                    return e.matmul(psf(tag + "_psum", 0, wl)[0:1, :], ONESB,
                                    arb(rc, c * iw + lo, c * iw + hi),
                                    start=st, stop=sp)
                mk.T(f1, [rc, "onesb"], [tag + "_psum"], tag=tag + "smm")
                def f2(e, c=c, lo=lo, hi=hi, st=st, sp=sp, wl=wl):
                    return e.matmul(psf(tag + "_psq", 0, wl)[0:1, :], ONESB,
                                    arb(rs, c * iw + lo, c * iw + hi),
                                    start=st, stop=sp)
                mk.T(f2, [rs, "onesb"], [tag + "_psq"], tag=tag + "qmm")
            def e1(e, lo=lo, wl=hi - lo):
                return e.tensor_copy(out=row("skrA", lo, lo + wl),
                                     in_=psf(tag + "_psum", 0, wl)[0:1, :])
            mk.V(e1, [tag + "_psum"], [tag + "_sum"], tag=tag + "sev")
            def e2(e, lo=lo, wl=hi - lo):
                return e.tensor_copy(out=row("skrB", lo, lo + wl),
                                     in_=psf(tag + "_psq", 0, wl)[0:1, :])
            mk.V(e2, [tag + "_psq"], [tag + "_sq_r"], tag=tag + "qev")
        PA.free(tag + "_psum")
        PA.free(tag + "_psq")
        if cast_name is None:
            A.free(rc)
        A.free(rs)
        mk.V(lambda e: e.tensor_scalar_mul(out=row("skrA"), in0=row("skrA"),
                                           scalar1=1.0 / CH), [tag + "_sum"], [tag + "_sum"])
        mk.V(lambda e: e.tensor_scalar_mul(out=row("skrB"), in0=row("skrB"),
                                           scalar1=1.0 / CH), [tag + "_sq_r"], [tag + "_sq_r"])
        mk.S(lambda e: e.activation(out=row("skrD"), in_=row("skrA"), func=AF.Square),
             [tag + "_sum"], [tag + "_mr1"])
        mk.V(lambda e: e.tensor_sub(out=row("skrB"), in0=row("skrB"), in1=row("skrD")),
             [tag + "_sq_r", tag + "_mr1"], [tag + "_sq_r"])
        if fold2:
            mk.S(lambda e: e.activation(out=row("skrC"), in_=row("skrB"), func=AF.Ln,
                                        scale=CONST[0:1, 3:4], bias=CONST[0:1, 1:2]),
                 [tag + "_sq_r", "const"], [tag + "_r2"])
            mk.S(lambda e: e.activation(out=row("skrC"), in_=row("skrC"), func=AF.Exp,
                                        scale=-0.5), [tag + "_r2"], [tag + "_r2"])
        mk.S(lambda e: e.activation(out=row("skrB"), in_=row("skrB"), func=AF.Ln,
                                    bias=CONST[0:1, 0:1]), [tag + "_sq_r", "const"], [tag + "_r1"])
        mk.S(lambda e: e.activation(out=row("skrB"), in_=row("skrB"), func=AF.Exp,
                                    scale=-0.5), [tag + "_r1"], [tag + "_r1"])
        mk.V(lambda e: e.tensor_mul(out=row("skrD"), in0=row("skrA"), in1=row("skrB")),
             [tag + "_sum", tag + "_r1"], [tag + "_mr1"])
        if fold2:
            mk.V(lambda e: e.tensor_mul(out=row("skrA"), in0=row("skrA"), in1=row("skrC")),
                 [tag + "_sum", tag + "_r2"], [tag + "_mr2"])
        else:
            A.free(sA, [tag + "_sum", tag + "_mr2"])
            A.free(sC, [tag + "_r2"])

    def bcast_row(tag, which, lo, wl, pname):
        rn = {"R1": tag + "sB", "R2": tag + "sC", "MR1": tag + "sD", "MR2": tag + "sA"}[which]
        rk = {"R1": tag + "_r1", "R2": tag + "_r2", "MR1": tag + "_mr1", "MR2": tag + "_mr2"}[which]
        def f(e):
            return e.matmul(psf(pname, 0, wl), ONESR,
                            arf(rn, lo, lo + wl)[0:1, :], start=True, stop=True)
        mk.T(f, [rk, "onesr"], [pname], tag="bcast")

    A.alloc("x0", 8 * WH)
    mk.dmaQ(lambda e: e.dma_start(out=arf("x0"), in_=xin), "ds_x0", [], ["x0"])
    ln_stats("x0", WH, WINS, fold2=True, tag="ln1")
    A.alloc("h", 8 * WH)
    A.alloc("xiw", 8 * 512)
    p1 = PA.alloc("bc1", 512)
    p2 = PA.alloc("bc2", 512)
    p3 = PA.alloc("bc3", 512)
    p4 = PA.alloc("bc4", 512)
    for wi, (lo, hi) in enumerate(WINS):
        wl = hi - lo
        bcast_row("ln1", "R1", lo, wl, "bc1")
        bcast_row("ln1", "MR1", lo, wl, "bc2")
        bcast_row("ln1", "R2", lo, wl, "bc3")
        bcast_row("ln1", "MR2", lo, wl, "bc4")
        for c in range(8):
            if lo >= 1:
                def a1(e, c=c, lo=lo, wl=wl):
                    return e.tensor_mul(out=arf("xiw", c * 512, c * 512 + wl),
                                        in0=arf("x0", c * WH + lo, c * WH + lo + wl),
                                        in1=psf("bc1", 0, wl))
                mk.V(a1, ["x0", "bc1"], ["xiw"], tag="xiap1")
                def a2(e, c=c, wl=wl):
                    return e.tensor_sub(out=arf("xiw", c * 512, c * 512 + wl),
                                        in0=arf("xiw", c * 512, c * 512 + wl),
                                        in1=psf("bc2", 0, wl))
                mk.V(a2, ["bc2"], ["xiw"], tag="xiap2")
            def a3(e, c=c, lo=lo, wl=wl):
                return e.tensor_mul(out=arf("h", c * WH + lo, c * WH + lo + wl),
                                    in0=arf("x0", c * WH + lo, c * WH + lo + wl),
                                    in1=psf("bc3", 0, wl))
            mk.V(a3, ["x0", "bc3"], ["h"], tag="hap1")
            def a4(e, c=c, lo=lo, wl=wl):
                return e.tensor_sub(out=arf("h", c * WH + lo, c * WH + lo + wl),
                                    in0=arf("h", c * WH + lo, c * WH + lo + wl),
                                    in1=psf("bc4", 0, wl))
            mk.V(a4, ["bc4"], ["h"], tag="hap2")
            def a5(e, c=c, lo=lo, wl=wl):
                return e.tensor_scalar(out=arf("h", c * WH + lo, c * WH + lo + wl),
                                       in0=arf("h", c * WH + lo, c * WH + lo + wl),
                                       scalar1=VEC[:, VG1 + c:VG1 + c + 1],
                                       scalar2=VEC[:, VB1 + c:VB1 + c + 1],
                                       op0=ALU.mult, op1=ALU.add)
            mk.P(a5, ["vec"], ["h"], tag="hap3")
        if lo >= 1:
            def sp(e, lo=lo, wl=wl):
                pat = [[8 * W, 128]] + [[W, 8], [1, wl]]
                return e.dma_start(
                    out=bass.AP(s_xi.tensor, lo - 1, pat),
                    in_=AR[:, A.off("xiw"):A.off("xiw") + 8 * 512].rearrange(
                        "p (c t) -> p c t", c=8)[:, :, 0:wl])
            mk.dmaQ(sp, "ds_sxi", ["xiw"], ["s_xi"], tag="spillxi")
    for nm in ("bc1", "bc2", "bc3", "bc4"):
        PA.free(nm)
    for c in range(8):
        def hz(e, c=c):
            return e.tensor_scalar_mul(out=arf("h", c * WH, c * WH + 1),
                                       in0=arf("h", c * WH, c * WH + 1),
                                       scalar1=SMASK[:, 0:1])
        mk.V(hz, ["smask"], ["h"], tag="hzero")
    A.free("x0")
    A.free("xiw")
    A.free("ln1sA", ["ln1_sum", "ln1_mr2"])
    A.free("ln1sB", ["ln1_sq_r", "ln1_r1"])
    A.free("ln1sC", ["ln1_r2"])
    A.free("ln1sD", ["ln1_mr1"])
    mk.dbg("h", lambda: arf("h"), [128, 8 * WH], ["h"])
    if seg < 2:
        return finish(nc, mk, A, PA)

    A.alloc("mxk", 8 * W // 2)
    A.alloc("mxv", 8 * W // 2)
    A.alloc("mxr", 8 * W // 2)
    A.alloc("dmix", W)
    for c in range(8):
        def d1(e, c=c):
            return e.tensor_sub(out=arf("dmix"),
                                in0=arf("h", c * WH + 1, c * WH + WH),
                                in1=arf("h", c * WH, c * WH + W))
        mk.V(d1, ["h"], ["dmix"], tag="dmix")
        for nm, vcol in (("mxr", VTMR), ("mxk", VTMK), ("mxv", VTMV)):
            def mx(e, c=c, nm=nm, vcol=vcol):
                return e.scalar_tensor_tensor(
                    out=arb(nm, c * W, c * W + W), in0=arf("dmix"),
                    scalar=VEC[:, vcol + c:vcol + c + 1],
                    in1=arf("h", c * WH, c * WH + W),
                    op0=ALU.mult, op1=ALU.add)
            mk.V(mx, ["dmix", "h", "vec"], [nm], tag=nm)
    A.free("dmix")
    A.free("h")
    mk.dbg("xk", lambda: arb("mxk"), [128, 8 * W], ["mxk"], BF16)

    A.alloc("wg0", 4096)
    A.alloc("wg1", 4096)
    mk.dmaG(lambda e: e.dma_start(out=arb("wg0"), in_=wr_d), "pb_wg0", [], ["wg0"])

    def mm_img(pname, plo, wname, m, rhs_name, rhs_iw, rlo, rhi, nk=8, tag="mm"):
        for k in range(nk):
            st, sp = (k == 0), (k == nk - 1)
            def f(e, k=k, st=st, sp=sp):
                return e.matmul(psf(pname, plo, plo + (rhi - rlo)),
                                arb(wname, k * CH + m * 128, k * CH + m * 128 + 128),
                                arb(rhs_name, k * rhs_iw + rlo, k * rhs_iw + rhi),
                                start=st, stop=sp)
            mk.T(f, [wname, rhs_name], [pname], tag=tag)

    A.alloc("sry", 8 * W // 2)
    PA.alloc("rp0", 512)
    PA.alloc("rp1", 512)
    for m in range(8):
        for wi, (lo, hi) in enumerate(WINS_W):
            pn = "rp%d" % (wi % 2)
            mm_img(pn, 0, "wg0", m, "mxr", W, lo, hi, tag="rmm")
            def sev(e, m=m, lo=lo, hi=hi, pn=pn):
                return e.activation(out=arh("sry", m * W + lo, m * W + hi),
                                    in_=psf(pn, 0, hi - lo), func=AF.Sigmoid)
            mk.S(sev, [pn], ["sry%d" % m], tag="srev")
    PA.free("rp0")
    PA.free("rp1")
    A.free("mxr")
    mk.dbg("sr", lambda: arh("sry"), [128, 8 * W], [f"sry{m}" for m in range(8)], FP16)

    mk.dmaG(lambda e: e.dma_start(out=arb("wg0"), in_=wk_d), "pb_wg0", [], ["wg0"])
    mk.dmaG(lambda e: e.dma_start(out=arb("wg1"), in_=wv_d), "pb_wg1", [], ["wg1"])

    for i in range(1):
        A.alloc(f"kv{i}", 2 * W)
        A.alloc(f"ab{i}", 2 * WH)
    A.alloc("nd0", 2 * W)
    A.alloc("nd1", 2 * W)
    PA.alloc("pwp", 2048)
    PA.alloc("kp0", 512)
    PA.alloc("kp1", 512)

    NSET = 1
    for m in range(8):
        i = m % NSET
        kv, ab, nd = f"kv{i}", f"ab{i}", f"nd{m % 2}"
        rk, rv = f"k{m}", f"v{m}"
        rab, rnd = f"ab{m}", f"nd{m}"
        if m >= NSET:
            em.inherit(rk, [f"k{m-NSET}"])
            em.inherit(rv, [f"v{m-NSET}"])
            em.inherit(rab, [f"ab{m-NSET}"])
        if m >= 2:
            em.inherit(rnd, [f"nd{m-2}"])
        for wi, (lo, hi) in enumerate(WINS_W):
            pn = "kp%d" % (wi % 2)
            mm_img(pn, 0, "wg0", m, "mxk", W, lo, hi, tag="kmm")
            def kev(e, lo=lo, hi=hi, pn=pn, kv=kv):
                return e.copy(out=arf(kv, lo, hi), in_=psf(pn, 0, hi - lo))
            mk.S(kev, [pn], [rk], tag="kev")
            mm_img(pn, 0, "wg1", m, "mxv", W, lo, hi, tag="vmm")
            def vev(e, lo=lo, hi=hi, pn=pn, kv=kv):
                return e.copy(out=arf(kv, W + lo, W + hi), in_=psf(pn, 0, hi - lo))
            mk.S(vev, [pn], [rv], tag="vev")
        def nmx(e, kv=kv, m=m):
            return e.reduce_max(out=STATE[:, SC_NK0 + m:SC_NK0 + m + 1],
                                in_=arf(kv, 0, W), axis=AX.X, negate=True)
        mk.V(nmx, [rk], [f"nk0_{m}"], tag="nk0")
        def ekf(e, kv=kv, m=m):
            return e.activation(out=arf(kv, 0, W), in_=arf(kv, 0, W), func=AF.Exp,
                                bias=STATE[:, SC_NK0 + m:SC_NK0 + m + 1])
        mk.S(ekf, [rk, f"nk0_{m}"], [rk], tag="ek")
        def ekvf(e, kv=kv):
            return e.tensor_mul(out=arf(kv, W, 2 * W), in0=arf(kv, W, 2 * W),
                                in1=arf(kv, 0, W))
        mk.V(ekvf, [rk, rv], [rv], tag="ekv")
        def z0(e, ab=ab):
            return e.memset(arf(ab, 0, 1), 0.0)
        mk.V(z0, [], [rab], tag="scz")
        def z1(e, ab=ab):
            return e.memset(arf(ab, WH, WH + 1), 0.0)
        mk.V(z1, [], [rab], tag="scz")
        lam_ap = bass.AP(STATE.tensor, SC_LAM + m, [[64, 128], [0, W]])
        def sca(e, ab=ab, kv=kv, lam_ap=lam_ap):
            return e.tensor_tensor_scan(out=arf(ab, 1, WH), data0=lam_ap,
                                        data1=arf(kv, W, 2 * W), initial=0.0,
                                        op0=ALU.mult, op1=ALU.add)
        mk.V(sca, [rv, "st_lam"], [rab], tag="scanA")
        def scb(e, ab=ab, kv=kv, lam_ap=lam_ap):
            return e.tensor_tensor_scan(out=arf(ab, WH + 1, 2 * WH), data0=lam_ap,
                                        data1=arf(kv, 0, W), initial=0.0,
                                        op0=ALU.mult, op1=ALU.add)
        mk.V(scb, [rk, "st_lam"], [rab], tag="scanB")
        ia = m % 2
        if m >= 2:
            em.inherit(f"agx{m}", [f"agx{m-2}"])
            em.inherit(f"agr2_{m}", [f"agr2_{m-2}"])
        def ex1(e, ab=ab, ia=ia):
            return e.tensor_copy(out=AGS[:, ia * 3:ia * 3 + 1], in_=arf(ab, W, WH))
        mk.V(ex1, [rab], [f"agx{m}"], tag="exA")
        def ex2(e, ab=ab, ia=ia):
            return e.tensor_copy(out=AGS[:, ia * 3 + 1:ia * 3 + 2], in_=arf(ab, WH + W, 2 * WH))
        mk.V(ex2, [rab], [f"agx{m}"], tag="exB")
        def ex3(e, ia=ia, m=m):
            return e.tensor_copy(out=AGS[:, ia * 3 + 2:ia * 3 + 3],
                                 in_=STATE[:, SC_NK0 + m:SC_NK0 + m + 1])
        mk.V(ex3, [f"nk0_{m}"], [f"agx{m}"], tag="exK")
        def nml(e, nd=nd, kv=kv, ab=ab, m=m):
            return e.scalar_tensor_tensor(out=arf(nd, 0, W), in0=arf(kv, W, 2 * W),
                                          scalar=STATE[:, SC_EU + m:SC_EU + m + 1],
                                          in1=arf(ab, 0, W), op0=ALU.mult, op1=ALU.add)
        mk.V(nml, [rv, rab, "st_eu"], [rnd], tag="num")
        def dnl(e, nd=nd, kv=kv, ab=ab, m=m):
            return e.scalar_tensor_tensor(out=arf(nd, W, 2 * W), in0=arf(kv, 0, W),
                                          scalar=STATE[:, SC_EU + m:SC_EU + m + 1],
                                          in1=arf(ab, WH, WH + W), op0=ALU.mult, op1=ALU.add)
        mk.V(dnl, [rk, rab, "st_eu"], [rnd], tag="den")
        ia = m % 2
        if m >= 2:
            em.inherit(f"agi_{m}", [f"agi_{m-2}"])
            em.inherit(f"ago_{m}", [f"ago_{m-2}"])
            em.inherit(f"agr_{m}", [f"agr_{m-2}"])
        def dout_f(e, ia=ia):
            return e.dma_start(out=bass.AP(agi[ia].tensor, 0, [[3, 128], [1, 3]]),
                               in_=AGS[:, ia * 3:ia * 3 + 3])
        mk.dmaG(dout_f, "pb_agi%d" % ia, [f"agx{m}"], [f"agi_{m}"], tag="agdma")
        def ccf(e, ia=ia):
            return e.collective_compute(
                "AllGather", ALU.bypass,
                replica_groups=[[0, 1], [2, 3], [4, 5], [6, 7]],
                ins=[agi[ia].opt()], outs=[ago[ia].opt()])
        em.cc(ccf, f"cs_ag{ia}", [f"agi_{m}"], [f"ago_{m}"], tag="agcc")
        def din_f(e, ia=ia):
            return e.dma_start(out=AGS[:, 32 + ia * 3:32 + ia * 3 + 3],
                               in_=bass.AP(ago[ia].tensor, 0, [[3, 128], [1, 3]]))
        mk.dmaG(din_f, "pb_ago%d" % ia, [f"ago_{m}"], [f"agr2_{m}"], tag="agback")
        def sp1(e, ia=ia, m=m):
            return e.tensor_sub(out=STATE[:, SC_TMP + 2:SC_TMP + 3],
                                in0=STATE[:, SC_NK0 + m:SC_NK0 + m + 1],
                                in1=AGS[:, 32 + ia * 3 + 2:32 + ia * 3 + 3])
        mk.V(sp1, [f"agr2_{m}", f"nk0_{m}"], ["st_f"], tag="stf1")
        mk.S(lambda e: e.activation(out=STATE[:, SC_TMP + 2:SC_TMP + 3],
                                    in_=STATE[:, SC_TMP + 2:SC_TMP + 3], func=AF.Exp),
             ["st_f"], ["st_f"], tag="stf2")
        mk.V(lambda e: e.tensor_mul(out=STATE[:, SC_TMP + 2:SC_TMP + 3],
                                    in0=STATE[:, SC_TMP + 2:SC_TMP + 3],
                                    in1=SMASK[:, 0:1]),
             ["st_f", "smask"], ["st_f"], tag="stf3")
        def spA(e, ia=ia):
            return e.tensor_mul(out=STATE[:, SC_TMP:SC_TMP + 1],
                                in0=AGS[:, 32 + ia * 3:32 + ia * 3 + 1],
                                in1=STATE[:, SC_TMP + 2:SC_TMP + 3])
        mk.V(spA, [f"agr2_{m}", "st_f"], ["st_ab"], tag="stA")
        def spB(e, ia=ia):
            return e.tensor_mul(out=STATE[:, SC_TMP + 1:SC_TMP + 2],
                                in0=AGS[:, 32 + ia * 3 + 1:32 + ia * 3 + 2],
                                in1=STATE[:, SC_TMP + 2:SC_TMP + 3])
        mk.V(spB, [f"agr2_{m}", "st_f"], ["st_ab"], tag="stB")
        lam_b2 = bass.AP(STATE.tensor, SC_LAM + m, [[64, 128], [0, W]])
        one_b = bass.AP(CONST.tensor, 4, [[8, 128], [0, W]])
        def pwf(e, m=m, lam_b2=lam_b2, one_b=one_b):
            return e.tensor_tensor_scan(out=psf("pwp", 0, 2048), data0=lam_b2,
                                        data1=one_b, initial=1.0,
                                        op0=ALU.mult, op1=ALU.mult)
        mk.V(pwf, ["st_lam", "const"], ["pw"], tag="pw")
        def c0(e, nd=nd):
            return e.tensor_add(out=arf(nd, 0, 1), in0=arf(nd, 0, 1),
                                in1=STATE[:, SC_TMP:SC_TMP + 1])
        mk.V(c0, ["st_ab"], [rnd], tag="corr")
        def c1(e, nd=nd):
            return e.scalar_tensor_tensor(out=arf(nd, 1, W), in0=psf("pwp", 0, W - 1),
                                          scalar=STATE[:, SC_TMP:SC_TMP + 1],
                                          in1=arf(nd, 1, W), op0=ALU.mult, op1=ALU.add)
        mk.V(c1, ["pw", "st_ab"], [rnd], tag="corr")
        def c2(e, nd=nd):
            return e.tensor_add(out=arf(nd, W, W + 1), in0=arf(nd, W, W + 1),
                                in1=STATE[:, SC_TMP + 1:SC_TMP + 2])
        mk.V(c2, ["st_ab"], [rnd], tag="corr")
        def c3(e, nd=nd):
            return e.scalar_tensor_tensor(out=arf(nd, W + 1, 2 * W), in0=psf("pwp", 0, W - 1),
                                          scalar=STATE[:, SC_TMP + 1:SC_TMP + 2],
                                          in1=arf(nd, W + 1, 2 * W), op0=ALU.mult, op1=ALU.add)
        mk.V(c3, ["pw", "st_ab"], [rnd], tag="corr")
        def rc_(e, nd=nd):
            return e.reciprocal(out=arf(nd, W, 2 * W), in_=arf(nd, W, 2 * W))
        mk.V(rc_, [rnd], [rnd], tag="recip")
        def ym(e, nd=nd):
            return e.tensor_mul(out=arf(nd, 0, W), in0=arf(nd, 0, W),
                                in1=arf(nd, W, 2 * W))
        mk.V(ym, [rnd], [rnd], tag="ymul")
        def ys(e, nd=nd, m=m):
            return e.tensor_mul(out=arb("sry", m * W, (m + 1) * W), in0=arf(nd, 0, W),
                                in1=arh("sry", m * W, (m + 1) * W))
        mk.V(ys, [rnd], [f"y{m}", f"sry{m}"], tag="ysr")
    for i in range(1):
        A.free(f"kv{i}", [f"k{m}" for m in range(8)] + [f"v{m}" for m in range(8)])
        A.free(f"ab{i}", [f"ab{m}" for m in range(8)])
    A.free("nd0", [f"nd{m}" for m in range(0, 8, 2)])
    A.free("nd1", [f"nd{m}" for m in range(1, 8, 2)])
    PA.free("pwp")
    A.free("mxk")
    A.free("mxv")
    PA.free("kp0")
    PA.free("kp1")
    mk.dbg("wkvy", lambda: arb("sry"), [128, 8 * W],
           [f"y{m}" for m in range(8)], BF16)
    if seg < 3:
        return finish(nc, mk, A, PA)

    mk.dmaG(lambda e: e.dma_start(out=arb("wg0"), in_=wo_d), "pb_wg0", [], ["wg0"])
    A.alloc("xi2", 8 * W)
    mk.dmaQ(lambda e: e.dma_start(out=arf("xi2"), in_=s_xi.tensor.ap()), "ds_sxi2",
            ["s_xi"], ["xi2"], tag="xilod")
    PA.alloc("wop0", 512)
    PA.alloc("wop1", 512)
    yrd = [f"y{m}" for m in range(8)]
    for mo in range(8):
        for wi, (lo, hi) in enumerate(WINS_W):
            pn = "wop%d" % (wi % 2)
            for k in range(8):
                st, sp = (k == 0), (k == 7)
                def f(e, k=k, mo=mo, lo=lo, hi=hi, st=st, sp=sp, pn=pn):
                    return e.matmul(psf(pn, 0, hi - lo),
                                    arb("wg0", k * CH + mo * 128, k * CH + mo * 128 + 128),
                                    arb("sry", k * W + lo, k * W + hi),
                                    start=st, stop=sp)
                mk.T(f, ["wg0"] + yrd, [pn], tag="womm")
            def ev(e, mo=mo, lo=lo, hi=hi, pn=pn):
                return e.tensor_add(out=arf("xi2", mo * W + lo, mo * W + hi),
                                    in0=arf("xi2", mo * W + lo, mo * W + hi),
                                    in1=psf(pn, 0, hi - lo))
            mk.V(ev, [pn, "xi2"], ["xi2"], tag="woev")
    PA.free("wop0")
    PA.free("wop1")
    A.free("sry", [f"sry{m}" for m in range(8)] + [f"y{m}" for m in range(8)])
    A.free("wg0")
    A.free("wg1")
    mk.dbg("xmid", lambda: arf("xi2"), [128, 8 * W], ["xi2"])
    mk.dmaQ(lambda e: e.dma_start(out=s_mid.tensor.ap(), in_=arf("xi2")), "ds_smid",
            ["xi2"], ["s_mid"], tag="midspill")
    ln_stats("xi2", W, WINS_W, fold2=False, tag="ln2")
    if seg < 4:
        return finish(nc, mk, A, PA)

    def stats_win(srcname, off, wl, tag):
        A.alloc(tag + "B", wl)
        A.alloc(tag + "D", wl)
        cb = tag + "_c"
        A.alloc(cb, (8 * wl + 1) // 2)
        def cst(e):
            return e.tensor_copy(out=arb(cb, 0, 8 * wl),
                                 in_=arf(srcname, off, off + 8 * wl))
        mk.P(cst, [srcname], [cb], tag=tag + "c")
        PA.alloc(tag + "_ps", 512)
        PA.alloc(tag + "_pq", 512)
        sqb = tag + "_s"
        A.alloc(sqb, (8 * wl + 1) // 2)
        def sqf(e):
            return e.activation(out=arb(sqb, 0, 8 * wl), in_=arb(cb, 0, 8 * wl),
                                func=AF.Square)
        mk.S(sqf, [cb], [sqb], tag=tag + "q")
        for c in range(8):
            st, sp = (c == 0), (c == 7)
            def f1(e, c=c, st=st, sp=sp):
                return e.matmul(psf(tag + "_ps", 0, wl)[0:1, :], ONESB,
                                arb(cb, c * wl, (c + 1) * wl), start=st, stop=sp)
            mk.T(f1, [cb, "onesb"], [tag + "_ps"], tag=tag + "m1")
            def f2(e, c=c, st=st, sp=sp):
                return e.matmul(psf(tag + "_pq", 0, wl)[0:1, :], ONESB,
                                arb(sqb, c * wl, (c + 1) * wl), start=st, stop=sp)
            mk.T(f2, [sqb, "onesb"], [tag + "_pq"], tag=tag + "m2")
        def rw(nm, lo=0, hi=None):
            return arf(nm, lo, hi)[0:1, :]
        def e1(e):
            return e.tensor_scalar_mul(out=rw(tag + "D"), in0=psf(tag + "_ps", 0, wl)[0:1, :],
                                       scalar1=1.0 / CH)
        mk.V(e1, [tag + "_ps"], [tag + "_mean"], tag=tag + "e1")
        def e2(e):
            return e.tensor_scalar_mul(out=rw(tag + "B"), in0=psf(tag + "_pq", 0, wl)[0:1, :],
                                       scalar1=1.0 / CH)
        mk.V(e2, [tag + "_pq"], [tag + "_ex2"], tag=tag + "e2")
        PA.free(tag + "_ps")
        PA.free(tag + "_pq")
        A.free(cb)
        A.free(sqb)
        A.alloc(tag + "T", wl)
        def e3(e):
            return e.activation(out=rw(tag + "T"), in_=rw(tag + "D"), func=AF.Square)
        mk.S(e3, [tag + "_mean"], [tag + "_msq"], tag=tag + "e3")
        def e4(e):
            return e.tensor_sub(out=rw(tag + "B"), in0=rw(tag + "B"), in1=rw(tag + "T"))
        mk.V(e4, [tag + "_ex2", tag + "_msq"], [tag + "_var"], tag=tag + "e4")
        def e5(e):
            return e.activation(out=rw(tag + "B"), in_=rw(tag + "B"), func=AF.Ln,
                                bias=CONST[0:1, 0:1])
        mk.S(e5, [tag + "_var", "const"], [tag + "_r"], tag=tag + "e5")
        def e6(e):
            return e.activation(out=rw(tag + "B"), in_=rw(tag + "B"), func=AF.Exp,
                                scale=-0.5)
        mk.S(e6, [tag + "_r"], [tag + "_r"], tag=tag + "e6")
        def e7(e):
            return e.tensor_mul(out=rw(tag + "D"), in0=rw(tag + "D"), in1=rw(tag + "B"))
        mk.V(e7, [tag + "_mean", tag + "_r"], [tag + "_mr"], tag=tag + "e7")
        A.free(tag + "T", [tag + "_msq"])
        return tag + "B", tag + "D", tag + "_r", tag + "_mr"

    def bc2(tag, pn1, pn2, wl, lo=0):
        def f1(e):
            return e.matmul(psf(pn1, 0, wl), ONESR, arf(tag + "B", lo, lo + wl)[0:1, :],
                            start=True, stop=True)
        mk.T(f1, [tag + "_r", "onesr"], [pn1], tag="bc2a")
        def f2(e):
            return e.matmul(psf(pn2, 0, wl), ONESR, arf(tag + "D", lo, lo + wl)[0:1, :],
                            start=True, stop=True)
        mk.T(f2, [tag + "_mr", "onesr"], [pn2], tag="bc2b")

    PA.alloc("gbt1", 512)
    PA.alloc("gbt2", 512)
    def gb1(e):
        return e.matmul(psf("gbt1", 0, 1), ONESR, arf("ln2sB", 2047, 2048)[0:1, :],
                        start=True, stop=True)
    mk.T(gb1, ["ln2_r1", "onesr"], ["gbt1"], tag="gbt")
    def gb2(e):
        return e.matmul(psf("gbt2", 0, 1), ONESR, arf("ln2sD", 2047, 2048)[0:1, :],
                        start=True, stop=True)
    mk.T(gb2, ["ln2_mr1", "onesr"], ["gbt2"], tag="gbt")
    for c in range(8):
        def g1(e, c=c):
            return e.tensor_mul(out=BTOK[:, c:c + 1],
                                in0=arf("xi2", c * W + 2047, c * W + 2048),
                                in1=psf("gbt1", 0, 1))
        mk.V(g1, ["xi2", "gbt1"], ["btok"], tag="gbt")
        def g2(e, c=c):
            return e.tensor_sub(out=BTOK[:, c:c + 1], in0=BTOK[:, c:c + 1],
                                in1=psf("gbt2", 0, 1))
        mk.V(g2, ["gbt2"], ["btok"], tag="gbt")
        def g3(e, c=c):
            return e.tensor_scalar(out=BTOK[:, c:c + 1], in0=BTOK[:, c:c + 1],
                                   scalar1=VEC[:, VG2 + c:VG2 + c + 1],
                                   scalar2=VEC[:, VB2 + c:VB2 + c + 1],
                                   op0=ALU.mult, op1=ALU.add)
        mk.V(g3, ["vec"], ["btok"], tag="gbt")
    PA.free("gbt1")
    PA.free("gbt2")
    mk.dmaG(lambda e: e.dma_start(out=bass.AP(aggi.tensor, 0, [[8, 128], [1, 8]]),
                                  in_=BTOK[:, 0:8]), "ds_aggi", ["btok"], ["aggi"], tag="gtokd")
    em.cc(lambda e: e.collective_compute("AllGather", ALU.bypass,
                                         replica_groups=[[0, 1], [2, 3], [4, 5], [6, 7]],
                                         ins=[aggi.opt()], outs=[aggo.opt()]),
          "cs_gtok", ["aggi"], ["aggo"], tag="gtokc")
    mk.dmaG(lambda e: e.dma_start(out=BTOK[:, 8:16],
                                  in_=bass.AP(aggo.tensor, 0, [[8, 128], [1, 8]])),
            "ds_aggo", ["aggo"], ["btokr"], tag="gtokr")
    A.free("xi2")

    A.alloc("fvw", 16384)
    A.alloc("frw", 4096)
    mk.dmaG(lambda e: e.dma_start(out=arb("fvw"), in_=fv_d), "pb_fvw", [], ["fvw"])
    mk.dmaG(lambda e: e.dma_start(out=arb("frw"), in_=fr_d), "pb_frw", [], ["frw"])

    for wi, (lo, hi) in enumerate(WINS_W):
        wl = 512
        xmw, gw = f"xmw{wi}", f"gw{wi}"
        A.alloc(xmw, 8 * wl)
        A.alloc(gw, 8 * 513)
        def xml(e, lo=lo, xmw=xmw):
            return e.dma_start(
                out=AR[:, A.off(xmw):A.off(xmw) + 8 * 512].rearrange(
                    "p (c t) -> p c t", c=8),
                in_=bass.AP(s_mid.tensor, lo, [[8 * W, 128], [W, 8], [1, 512]]))
        mk.dmaQ(xml, f"ds_{xmw}", ["s_mid"], [xmw], tag="xml")
        PA.alloc(f"gwb1_{wi}", 512)
        PA.alloc(f"gwb2_{wi}", 512)
        def b1f(e, lo=lo, wi=wi):
            return e.matmul(psf(f"gwb1_{wi}", 0, 512), ONESR,
                            arf("ln2sB", lo, lo + 512)[0:1, :], start=True, stop=True)
        mk.T(b1f, ["ln2_r1", "onesr"], [f"gwb1_{wi}"], tag="gwb")
        def b2f(e, lo=lo, wi=wi):
            return e.matmul(psf(f"gwb2_{wi}", 0, 512), ONESR,
                            arf("ln2sD", lo, lo + 512)[0:1, :], start=True, stop=True)
        mk.T(b2f, ["ln2_mr1", "onesr"], [f"gwb2_{wi}"], tag="gwb")
        for c in range(8):
            def a1(e, c=c, xmw=xmw, gw=gw, wi=wi):
                return e.tensor_mul(out=arf(gw, c * 513 + 1, c * 513 + 513),
                                    in0=arf(xmw, c * 512, c * 512 + 512),
                                    in1=psf(f"gwb1_{wi}", 0, 512))
            mk.V(a1, [xmw, f"gwb1_{wi}"], [gw], tag="gap1")
            def a2(e, c=c, gw=gw, wi=wi):
                return e.tensor_sub(out=arf(gw, c * 513 + 1, c * 513 + 513),
                                    in0=arf(gw, c * 513 + 1, c * 513 + 513),
                                    in1=psf(f"gwb2_{wi}", 0, 512))
            mk.V(a2, [f"gwb2_{wi}"], [gw], tag="gap2")
            def a3(e, c=c, gw=gw):
                return e.tensor_scalar(out=arf(gw, c * 513 + 1, c * 513 + 513),
                                       in0=arf(gw, c * 513 + 1, c * 513 + 513),
                                       scalar1=VEC[:, VG2 + c:VG2 + c + 1],
                                       scalar2=VEC[:, VB2 + c:VB2 + c + 1],
                                       op0=ALU.mult, op1=ALU.add)
            mk.P(a3, ["vec"], [gw], tag="gap3")
            if wi == 0:
                def a4(e, c=c, gw=gw):
                    return e.tensor_mul(out=arf(gw, c * 513, c * 513 + 1),
                                        in0=BTOK[:, 8 + c:9 + c], in1=SMASK[:, 0:1])
                mk.V(a4, ["btokr", "smask"], [gw], tag="gbnd")
            else:
                def a4(e, c=c, gw=gw):
                    return e.tensor_copy(out=arf(gw, c * 513, c * 513 + 1),
                                         in_=BTOK[:, 24 + c:25 + c])
                mk.V(a4, ["gsave"], [gw], tag="gbnd")
        PA.free(f"gwb1_{wi}")
        PA.free(f"gwb2_{wi}")
        for c in range(8):
            def gsv(e, c=c, gw=gw):
                return e.tensor_copy(out=BTOK[:, 24 + c:25 + c],
                                     in_=arf(gw, c * 513 + 512, c * 513 + 513))
            mk.V(gsv, [gw], ["gsave"], tag="gsv")
        gkw, grw, dgw = f"gkw{wi}", f"grw{wi}", f"dgw{wi}"
        A.alloc(gkw, 8 * wl // 2)
        A.alloc(grw, 8 * wl // 2)
        A.alloc(dgw, wl)
        for c in range(8):
            def d1(e, c=c, gw=gw, dgw=dgw):
                return e.tensor_sub(out=arf(dgw),
                                    in0=arf(gw, c * 513 + 1, c * 513 + 513),
                                    in1=arf(gw, c * 513, c * 513 + 512))
            mk.V(d1, [gw], [dgw], tag="dg")
            for nm, vcol in ((gkw, VFMK), (grw, VFMR)):
                def mx(e, c=c, nm=nm, vcol=vcol, gw=gw, dgw=dgw):
                    return e.scalar_tensor_tensor(
                        out=arb(nm, c * 512, c * 512 + 512), in0=arf(dgw),
                        scalar=VEC[:, vcol + c:vcol + c + 1],
                        in1=arf(gw, c * 513, c * 513 + 512),
                        op0=ALU.mult, op1=ALU.add)
                mk.V(mx, [dgw, gw, "vec"], [nm], tag="gmx")
        A.free(dgw)
        A.free(gw)
        srw = f"srw{wi}"
        A.alloc(srw, 8 * wl)
        PA.alloc(f"frp_{wi}_0", 512)
        PA.alloc(f"frp_{wi}_1", 512)
        for mo in range(8):
            pp = 512 * (mo % 2)
            for k in range(8):
                st, sp = (k == 0), (k == 7)
                def f(e, k=k, mo=mo, st=st, sp=sp, wi=wi, grw=grw, pp=pp):
                    return e.matmul(psf(f"frp_{wi}_{pp // 512}", 0, 512),
                                    arb("frw", k * CH + mo * 128, k * CH + mo * 128 + 128),
                                    arb(grw, k * 512, k * 512 + 512), start=st, stop=sp)
                mk.T(f, ["frw", grw], [f"frp_{wi}_{mo%2}"], tag="frmm")
            def sev(e, mo=mo, wi=wi, srw=srw, pp=pp):
                return e.activation(out=arf(srw, mo * 512, mo * 512 + 512),
                                    in_=psf(f"frp_{wi}_{pp // 512}", 0, 512), func=AF.Sigmoid)
            mk.S(sev, [f"frp_{wi}_{mo%2}"], [srw], tag="srev2")
        PA.free(f"frp_{wi}_0")
        PA.free(f"frp_{wi}_1")
        A.free(grw)
        kfb, kfw = f"kfb{wi}", f"kfw{wi}"
        A.alloc(kfb, 32 * 512 // 2)
        A.alloc(kfw, 512)
        PA.alloc(f"fkp_{wi}_0", 512)
        PA.alloc(f"fkp_{wi}_1", 512)
        if wi == 0:
            A.alloc("fkgA", 4096)
        for grp in range(4):
            fg = "fkgA"
            def ld(e, grp=grp, fg=fg):
                return e.dma_start(
                    out=ARB[:, 2 * A.off(fg):2 * A.off(fg) + 8192].rearrange(
                        "p (k m) -> p k m", k=8),
                    in_=bass.AP(fk_d.tensor, grp * 1024,
                                [[8 * 4096, 128], [4096, 8], [1, 1024]]))
            mk.dmaG(ld, "pb_" + fg, [], [fg], tag="fkld")
            for fi in range(8):
                fch = grp * 8 + fi
                pp = 512 * (fch % 2)
                for k in range(8):
                    st, sp = (k == 0), (k == 7)
                    def f(e, k=k, fi=fi, st=st, sp=sp, wi=wi, gkw=gkw, pp=pp, fg=fg):
                        return e.matmul(psf(f"fkp_{wi}_{pp // 512}", 0, 512),
                                        arb(fg, k * 1024 + fi * 128, k * 1024 + fi * 128 + 128),
                                        arb(gkw, k * 512, k * 512 + 512), start=st, stop=sp)
                    mk.T(f, [fg, gkw], [f"fkp_{wi}_{fch%2}"], tag="fkmm")
                def rev(e, wi=wi, kfw=kfw, fch=fch, pp=pp):
                    return e.activation(out=arf(kfw),
                                        in_=psf(f"fkp_{wi}_{pp // 512}", 0, 512), func=AF.Relu)
                mk.S(rev, [f"fkp_{wi}_{fch%2}"], [f"kfr{wi}"], tag="fkrelu")
                def sqv(e, fch=fch, kfb=kfb, kfw=kfw):
                    return e.tensor_mul(out=arb(kfb, fch * 512, fch * 512 + 512),
                                        in0=arf(kfw), in1=arf(kfw))
                mk.P(sqv, [f"kfr{wi}"], [kfb], tag="fksq")
        if wi == 3:
            A.free("fkgA")
        PA.free(f"fkp_{wi}_0")
        PA.free(f"fkp_{wi}_1")
        A.free(kfw, [f"kfr{wi}"])
        A.free(gkw)
        PA.alloc(f"kvp_{wi}_0", 512)
        PA.alloc(f"kvp_{wi}_1", 512)
        A.alloc(f"t5_{wi}", 512)
        for mo in range(8):
            pp = 512 * (mo % 2)
            for fch in range(32):
                st, sp = (fch == 0), (fch == 31)
                def f(e, fch=fch, mo=mo, st=st, sp=sp, wi=wi, kfb=kfb, pp=pp):
                    return e.matmul(psf(f"kvp_{wi}_{pp // 512}", 0, 512),
                                    arb("fvw", fch * CH + mo * 128, fch * CH + mo * 128 + 128),
                                    arb(kfb, fch * 512, fch * 512 + 512), start=st, stop=sp)
                mk.T(f, ["fvw", kfb], [f"kvp_{wi}_{mo%2}"], tag="fvmm")
            def cmb(e, mo=mo, wi=wi, srw=srw, pp=pp):
                return e.tensor_mul(out=arf(f"t5_{wi}"),
                                    in0=arf(srw, mo * 512, mo * 512 + 512),
                                    in1=psf(f"kvp_{wi}_{pp // 512}", 0, 512))
            mk.V(cmb, [srw, f"kvp_{wi}_{mo%2}"], [f"t5w_{wi}"], tag="cmb1")
            def cmb2(e, mo=mo, wi=wi, xmw=xmw):
                return e.tensor_add(out=arf(xmw, mo * 512, mo * 512 + 512),
                                    in0=arf(xmw, mo * 512, mo * 512 + 512),
                                    in1=arf(f"t5_{wi}"))
            mk.V(cmb2, [f"t5w_{wi}"], [xmw], tag="cmb2")
        PA.free(f"kvp_{wi}_0")
        PA.free(f"kvp_{wi}_1")
        A.free(f"t5_{wi}", [f"t5w_{wi}"])
        A.free(kfb)
        A.free(srw)
        stats_win(xmw, 0, 512, f"sw{wi}")
        x0w = f"x0w{wi}"
        A.alloc(x0w, 8 * wl)
        def x0l(e, lo=lo, x0w=x0w):
            return e.dma_start(
                out=AR[:, A.off(x0w):A.off(x0w) + 8 * 512].rearrange(
                    "p (c t) -> p c t", c=8),
                in_=bass.AP(xin.tensor, lo + 1, [[8 * WH, 128], [WH, 8], [1, 512]]))
        mk.dmaQ(x0l, f"ds_{x0w}", [], [x0w], tag="x0l")
        PA.alloc(f"swb1_{wi}", 512)
        PA.alloc(f"swb2_{wi}", 512)
        bc2(f"sw{wi}", f"swb1_{wi}", f"swb2_{wi}", 512)
        for c in range(8):
            def s1f(e, c=c, xmw=xmw, wi=wi):
                return e.tensor_mul(out=arf(xmw, c * 512, c * 512 + 512),
                                    in0=arf(xmw, c * 512, c * 512 + 512),
                                    in1=psf(f"swb1_{wi}", 0, 512))
            mk.V(s1f, [f"swb1_{wi}"], [xmw], tag="x1a")
            def s2f(e, c=c, xmw=xmw, wi=wi):
                return e.tensor_sub(out=arf(xmw, c * 512, c * 512 + 512),
                                    in0=arf(xmw, c * 512, c * 512 + 512),
                                    in1=psf(f"swb2_{wi}", 0, 512))
            mk.V(s2f, [f"swb2_{wi}"], [xmw], tag="x1b")
            def s3f(e, c=c, xmw=xmw, x0w=x0w):
                return e.tensor_add(out=arf(xmw, c * 512, c * 512 + 512),
                                    in0=arf(xmw, c * 512, c * 512 + 512),
                                    in1=arf(x0w, c * 512, c * 512 + 512))
            mk.V(s3f, [x0w], [xmw], tag="x1c")
        PA.free(f"swb1_{wi}")
        PA.free(f"swb2_{wi}")
        A.free(x0w)
        def x1s(e, lo=lo, xmw=xmw):
            return e.dma_start(
                out=bass.AP(s_x1.tensor, lo, [[8 * W, 128], [W, 8], [1, 512]]),
                in_=AR[:, A.off(xmw):A.off(xmw) + 8 * 512].rearrange(
                    "p (c t) -> p c t", c=8))
        mk.dmaQ(x1s, f"ds_x1s{wi}", [xmw], ["s_x1"], tag="x1s")
        stats_win(xmw, 0, 512, f"pw{wi}")
        PA.alloc(f"pwb1_{wi}", 512)
        PA.alloc(f"pwb2_{wi}", 512)
        bc2(f"pw{wi}", f"pwb1_{wi}", f"pwb2_{wi}", 512)
        for c in range(8):
            def p1f(e, c=c, xmw=xmw, wi=wi):
                return e.tensor_mul(out=arf(xmw, c * 512, c * 512 + 512),
                                    in0=arf(xmw, c * 512, c * 512 + 512),
                                    in1=psf(f"pwb1_{wi}", 0, 512))
            mk.V(p1f, [f"pwb1_{wi}"], [xmw], tag="pn1")
            def p2f(e, c=c, xmw=xmw, wi=wi):
                return e.tensor_sub(out=arf(xmw, c * 512, c * 512 + 512),
                                    in0=arf(xmw, c * 512, c * 512 + 512),
                                    in1=psf(f"pwb2_{wi}", 0, 512))
            mk.V(p2f, [f"pwb2_{wi}"], [xmw], tag="pn2")
        PA.free(f"pwb1_{wi}")
        PA.free(f"pwb2_{wi}")
        if wi == 3:
            for c in range(8):
                def xbs(e, c=c, xmw=xmw):
                    return e.tensor_copy(out=XBND[:, c:c + 1],
                                         in_=arf(xmw, c * 512 + 511, c * 512 + 512))
                mk.V(xbs, [xmw], ["xbnd0"], tag="xbs")
        def xis(e, lo=lo, xmw=xmw):
            return e.dma_start(
                out=bass.AP(s_xi.tensor, lo, [[8 * W, 128], [W, 8], [1, 512]]),
                in_=AR[:, A.off(xmw):A.off(xmw) + 8 * 512].rearrange(
                    "p (c t) -> p c t", c=8))
        mk.dmaQ(xis, f"ds_xis{wi}", [xmw], ["s_xi2"], tag="xis")
        A.free(f"sw{wi}B", [f"sw{wi}_ex2", f"sw{wi}_var", f"sw{wi}_r"])
        A.free(f"sw{wi}D", [f"sw{wi}_mean", f"sw{wi}_mr"])
        A.free(f"pw{wi}B", [f"pw{wi}_ex2", f"pw{wi}_var", f"pw{wi}_r"])
        A.free(f"pw{wi}D", [f"pw{wi}_mean", f"pw{wi}_mr"])
        A.free(xmw)
    A.free("fvw")
    A.free("frw")
    A.free("ln2sB", ["ln2_sq_r", "ln2_r1"])
    A.free("ln2sD", ["ln2_mr1"])
    if "xin2" in mk.debug:
        A.alloc("xin2dbg", 8 * W)
        mk.dmaQ(lambda e: e.dma_start(out=arf("xin2dbg"), in_=s_xi),
                "ds_xin2dbg", ["s_xi2"], ["xin2dbg"], tag="xd")
        mk.dbg("xin2", lambda: arf("xin2dbg"), [128, 8 * W], ["xin2dbg"])
        A.free("xin2dbg")
    if "x1" in mk.debug:
        A.alloc("x1dbg", 8 * W)
        mk.dmaQ(lambda e: e.dma_start(out=arf("x1dbg"), in_=s_x1),
                "ds_x1dbg", ["s_x1"], ["x1dbg"], tag="xd1")
        mk.dbg("x1", lambda: arf("x1dbg"), [128, 8 * W], ["x1dbg"])
        A.free("x1dbg")
    if seg < 5:
        return finish(nc, mk, A, PA)

    def ffn_layer(l, src_ap, src_rgn):
        L = f"L{l}"
        QSC = 4 * l
        def bt1(e, l=l):
            return e.dma_start(out=bass.AP(tok_i[l].tensor, 0, [[8, 128], [1, 8]]),
                               in_=XBND[:, 8 * l:8 * l + 8])
        mk.dmaG(bt1, f"ds_bti{L}", [f"xbnd{l}"], [f"toki{L}"], tag="btok")
        em.cc(lambda e, l=l: e.collective_compute(
            "AllGather", ALU.bypass, replica_groups=[[0, 1], [2, 3], [4, 5], [6, 7]],
            ins=[tok_i[l].opt()], outs=[tok_o[l].opt()]),
            f"cs_tok{L}", [f"toki{L}"], [f"toko{L}"], tag="btok")
        def bt2(e, l=l):
            return e.dma_start(out=BTOK[:, 8:16],
                               in_=bass.AP(tok_o[l].tensor, 0, [[8, 128], [1, 8]]))
        mk.dmaG(bt2, f"ds_bto{L}", [f"toko{L}"], [f"btr{L}"], tag="btok")
        hb = f"hb{L}"
        A.alloc(hb, 8192)
        for c in range(8):
            if c < 4:
                def hc(e, c=c, hb=hb):
                    return e.dma_start(out=arb(hb, c * W, (c + 1) * W),
                                       in_=bass.AP(src_ap.tensor, c * W, [[8 * W, 128], [1, W]]))
            else:
                def hc(e, c=c, hb=hb):
                    return e.dma_start(out=arb(hb, c * W + 1, (c + 1) * W),
                                       in_=bass.AP(src_ap.tensor, c * W, [[8 * W, 128], [1, W - 1]]))
            mk.dmaG(hc, "pb_hb" + L, [src_rgn], [hb], tag="hbld")
        for c in range(4, 8):
            def hc0(e, c=c, hb=hb):
                return e.tensor_tensor(out=arb(hb, c * W, c * W + 1),
                                       in0=BTOK[:, 8 + c:9 + c], in1=SMASK[:, 0:1],
                                       op=ALU.mult)
            mk.V(hc0, [f"btr{L}", "smask"], [hb], tag="hb0")
        if subseg == 11:
            return None
        for i, nm in enumerate(("tqA", "tqB", "tkA", "tkB")):
            A.alloc(nm + L, 2048)
            def tl(e, i=i, nm=nm):
                return e.dma_start(out=arf(nm + L),
                                   in_=bass.AP(tabs_d.tensor, i * 2048, [[8192, 128], [1, 2048]]))
            mk.dmaQ(tl, f"ds_{nm}{L}", [], [nm + L + "_c"], tag="tab")
            def tp(e, i=i, nm=nm):
                return e.tensor_scalar_mul(out=arf(nm + L), in0=arf(nm + L),
                                           scalar1=QS[:, QSC + i:QSC + i + 1])
            mk.V(tp, [nm + L + "_c", "qs"], [nm + L], tag="tabp")
        if subseg == 12:
            return None
        wq = f"wq{L}"
        A.alloc(wq, 10240)
        mk.dmaG(lambda e, l=l: e.dma_start(out=arb(wq), in_=wqkv_d[l]), "pb_wq" + L,
                [], [wq], tag="wql")
        qr, kr, vf = f"qr{L}", f"kr{L}", f"vf{L}"
        A.alloc(qr, 4096)
        A.alloc(kr, 5120)
        A.alloc(vf, 4096)
        A.alloc(f"rt1{L}", 512)
        A.alloc(f"rt2{L}", 512)
        A.alloc(f"sqs{L}", 256)
        A.alloc(f"qsr{L}", 2048)
        A.alloc(f"ksr{L}", 2048)
        PA.alloc(f"qp0{L}", 512)
        PA.alloc(f"qp1{L}", 512)
        PA.alloc(f"qp2{L}", 512)
        PA.alloc(f"qp3{L}", 512)
        PA.alloc(f"qst{L}", 512)

        def qkv_mm(mc, win, pn):
            lo = win * 512
            for k in range(8):
                st, sp = (k == 0), (k == 7)
                def f(e, k=k, mc=mc, lo=lo, st=st, sp=sp, pn=pn):
                    return e.matmul(psf(pn, 0, 512),
                                    arb(wq, k * 2560 + mc * 128, k * 2560 + mc * 128 + 128),
                                    arb(hb, k * W + lo, k * W + lo + 512),
                                    start=st, stop=sp)
                mk.T(f, [wq, hb], [pn], tag="qkvmm")

        if subseg == 13:
            return None
        for win in range(4):
            if subseg in (14, 15, 16, 17, 18, 19) and win > 0:
                break
            lo = win * 512
            for mq in range(4):
                qp0n = f"qp{2 * (mq % 2)}{L}"
                qp1n = f"qp{2 * (mq % 2) + 1}{L}"
                qkv_mm(mq, win, qp0n)
                if subseg == 15:
                    def pe_(e, mq=mq, win=win, qp0n=qp0n):
                        return e.copy(out=arf(f"rt1{L}"), in_=psf(qp0n, 0, 512))
                    mk.S(pe_, [qp0n], [f"rt1{L}"], tag="dbgev")
                    continue
                qkv_mm(12 + mq, win, qp1n)
                if subseg == 16:
                    def pe_(e, mq=mq, win=win, qp0n=qp0n):
                        return e.copy(out=arf(f"rt1{L}"), in_=psf(qp0n, 0, 512))
                    mk.S(pe_, [qp0n], [f"rt1{L}"], tag="dbgev")
                    def pe2_(e, mq=mq, win=win, qp1n=qp1n):
                        return e.copy(out=arf(f"rt2{L}"), in_=psf(qp1n, 0, 512))
                    mk.S(pe2_, [qp1n], [f"rt2{L}"], tag="dbgev2")
                    continue
                def sq_(e, win=win, mq=mq, qp0n=qp0n):
                    return e.activation(out=arb(f"sqs{L}", 0, 512), in_=psf(qp0n, 0, 512),
                                        func=AF.Square)
                if subseg != 18:
                    mk.S(sq_, [qp0n], [f"sqs{L}"], tag="sq")
                if subseg == 19:
                    continue
                if subseg not in (17, 18):
                    def hmm(e, mq=mq, win=win):
                        return e.matmul(psf(f"qst{L}", 0, 512)[0:16, :],
                                        HSEG[:, mq * 16:mq * 16 + 16], arb(f"sqs{L}", 0, 512),
                                        start=(mq == 0), stop=(mq == 3))
                    mk.T(hmm, [f"sqs{L}", "hseg"], [f"qst{L}"], tag="hmm")
                def r1(e, mq=mq, win=win, qp0n=qp0n):
                    return e.tensor_mul(out=arf(f"rt1{L}"), in0=psf(qp0n, 0, 512),
                                        in1=arf("tqA" + L, win * 512, win * 512 + 512))
                mk.V(r1, [qp0n, "tqA" + L, f"sqs{L}"], [f"rt1{L}"], tag="rot1")
                def r2(e, mq=mq, win=win, qp1n=qp1n):
                    return e.tensor_mul(out=arf(f"rt2{L}"), in0=psf(qp1n, 0, 512),
                                        in1=arf("tqB" + L, win * 512, win * 512 + 512))
                mk.V(r2, [qp1n, "tqB" + L], [f"rt2{L}"], tag="rot2")
                def r3(e, mq=mq, win=win):
                    return e.tensor_add(out=arb(qr, mq * W + win * 512, mq * W + win * 512 + 512),
                                        in0=arf(f"rt1{L}"), in1=arf(f"rt2{L}"))
                mk.V(r3, [f"rt1{L}", f"rt2{L}"], [qr], tag="rot3")
            if subseg not in (17, 18, 19):
                def qev(e, win=win):
                    return e.tensor_copy(out=AR[0:16, A.off(f"qsr{L}") + win * 512:A.off(f"qsr{L}") + win * 512 + 512],
                                         in_=psf(f"qst{L}", 0, 512)[0:16, :])
                mk.V(qev, [f"qst{L}"], [f"qsr{L}"], tag="qev")
            if subseg in (15, 16, 17, 18, 19):
                continue
            for mk_ in range(4):
                qkv_mm(4 + mk_, win, f"qp0{L}")
                qkv_mm(16 + mk_, win, f"qp1{L}")
                def sq_(e, mk_=mk_, win=win):
                    return e.activation(out=arb(f"sqs{L}", 0, 512), in_=psf(f"qp0{L}", 0, 512),
                                        func=AF.Square)
                mk.S(sq_, [f"qp0{L}"], [f"sqs{L}"], tag="sqk")
                def hmm(e, mk_=mk_, win=win):
                    return e.matmul(psf(f"qst{L}", 0, 512)[0:16, :],
                                    HSEG[:, mk_ * 16:mk_ * 16 + 16], arb(f"sqs{L}", 0, 512),
                                    start=(mk_ == 0), stop=(mk_ == 3))
                mk.T(hmm, [f"sqs{L}", "hseg"], [f"qst{L}"], tag="hmmk")
                def r1(e, mk_=mk_, win=win):
                    return e.tensor_mul(out=arf(f"rt1{L}"), in0=psf(f"qp0{L}", 0, 512),
                                        in1=arf("tkA" + L, win * 512, win * 512 + 512))
                mk.V(r1, [f"qp0{L}", "tkA" + L, f"sqs{L}"], [f"rt1{L}"], tag="krot1")
                def r2(e, mk_=mk_, win=win):
                    return e.tensor_mul(out=arf(f"rt2{L}"), in0=psf(f"qp1{L}", 0, 512),
                                        in1=arf("tkB" + L, win * 512, win * 512 + 512))
                mk.V(r2, [f"qp1{L}", "tkB" + L], [f"rt2{L}"], tag="krot2")
                def r3(e, mk_=mk_, win=win):
                    return e.tensor_add(
                        out=arb(kr, mk_ * 2560 + 512 + win * 512,
                                mk_ * 2560 + 512 + win * 512 + 512),
                        in0=arf(f"rt1{L}"), in1=arf(f"rt2{L}"))
                mk.V(r3, [f"rt1{L}", f"rt2{L}"], [kr], tag="krot3")
            def kev(e, win=win):
                return e.tensor_copy(out=AR[0:16, A.off(f"ksr{L}") + win * 512:A.off(f"ksr{L}") + win * 512 + 512],
                                     in_=psf(f"qst{L}", 0, 512)[0:16, :])
            mk.V(kev, [f"qst{L}"], [f"ksr{L}"], tag="kev")
            for mv in range(4):
                qkv_mm(8 + mv, win, f"qp0{L}")
                def vev(e, mv=mv, win=win):
                    return e.copy(out=arb(vf, mv * W + win * 512, mv * W + win * 512 + 512),
                                  in_=psf(f"qp0{L}", 0, 512))
                mk.S(vev, [f"qp0{L}"], [vf], tag="vev")
        if subseg < 2 or subseg in (15, 16, 17, 18, 19):
            return None
        PA.free(f"qp0{L}")
        PA.free(f"qp1{L}")
        PA.free(f"qp2{L}")
        PA.free(f"qp3{L}")
        PA.free(f"qst{L}")
        A.free(f"rt1{L}")
        A.free(f"rt2{L}")
        A.free(f"sqs{L}")
        A.free(wq)
        A.free(hb)
        for nm in ("tqA", "tqB", "tkA", "tkB"):
            A.free(nm + L, [nm + L + "_c"])

        for tg in (f"qsr{L}", f"ksr{L}"):
            def l1(e, tg=tg):
                return e.activation(out=AR[0:16, A.off(tg):A.off(tg) + 2048],
                                    in_=AR[0:16, A.off(tg):A.off(tg) + 2048],
                                    func=AF.Ln, bias=CONST[0:16, 2:3])
            mk.S(l1, [tg, "const"], [tg], tag="rsln")
            def l2(e, tg=tg):
                return e.activation(out=AR[0:16, A.off(tg):A.off(tg) + 2048],
                                    in_=AR[0:16, A.off(tg):A.off(tg) + 2048],
                                    func=AF.Exp, scale=-0.5)
            mk.S(l2, [tg], [tg], tag="rsex")
            def l3(e, tg=tg):
                return e.tensor_scalar_min(out=AR[0:16, A.off(tg):A.off(tg) + 2048],
                                           in0=AR[0:16, A.off(tg):A.off(tg) + 2048],
                                           scalar1=1e12)
            mk.V(l3, [tg], [tg], tag="rsmin")
        A.alloc(f"rsq{L}", 256)
        PA.alloc(f"rqt{L}", 512)
        for tc in range(16):
            def tr(e, tc=tc):
                return e.transpose(psf(f"rqt{L}", 0, 16)[:, :],
                                   AR[0:16, A.off(f"qsr{L}") + tc * 128:A.off(f"qsr{L}") + tc * 128 + 128],
                                   IDENTF[0:16, 0:16])
            mk.T(tr, [f"qsr{L}", "identf"], [f"rqt{L}"], tag="rstr")
            def ev(e, tc=tc):
                return e.tensor_scalar_mul(out=arf(f"rsq{L}", tc * 16, tc * 16 + 16),
                                           in0=psf(f"rqt{L}", 0, 16), scalar1=8.0)
            mk.V(ev, [f"rqt{L}"], [f"rsq{L}"], tag="rsev")
        PA.free(f"rqt{L}")
        A.free(f"qsr{L}")
        PA.alloc(f"kb{L}", 512)
        for mc in range(4):
            for win in range(4):
                def bm(e, mc=mc, win=win):
                    return e.matmul(psf(f"kb{L}", 0, 512),
                                    EPAIR[:, mc * 128:mc * 128 + 128],
                                    AR[0:16, A.off(f"ksr{L}") + win * 512:A.off(f"ksr{L}") + win * 512 + 512],
                                    start=True, stop=True)
                mk.T(bm, [f"ksr{L}", "epair"], [f"kb{L}"], tag="kbm")
                def km(e, mc=mc, win=win):
                    return e.tensor_mul(
                        out=arb(kr, mc * 2560 + 512 + win * 512, mc * 2560 + 512 + win * 512 + 512),
                        in0=arb(kr, mc * 2560 + 512 + win * 512, mc * 2560 + 512 + win * 512 + 512),
                        in1=psf(f"kb{L}", 0, 512))
                mk.V(km, [f"kb{L}"], [kr], tag="kbmul")
        PA.free(f"kb{L}")
        A.free(f"ksr{L}")
        if subseg < 3:
            return None
        vt = f"vt{L}"
        A.alloc(vt, 5120)
        PA.alloc(f"tpv{L}", 512)
        tpvb = None
        for vc in range(4):
            for tc in range(16):
                def tr(e, vc=vc, tc=tc):
                    pb = PS.bitcast(BF16)
                    off = PA.addr[f"tpv{L}"][0]
                    return e.transpose(pb[:, 2 * off:2 * off + 128],
                                       arb(vf, vc * W + tc * 128, vc * W + tc * 128 + 128),
                                       IDENT)
                mk.T(tr, [vf, "ident"], [f"tpv{L}"], tag="vtr")
                def ev(e, vc=vc, tc=tc):
                    pb = PS.bitcast(BF16)
                    off = PA.addr[f"tpv{L}"][0]
                    return e.tensor_copy(
                        out=arb(vt, (4 + tc) * 512 + vc * 128, (4 + tc) * 512 + vc * 128 + 128),
                        in_=pb[:, 2 * off:2 * off + 128])
                mk.V(ev, [f"tpv{L}"], [vt], tag="vtev")
        PA.free(f"tpv{L}")
        A.free(vf)
        if subseg < 4:
            return None
        def he1(e):
            return e.dma_start(
                out=bass.AP(hal_i[l].tensor, 0, [[2048, 128], [512, 4], [1, 512]]),
                in_=bass.AP(ARB.tensor, None, None) if False else
                ARB[:, 2 * A.off(kr):2 * A.off(kr) + 10240].rearrange(
                    "p (m t) -> p m t", m=4)[:, :, 2048:2560])
        mk.dmaG(he1, f"ds_he1{L}", [kr], [f"hali{L}"], tag="halo")
        def he2(e):
            return e.dma_start(
                out=bass.AP(hal_i[l].tensor, 262144, [[2048, 128], [1, 2048]]),
                in_=arb(vt, 16 * 512, 20 * 512))
        mk.dmaG(he2, f"ds_he2{L}", [vt], [f"hali{L}"], tag="halo")
        em.cc(lambda e, l=l: e.collective_compute(
            "AllGather", ALU.bypass, replica_groups=[[0, 1], [2, 3], [4, 5], [6, 7]],
            ins=[hal_i[l].opt()], outs=[hal_o[l].opt()]),
            f"cs_hal{L}", [f"hali{L}"], [f"halo{L}"], tag="haloc")
        A.alloc(f"hrt{L}", 2048)
        def hr1(e):
            return e.dma_start(
                out=arb(f"hrt{L}", 0, 2048).rearrange("p (m t) -> p m t", m=4),
                in_=bass.AP(hal_o[l].tensor, 0, [[2048, 128], [512, 4], [1, 512]]))
        mk.dmaG(hr1, f"ds_hr1{L}", [f"halo{L}"], [f"hrt1{L}"], tag="halor")
        def hr2(e):
            return e.dma_start(
                out=arb(f"hrt{L}", 2048, 4096),
                in_=bass.AP(hal_o[l].tensor, 262144, [[2048, 128], [1, 2048]]))
        mk.dmaG(hr2, f"ds_hr2{L}", [f"halo{L}"], [f"hrt2{L}"], tag="halor")
        for mc in range(4):
            def hm1(e, mc=mc):
                return e.tensor_scalar_mul(out=arb(kr, mc * 2560, mc * 2560 + 512),
                                           in0=arb(f"hrt{L}", mc * 512, mc * 512 + 512),
                                           scalar1=SMASK[:, 0:1])
            mk.P(hm1, [f"hrt1{L}", "smask"], [kr], tag="hmk")
            def hm2(e, mc=mc):
                return e.tensor_scalar_mul(out=arb(vt, mc * 512, mc * 512 + 512),
                                           in0=arb(f"hrt{L}", 2048 + mc * 512, 2048 + mc * 512 + 512),
                                           scalar1=SMASK[:, 0:1])
            mk.P(hm2, [f"hrt2{L}", "smask"], [vt], tag="hmv")
        A.free(f"hrt{L}", [f"hrt1{L}", f"hrt2{L}"])
        if subseg < 5:
            return None
        ol = f"ol{L}"
        A.alloc(ol, 4096)
        A.alloc(f"aex{L}", 1024)
        A.alloc(f"atx{L}", 1024)
        A.alloc(f"oscb{L}", 512)
        PA.alloc(f"sc0{L}", 1024)
        PA.alloc(f"sc1{L}", 1024)
        PA.alloc(f"tp0{L}", 512)
        PA.alloc(f"tp1{L}", 512)
        PA.alloc(f"op{L}", 512)
        PA.alloc(f"rr{L}", 512)
        PSB = PS.bitcast(BF16)
        qroff = 2 * A.off(qr)
        kroff = 2 * A.off(kr)
        vtoff = 2 * A.off(vt)
        oloff = 2 * A.off(ol)
        aexoff = 2 * A.off(f"aex{L}")
        atxoff = 2 * A.off(f"atx{L}")
        for qt in range(16):
            for h in range(8):
                pr = (qt * 8 + h) % 2
                scn, tpn = f"sc{pr}{L}", f"tp{pr}{L}"
                tpoff = 2 * PA.addr[tpn][0]
                mc, pb_ = h // 2, (h % 2) * 64
                def smm1(e, qt=qt, mc=mc, pb_=pb_, scn=scn):
                    return e.matmul(psf(scn, 0, 512),
                                    ARB[pb_:pb_ + 64, qroff + mc * 2048 + qt * 128:
                                        qroff + mc * 2048 + qt * 128 + 128],
                                    ARB[pb_:pb_ + 64, kroff + mc * 2560 + qt * 128:
                                        kroff + mc * 2560 + qt * 128 + 512],
                                    start=True, stop=True)
                mk.T(smm1, [qr, kr], [scn], tag="smm1")
                def smm2(e, qt=qt, mc=mc, pb_=pb_, scn=scn):
                    return e.matmul(psf(scn, 512, 640),
                                    ARB[pb_:pb_ + 64, qroff + mc * 2048 + qt * 128:
                                        qroff + mc * 2048 + qt * 128 + 128],
                                    ARB[pb_:pb_ + 64, kroff + mc * 2560 + qt * 128 + 512:
                                        kroff + mc * 2560 + qt * 128 + 640],
                                    start=True, stop=True)
                mk.T(smm2, [qr, kr], [scn], tag="smm2")
                def tm1(e, scn=scn):
                    off = PA.addr[scn][0]
                    pv = bass.AP(PS.tensor, off, [[4096, 128], [512, 2], [1, 128]])
                    tv = bass.AP(TRI.tensor, 0, [[256, 128], [128, 2], [1, 128]])
                    return e.tensor_tensor(out=pv, in0=pv, in1=tv, op=ALU.add)
                mk.V(tm1, ["tri", scn], [scn], tag="tri1")
                if qt < 4:
                    def tm3(e, qt=qt, scn=scn):
                        return e.tensor_add(out=psf(scn, 0, 512), in0=psf(scn, 0, 512),
                                            in1=EDGE[:, qt * 512:qt * 512 + 512])
                    mk.V(tm3, ["edge", scn], [scn], tag="tri3")
                def nmx(e, scn=scn, h=h):
                    return e.reduce_max(out=SUMS[:, 16 + h:17 + h], in_=psf(scn, 0, 640),
                                        axis=AX.X, negate=True)
                mk.V(nmx, [scn], [f"s_nmx{h}"], tag="anmx")
                def bia(e, qt=qt, h=h):
                    return e.tensor_mul(out=SUMS[:, 24 + h:25 + h], in0=SUMS[:, 16 + h:17 + h],
                                        in1=arf(f"rsq{L}", qt * 16 + h, qt * 16 + h + 1))
                mk.V(bia, [f"s_nmx{h}", f"rsq{L}"], [f"s_bia{h}"], tag="abia")
                def ex_(e, qt=qt, h=h, pr=pr, scn=scn):
                    return e.activation(out=ARB[:, aexoff + pr * 640:aexoff + pr * 640 + 640],
                                        in_=psf(scn, 0, 640), func=AF.Exp,
                                        scale=arf(f"rsq{L}", qt * 16 + h, qt * 16 + h + 1),
                                        bias=SUMS[:, 24 + h:25 + h], accum_out=SUMS[:, h:h + 1])
                mk.S(ex_, [scn, f"s_bia{h}", f"rsq{L}"], [f"aex{pr}{L}", f"s_sum{h}"], tag="aexp")
                for kc in range(5):
                    def tr(e, kc=kc, pr=pr, tpoff=tpoff):
                        return e.transpose(PSB[:, tpoff + kc * 128:tpoff + kc * 128 + 128],
                                           ARB[:, aexoff + pr * 640 + kc * 128:
                                               aexoff + pr * 640 + kc * 128 + 128], IDENT)
                    mk.T(tr, [f"aex{pr}{L}", "ident"], [tpn], tag="atr")
                def aev(e, pr=pr, tpoff=tpoff):
                    return e.copy(out=ARB[:, atxoff + pr * 640:atxoff + pr * 640 + 640],
                                  in_=PSB[:, tpoff:tpoff + 640])
                mk.S(aev, [tpn], [f"atx{pr}{L}"], tag="aev")
                for kc in range(5):
                    st, sp = (kc == 0), (kc == 4)
                    def vm(e, kc=kc, qt=qt, h=h, pb_=pb_, st=st, sp=sp, pr=pr):
                        return e.matmul(psf(f"op{L}", 0, 128)[pb_:pb_ + 64, :],
                                        ARB[:, vtoff + (qt + kc) * 512 + h * 64:
                                            vtoff + (qt + kc) * 512 + h * 64 + 64],
                                        ARB[:, atxoff + pr * 640 + kc * 128:
                                            atxoff + pr * 640 + kc * 128 + 128],
                                        start=st, stop=sp)
                    mk.T(vm, [vt, f"atx{pr}{L}"], [f"op{L}"], tag="avm")
                def oev(e, h=h, pb_=pb_):
                    return e.copy(out=AR[pb_:pb_ + 64,
                                         A.off(f"oscb{L}") + (h // 2) * 128:
                                         A.off(f"oscb{L}") + (h // 2) * 128 + 128],
                                  in_=psf(f"op{L}", 0, 128)[pb_:pb_ + 64, :])
                mk.S(oev, [f"op{L}"], [f"oscb{h}"], tag="aoev")
            def rc_(e):
                return e.reciprocal(out=SUMS[:, 0:8], in_=SUMS[:, 0:8])
            mk.V(rc_, [f"s_sum{h2}" for h2 in range(8)], ["s_rec"], tag="arec")
            def tr8(e):
                return e.transpose(psf(f"rr{L}", 0, 128)[0:8, :], SUMS[:, 0:8], IDENTF)
            mk.T(tr8, ["s_rec", "identf"], [f"rr{L}"], tag="atr8")
            def rev(e):
                return e.tensor_copy(out=RST[:, :], in_=psf(f"rr{L}", 0, 128)[0:8, :])
            mk.V(rev, [f"rr{L}"], ["rstt"], tag="arev")
            for h in range(8):
                mc, pb_ = h // 2, (h % 2) * 64
                def ehm(e, h=h, pb_=pb_):
                    return e.matmul(psf(f"rr{L}", 128, 256)[pb_:pb_ + 64, :],
                                    EHSET[:, h * 64:h * 64 + 64], RST[:, :],
                                    start=True, stop=True)
                mk.T(ehm, ["rstt", "ehset"], [f"rr{L}b"], tag="aehm")
                def omu(e, h=h, pb_=pb_, qt=qt, mc=mc):
                    return e.tensor_mul(out=ARB[pb_:pb_ + 64, oloff + mc * 2048 + qt * 128:
                                                oloff + mc * 2048 + qt * 128 + 128],
                                        in0=AR[pb_:pb_ + 64,
                                               A.off(f"oscb{L}") + mc * 128:
                                               A.off(f"oscb{L}") + mc * 128 + 128],
                                        in1=psf(f"rr{L}", 128, 256)[pb_:pb_ + 64, :])
                mk.V(omu, [f"oscb{h}", f"rr{L}b"], [ol], tag="aomu")
        PA.free(f"sc0{L}")
        PA.free(f"sc1{L}")
        PA.free(f"tp0{L}")
        PA.free(f"tp1{L}")
        PA.free(f"op{L}")
        PA.free(f"rr{L}")
        if subseg < 6:
            return None
        A.free(f"aex{L}", [f"aex0{L}", f"aex1{L}"])
        A.free(f"atx{L}", [f"atx0{L}", f"atx1{L}"])
        A.free(f"oscb{L}", [f"oscb{h}" for h in range(8)])
        A.free(f"rsq{L}")
        A.free(qr)
        A.free(kr)
        A.free(vt)
        xs = f"xs{L}"
        A.alloc(xs, 8 * W)
        mk.dmaQ(lambda e: e.dma_start(out=arf(xs), in_=src_ap), f"ds_xs{L}",
                [src_rgn], [xs], tag="xsld")
        wow = f"wow{L}"
        A.alloc(wow, 2048)
        mk.dmaG(lambda e, l=l: e.dma_start(out=arb(wow), in_=woff_d[l]), "pb_wow" + L,
                [], [wow], tag="wowld")
        PA.alloc(f"wp0{L}", 512)
        PA.alloc(f"wp1{L}", 512)
        for mo in range(8):
            for win in range(4):
                pn = f"wp{win % 2}{L}"
                for k in range(4):
                    st, sp = (k == 0), (k == 3)
                    def f(e, k=k, mo=mo, win=win, st=st, sp=sp, pn=pn):
                        return e.matmul(psf(pn, 0, 512),
                                        arb(wow, k * CH + mo * 128, k * CH + mo * 128 + 128),
                                        arb(ol, k * W + win * 512, k * W + win * 512 + 512),
                                        start=st, stop=sp)
                    mk.T(f, [wow, ol], [pn], tag="womm2")
                def ev(e, mo=mo, win=win, pn=pn):
                    return e.tensor_add(out=arf(xs, mo * W + win * 512, mo * W + win * 512 + 512),
                                        in0=arf(xs, mo * W + win * 512, mo * W + win * 512 + 512),
                                        in1=psf(pn, 0, 512))
                mk.V(ev, [pn], [xs], tag="woev2")
        PA.free(f"wp0{L}")
        PA.free(f"wp1{L}")
        A.free(wow)
        A.free(ol)
        if subseg < 7:
            return None
        for c in range(8):
            def mb0(e, c=c):
                return e.tensor_copy(out=BTOK[:, 16 + c:17 + c],
                                     in_=arf(xs, c * W + 2047, c * W + 2048))
            mk.V(mb0, [xs], [f"mbx{L}"], tag="mtok")
        def mb1(e, l=l):
            return e.dma_start(out=bass.AP(tok_i[2 + l].tensor, 0, [[8, 128], [1, 8]]),
                               in_=BTOK[:, 16:24])
        mk.dmaG(mb1, f"ds_mbi{L}", [f"mbx{L}"], [f"mtoki{L}"], tag="mtok")
        em.cc(lambda e, l=l: e.collective_compute(
            "AllGather", ALU.bypass, replica_groups=[[0, 1], [2, 3], [4, 5], [6, 7]],
            ins=[tok_i[2 + l].opt()], outs=[tok_o[2 + l].opt()]),
            f"cs_mtok{L}", [f"mtoki{L}"], [f"mtoko{L}"], tag="mtok")
        def mb2(e, l=l):
            return e.dma_start(out=BTOK[:, 24:32],
                               in_=bass.AP(tok_o[2 + l].tensor, 0, [[8, 128], [1, 8]]))
        mk.dmaG(mb2, f"ds_mbo{L}", [f"mtoko{L}"], [f"mbr{L}"], tag="mtok")
        mcast = f"mcast{L}"
        A.alloc(mcast, 8192)
        for c in range(8):
            if c < 4:
                def cst(e, c=c):
                    return e.tensor_copy(out=arb(mcast, c * W, (c + 1) * W),
                                         in_=arf(xs, c * W, (c + 1) * W))
                mk.P(cst, [xs], [mcast], tag="mcst")
            else:
                def cst(e, c=c):
                    return e.tensor_copy(out=arb(mcast, c * W + 1, (c + 1) * W),
                                         in_=arf(xs, c * W, (c + 1) * W - 1))
                mk.P(cst, [xs], [mcast], tag="mcst")
                def cs0(e, c=c):
                    return e.tensor_tensor(out=arb(mcast, c * W, c * W + 1),
                                           in0=BTOK[:, 24 + c:25 + c], in1=SMASK[:, 0:1],
                                           op=ALU.mult)
                mk.V(cs0, [f"mbr{L}", "smask"], [mcast], tag="mcs0")
        ln_stats(None, W, WINS_W, fold2=False, tag=f"mst{L}", cast_name=mcast)
        hbm = f"hbm{L}"
        A.alloc(hbm, 8192)
        A.alloc(f"mt{L}", 512)
        vg = VLNG0 if l == 0 else VLNG1
        vb = VLNB0 if l == 0 else VLNB1
        PA.alloc(f"mb1{L}", 512)
        PA.alloc(f"mb2{L}", 512)
        for win in range(4):
            lo = win * 512
            def b1f(e, lo=lo):
                return e.matmul(psf(f"mb1{L}", 0, 512), ONESR,
                                arf(f"mst{L}sB", lo, lo + 512)[0:1, :], start=True, stop=True)
            mk.T(b1f, [f"mst{L}_r1", "onesr"], [f"mb1{L}"], tag="mb1")
            def b2f(e, lo=lo):
                return e.matmul(psf(f"mb2{L}", 0, 512), ONESR,
                                arf(f"mst{L}sD", lo, lo + 512)[0:1, :], start=True, stop=True)
            mk.T(b2f, [f"mst{L}_mr1", "onesr"], [f"mb2{L}"], tag="mb2")
            for c in range(8):
                sh = 1 if c >= 4 else 0
                llo = lo - sh
                if c >= 4 and win == 0:
                    rlo, rwid, wlo = 0, 511, 1
                else:
                    rlo, rwid, wlo = llo, 512, lo
                def m1f(e, c=c, rlo=rlo, rwid=rwid, wlo=wlo, win=win):
                    return e.tensor_mul(out=arf(f"mt{L}", 0, rwid),
                                        in0=arf(xs, c * W + rlo, c * W + rlo + rwid),
                                        in1=psf(f"mb1{L}", 512 - rwid, 512))
                mk.V(m1f, [xs, f"mb1{L}"], [f"mt{L}"], tag="m1f")
                def m2f(e, c=c, rwid=rwid):
                    return e.tensor_sub(out=arf(f"mt{L}", 0, rwid),
                                        in0=arf(f"mt{L}", 0, rwid),
                                        in1=psf(f"mb2{L}", 512 - rwid, 512))
                mk.V(m2f, [f"mb2{L}"], [f"mt{L}"], tag="m2f")
                def m3f(e, c=c, rwid=rwid, wlo=wlo, vg=vg, vb=vb):
                    return e.tensor_scalar(out=arb(hbm, c * W + wlo, c * W + wlo + rwid),
                                           in0=arf(f"mt{L}", 0, rwid),
                                           scalar1=VEC[:, vg + c:vg + c + 1],
                                           scalar2=VEC[:, vb + c:vb + c + 1],
                                           op0=ALU.mult, op1=ALU.add)
                mk.V(m3f, [f"mt{L}", "vec"], [hbm], tag="m3f")
                if c >= 4 and win == 0:
                    def c1f(e, c=c):
                        return e.tensor_tensor(out=SUMS[:, 10:11], in0=BTOK[:, 24 + c:25 + c],
                                               in1=SMASK[:, 0:1], op=ALU.mult)
                    mk.V(c1f, [f"mbr{L}", "smask"], ["s_c1"], tag="c1f")
                    def c2f(e, c=c):
                        return e.tensor_mul(out=SUMS[:, 10:11], in0=SUMS[:, 10:11],
                                            in1=psf(f"mb1{L}", 0, 1))
                    mk.V(c2f, ["s_c1", f"mb1{L}"], ["s_c1"], tag="c2f")
                    def c3f(e, c=c):
                        return e.tensor_sub(out=SUMS[:, 10:11], in0=SUMS[:, 10:11],
                                            in1=psf(f"mb2{L}", 0, 1))
                    mk.V(c3f, ["s_c1", f"mb2{L}"], ["s_c1"], tag="c3f")
                    def c4f(e, c=c, vg=vg, vb=vb):
                        return e.tensor_scalar(out=arb(hbm, c * W, c * W + 1),
                                               in0=SUMS[:, 10:11],
                                               scalar1=VEC[:, vg + c:vg + c + 1],
                                               scalar2=VEC[:, vb + c:vb + c + 1],
                                               op0=ALU.mult, op1=ALU.add)
                    mk.V(c4f, ["s_c1", "vec"], [hbm], tag="c4f")
        PA.free(f"mb1{L}")
        PA.free(f"mb2{L}")
        A.free(f"mt{L}")
        A.free(mcast)
        A.free(f"mst{L}sB", [f"mst{L}_sq_r", f"mst{L}_r1"])
        A.free(f"mst{L}sD", [f"mst{L}_mr1"])
        spill_ap = s_xi if l == 0 else s_mid
        spill_rgn = f"sxp{L}"
        out_ap = s_mid if l == 0 else s_xi
        out_rgn = f"sxo{L}"
        mk.dmaQ(lambda e: e.dma_start(out=spill_ap, in_=arf(xs)), f"ds_sxp{L}",
                [xs], [spill_rgn], tag="mlpspill")
        A.free(xs)
        w2w = f"w2w{L}"
        A.alloc(w2w, 11264)
        mk.dmaG(lambda e, l=l: e.dma_start(out=arb(w2w), in_=w2f_d[l]), "pb_w2w" + L,
                [], [w2w], tag="w2ld")
        bcol = VB2F0 if l == 0 else VB2F1
        for wb in range(2):
            glu = f"glu{L}_{wb}"
            A.alloc(glu, 11264)
            A.alloc(f"gel{L}_{wb}0", 512)
            A.alloc(f"gel{L}_{wb}1", 512)
            PA.alloc(f"ap{L}{wb}0", 512)
            PA.alloc(f"gp{L}{wb}0", 512)
            PA.alloc(f"ap{L}{wb}1", 512)
            PA.alloc(f"gp{L}{wb}1", 512)
            if wb == 0:
                A.alloc(f"w1bA{L}", 1024)
                A.alloc(f"w1bB{L}", 1024)
            for mp in range(22):
                apn = f"ap{L}{wb}{mp % 2}"
                gpn = f"gp{L}{wb}{mp % 2}"
                w1b = (f"w1bA{L}" if mp % 2 == 0 else f"w1bB{L}")
                def wld(e, mp=mp, l=l, w1b=w1b):
                    return e.dma_start(
                        out=arb(w1b, 0, 2048).rearrange("p (k m) -> p k m", k=8),
                        in_=bass.AP(w1f_d[l].tensor, mp * 256,
                                    [[8 * 5632, 128], [5632, 8], [1, 256]]))
                mk.dmaG(wld, "pb_" + w1b, [], [w1b], tag="w1ld")
                for sub in range(2):
                    lo = wb * 1024 + sub * 512
                    for k in range(8):
                        st, sp = (k == 0), (k == 7)
                        def fa(e, k=k, lo=lo, st=st, sp=sp, w1b=w1b, apn=apn):
                            return e.matmul(psf(apn, 0, 512),
                                            arb(w1b, k * 256, k * 256 + 128),
                                            arb(hbm, k * W + lo, k * W + lo + 512),
                                            start=st, stop=sp)
                        mk.T(fa, [w1b, hbm], [apn], tag="w1a")
                        def fg(e, k=k, lo=lo, st=st, sp=sp, w1b=w1b, gpn=gpn):
                            return e.matmul(psf(gpn, 0, 512),
                                            arb(w1b, k * 256 + 128, k * 256 + 256),
                                            arb(hbm, k * W + lo, k * W + lo + 512),
                                            start=st, stop=sp)
                        mk.T(fg, [w1b, hbm], [gpn], tag="w1g")
                    def gev(e, mp=mp, wb=wb, l=l, gpn=gpn):
                        return e.activation(out=arf(f"gel{L}_{wb}" + str(mp % 2)),
                                            in_=psf(gpn, 0, 512), func=AF.Gelu,
                                            bias=B1[:, l * 44 + 2 * mp + 1:l * 44 + 2 * mp + 2])
                    mk.S(gev, [gpn, "b1"], [f"gel{L}_{wb}" + str(mp % 2)], tag="gelu")
                    def glf(e, mp=mp, wb=wb, sub=sub, l=l, glu=glu, apn=apn):
                        return e.scalar_tensor_tensor(
                            out=arb(glu, mp * 1024 + sub * 512, mp * 1024 + sub * 512 + 512),
                            in0=psf(apn, 0, 512),
                            scalar=B1[:, l * 44 + 2 * mp:l * 44 + 2 * mp + 1],
                            in1=arf(f"gel{L}_{wb}" + str(mp % 2)), op0=ALU.add, op1=ALU.mult)
                    mk.V(glf, [apn, "b1", f"gel{L}_{wb}" + str(mp % 2)], [glu], tag="glu")
            for pnm in (f"ap{L}{wb}0", f"gp{L}{wb}0", f"ap{L}{wb}1", f"gp{L}{wb}1"):
                PA.free(pnm)
            A.free(f"gel{L}_{wb}0")
            A.free(f"gel{L}_{wb}1")
            for mo in range(8):
                PA.alloc(f"xp{L}{wb}{mo}", 512)
            for sub in range(2):
                lo = wb * 1024 + sub * 512
                xw = f"xw{L}{wb}{sub}"
                A.alloc(xw, 4096)
                def xwl(e, lo=lo, xw=xw, spill_ap=spill_ap):
                    return e.dma_start(
                        out=AR[:, A.off(xw):A.off(xw) + 4096].rearrange(
                            "p (c t) -> p c t", c=8),
                        in_=bass.AP(spill_ap.tensor, lo, [[8 * W, 128], [W, 8], [1, 512]]))
                mk.dmaQ(xwl, f"ds_{xw}", [spill_rgn], [xw], tag="xwl")
                for mo in range(8):
                    for kp in range(22):
                        st, sp = (kp == 0), (kp == 21)
                        def f2(e, kp=kp, mo=mo, st=st, sp=sp, glu=glu, sub=sub):
                            return e.matmul(psf(f"xp{L}{wb}{mo}", 0, 512),
                                            arb(w2w, kp * CH + mo * 128, kp * CH + mo * 128 + 128),
                                            arb(glu, kp * 1024 + sub * 512, kp * 1024 + sub * 512 + 512),
                                            start=st, stop=sp)
                        mk.T(f2, [w2w, glu], [f"xp{L}{wb}{mo}"], tag="w2mm")
                    def ev2(e, mo=mo, bcol=bcol, xw=xw):
                        return e.scalar_tensor_tensor(
                            out=arf(xw, mo * 512, mo * 512 + 512),
                            in0=psf(f"xp{L}{wb}{mo}", 0, 512),
                            scalar=VEC[:, bcol + mo:bcol + mo + 1],
                            in1=arf(xw, mo * 512, mo * 512 + 512),
                            op0=ALU.add, op1=ALU.add)
                    mk.V(ev2, [f"xp{L}{wb}{mo}", "vec", xw], [xw], tag="w2ev")
                if wb == 1 and sub == 1:
                    for c in range(8):
                        def xbs2(e, c=c, xw=xw, l=l):
                            return e.tensor_copy(out=XBND[:, 8 + c:9 + c],
                                                 in_=arf(xw, c * 512 + 511, c * 512 + 512))
                        if l == 0:
                            mk.V(xbs2, [xw], ["xbnd1"], tag="xbs2")
                def xws(e, lo=lo, xw=xw, out_ap=out_ap):
                    return e.dma_start(
                        out=bass.AP(out_ap.tensor, lo, [[8 * W, 128], [W, 8], [1, 512]]),
                        in_=AR[:, A.off(xw):A.off(xw) + 4096].rearrange(
                            "p (c t) -> p c t", c=8))
                mk.dmaQ(xws, f"ds_{xw}s", [xw], [out_rgn], tag="xws")
                A.free(xw)
            for mo in range(8):
                PA.free(f"xp{L}{wb}{mo}")
            A.free(glu)
            if wb == 1:
                A.free(f"w1bA{L}")
                A.free(f"w1bB{L}")
        A.free(w2w)
        A.free(hbm)
        return out_ap, out_rgn

    r0 = ffn_layer(0, s_xi, "s_xi2")
    if r0 is None:
        return finish(nc, mk, A, PA)
    o0_ap, o0_rgn = r0
    if "xl1o" in mk.debug:
        A.alloc("xl1odbg", 8 * W)
        mk.dmaQ(lambda e: e.dma_start(out=arf("xl1odbg"), in_=o0_ap),
                "ds_xl1odbg", [o0_rgn], ["xl1odbg"], tag="xl1od")
        mk.dbg("xl1o", lambda: arf("xl1odbg"), [128, 8 * W], ["xl1odbg"])
        A.free("xl1odbg")
    if seg < 6:
        return finish(nc, mk, A, PA)
    o1_ap, o1_rgn = ffn_layer(1, o0_ap, o0_rgn)
    xs1 = "xsF"
    A.alloc(xs1, 8 * W)
    mk.dmaQ(lambda e: e.dma_start(out=arf(xs1), in_=o1_ap), "ds_xsF",
            [o1_rgn], [xs1], tag="xsF")
    ln_stats(xs1, W, WINS_W, fold2=False, tag="fin")
    A.alloc("x1f", 8 * W)
    mk.dmaQ(lambda e: e.dma_start(out=arf("x1f"), in_=s_x1), "ds_x1f",
            ["s_x1"], ["x1f"], tag="x1f")
    PA.alloc("fb1", 512)
    PA.alloc("fb2", 512)
    for win in range(4):
        lo = win * 512
        def b1f(e, lo=lo):
            return e.matmul(psf("fb1", 0, 512), ONESR,
                            arf("finsB", lo, lo + 512)[0:1, :], start=True, stop=True)
        mk.T(b1f, ["fin_r1", "onesr"], ["fb1"], tag="fb1")
        def b2f(e, lo=lo):
            return e.matmul(psf("fb2", 0, 512), ONESR,
                            arf("finsD", lo, lo + 512)[0:1, :], start=True, stop=True)
        mk.T(b2f, ["fin_mr1", "onesr"], ["fb2"], tag="fb2")
        for c in range(8):
            def f1(e, c=c, lo=lo):
                return e.tensor_mul(out=arf(xs1, c * W + lo, c * W + lo + 512),
                                    in0=arf(xs1, c * W + lo, c * W + lo + 512),
                                    in1=psf("fb1", 0, 512))
            mk.V(f1, ["fb1"], [xs1], tag="f1")
            def f2(e, c=c, lo=lo):
                return e.tensor_sub(out=arf(xs1, c * W + lo, c * W + lo + 512),
                                    in0=arf(xs1, c * W + lo, c * W + lo + 512),
                                    in1=psf("fb2", 0, 512))
            mk.V(f2, ["fb2"], [xs1], tag="f2")
            def f3(e, c=c, lo=lo):
                return e.tensor_add(out=arf(xs1, c * W + lo, c * W + lo + 512),
                                    in0=arf(xs1, c * W + lo, c * W + lo + 512),
                                    in1=arf("x1f", c * W + lo, c * W + lo + 512))
            mk.V(f3, ["x1f"], [xs1], tag="f3")
    PA.free("fb1")
    PA.free("fb2")
    mk.dmaQ(lambda e: e.dma_start(out=out_d, in_=arf(xs1)), "ds_out",
            [xs1], ["out"], tag="outdma")

    return finish(nc, mk, A, PA)

def finish(nc, mk, A, PA):
    em = mk.em
    if "out" not in em.last_w:
        allr0 = list(em.last_w.keys())
        mk.dmaQ(lambda e: e.dma_start(out=mk.out_d, in_=mk.AR[:, 0:8 * W]),
                "ds_out", allr0, ["out"], tag="dummyout")
    allr = list(em.last_w.keys())
    mk.P(lambda e: e.engine_nop(), allr, [], tag="fin")
    from contextlib import ExitStack
    ctx = ExitStack()
    with nc.Block() as block:
        em.build(nc, block, ctx)
    ctx.close()
    return nc, mk



def img_vec(v):
    return np.ascontiguousarray(v.reshape(8, 128).T)

def img_lhsT(Wdc):
    WT = Wdc.T
    kin, mout = WT.shape
    return np.ascontiguousarray(WT.reshape(kin // 128, 128, mout).transpose(1, 0, 2).reshape(128, -1))

def stage(I):
    x = np.asarray(I["L0_attn_x"])
    f32 = np.float32
    common = {}
    common["wk"] = img_lhsT(I["L0_attn_Wk"])
    common["wv"] = img_lhsT(I["L0_attn_Wv"])
    common["wr"] = img_lhsT(I["L0_attn_Wr"])
    common["wo"] = img_lhsT(I["L0_attn_Wo"])
    common["fk"] = img_lhsT(I["L0_attn_Fk"])
    common["fv"] = img_lhsT(I["L0_attn_Fv"])
    common["fr"] = img_lhsT(I["L0_attn_Fr"])
    vecs = np.zeros((128, 136), f32)
    cols = [("L0_attn_tm_k", 0), ("L0_attn_tm_v", 8), ("L0_attn_tm_r", 16),
            ("L0_attn_ln1_g", 24), ("L0_attn_ln1_b", 32), ("L0_attn_ln2_g", 40),
            ("L0_attn_ln2_b", 48), ("L0_attn_fm_k", 56), ("L0_attn_fm_r", 64),
            ("L0_attn_time_decay", 72), ("L0_attn_time_first", 80)]
    for nm, c0 in cols:
        vecs[:, c0:c0 + 8] = img_vec(np.asarray(I[nm]))
    vecs[:, 88:96] = img_vec(np.asarray(I["L0_ffn_ln_g"][0]))
    vecs[:, 96:104] = img_vec(np.asarray(I["L0_ffn_ln_b"][0]))
    vecs[:, 104:112] = img_vec(np.asarray(I["L0_ffn_ln_g"][1]))
    vecs[:, 112:120] = img_vec(np.asarray(I["L0_ffn_ln_b"][1]))
    vecs[:, 120:128] = img_vec(np.asarray(I["L0_ffn_b2"][0]))
    vecs[:, 128:136] = img_vec(np.asarray(I["L0_ffn_b2"][1]))
    common["vecs"] = vecs
    common["iota1"] = np.broadcast_to(np.arange(1, TL + 1, dtype=f32), (128, TL)).copy()
    common["onesr"] = np.ones((1, 128), f32)
    common["onesc"] = np.ones((128, 1), f32)

    in_maps = []
    for c in range(8):
        b, h = c // 2, c % 2
        t0 = h * TL
        m = dict(common)
        xi = np.zeros((128, 8 * WH), f32)
        xs = x[b, t0:t0 + TL].T
        for ck in range(8):
            xi[:, ck * WH + 1:(ck + 1) * WH] = xs[ck * 128:(ck + 1) * 128]
            if h == 1:
                xi[:, ck * WH] = x[b, t0 - 1, ck * 128:(ck + 1) * 128]
        m["xin"] = xi
        m["smask"] = np.full((128, 1), float(h), f32)
        in_maps.append(m)
    return stage_ffn(I, in_maps)

def _img_from_WT(WT):
    kin, mout = WT.shape
    return np.ascontiguousarray(WT.reshape(kin // 128, 128, mout).transpose(1, 0, 2).reshape(128, -1))

def _swap64(a):
    return np.concatenate([a[32:64], a[0:32]])

def stage_ffn(I, in_maps):
    f32 = np.float32
    for l in range(2):
        wq = np.asarray(I["L0_ffn_wqkv"][l], f32)
        q, k, v = wq[:, :512], wq[:, 512:1024], wq[:, 1024:]
        sw = np.concatenate([np.arange(32, 64), np.arange(0, 32)])
        idx = (np.arange(512) // 64) * 64 + sw[np.arange(512) % 64]
        qrot = q[:, idx]
        krot = k[:, idx]
        ext = np.concatenate([q, k, v, qrot, krot], 1)
        w1 = np.asarray(I["L0_ffn_w1"][l], f32)
        a, g = w1[:, :2730], w1[:, 2730:]
        ap = np.zeros((1024, 2816), f32); ap[:, :2730] = a
        gp = np.zeros((1024, 2816), f32); gp[:, :2730] = g
        pairs = []
        for i in range(22):
            pairs.append(ap[:, i * 128:(i + 1) * 128])
            pairs.append(gp[:, i * 128:(i + 1) * 128])
        w1re = np.concatenate(pairs, 1)
        w2 = np.asarray(I["L0_ffn_w2"][l], f32)
        w2p = np.zeros((2816, 1024), f32); w2p[:2730] = w2
        for m in in_maps:
            m[f"wqkv{l}"] = _img_from_WT(ext)
            m[f"woff{l}"] = _img_from_WT(np.asarray(I["L0_ffn_wo"][l], f32))
            m[f"w1f{l}"] = _img_from_WT(w1re)
            m[f"w2f{l}"] = _img_from_WT(w2p)
    b1re = np.zeros((128, 88), f32)
    for l in range(2):
        b1 = np.asarray(I["L0_ffn_b1"][l], f32)
        ap = np.zeros(2816, f32); ap[:2730] = b1[:2730]
        gp = np.zeros(2816, f32); gp[:2730] = b1[2730:]
        for i in range(22):
            b1re[:, l * 44 + 2 * i] = ap[i * 128:(i + 1) * 128]
            b1re[:, l * 44 + 2 * i + 1] = gp[i * 128:(i + 1) * 128]
    qst = np.zeros((128, 8), f32)
    for l in range(2):
        qs = np.asarray(I["L0_ffn_q_scale"][l], f32)
        ks = np.asarray(I["L0_ffn_k_scale"][l], f32)
        d = np.arange(128) % 64
        qst[:, 4 * l + 0] = qs[d]
        qst[:, 4 * l + 1] = _swap64(qs)[d]
        qst[:, 4 * l + 2] = ks[d]
        qst[:, 4 * l + 3] = _swap64(ks)[d]
    tri = np.zeros((128, 256), f32)
    ii = np.arange(128)[:, None]; jj = np.arange(128)[None, :]
    tri[:, :128] = np.where(jj <= ii, -1e30, 0.0)
    tri[:, 128:] = np.where(jj > ii, -1e30, 0.0)
    ident = np.eye(128, dtype=f32)
    hseg = np.zeros((128, 128), f32)
    for m in range(8):
        hseg[:64, m * 16 + 2 * m] = 1.0
        hseg[64:, m * 16 + 2 * m + 1] = 1.0
    epair = np.zeros((16, 512), f32)
    for m in range(4):
        epair[2 * m, m * 128:m * 128 + 64] = 1.0
        epair[2 * m + 1, m * 128 + 64:m * 128 + 128] = 1.0
    ehset = np.zeros((8, 512), f32)
    for hh in range(8):
        ehset[hh, hh * 64:(hh + 1) * 64] = 1.0
    DH, WIN = 64, 512
    half = np.arange(0, DH, 2, dtype=np.float64)
    inv_freq = 1.0 / 10000.0 ** (half / DH)
    base = (half + 0.4 * DH) / (1.4 * DH)
    sign = np.where(np.arange(64) < 32, -1.0, 1.0)
    for c, m in enumerate(in_maps):
        h = c % 2
        t = np.arange(h * 2048, h * 2048 + 2048, dtype=np.float64)
        fr = t[:, None] * inv_freq[None, :]
        fr = np.concatenate([fr, fr], 1)
        power = (t - 2048.0) / (WIN // 2)
        sc = base[None, :] ** power[:, None]
        sc = np.concatenate([sc, sc], 1)
        cos, sin = np.cos(fr), np.sin(fr)
        Aq = (cos * sc).T
        Bq = (sign[:, None] * (sin * sc).T)
        Ak = (cos / sc).T
        Bk = (sign[:, None] * (sin / sc).T)
        tabs = np.zeros((128, 4 * 2048), f32)
        for i, Tb in enumerate((Aq, Bq, Ak, Bk)):
            tabs[:64, i * 2048:(i + 1) * 2048] = Tb
            tabs[64:, i * 2048:(i + 1) * 2048] = Tb
        m["tabs"] = tabs
        edge = np.zeros((128, 2048), f32)
        if h == 0:
            for qt in range(4):
                edge[:, qt * 512:qt * 512 + (512 - qt * 128)] = -1e30
        m["edge"] = edge
        m["qst"] = qst
        m["tri"] = tri
        m["ident"] = ident
        m["hseg"] = hseg
        m["epair"] = epair
        m["ehset"] = ehset
        m["b1re"] = b1re
    return in_maps


_NC_CACHE = {}

def kernel(**inputs):
    in_maps = stage(inputs)
    if "nc" not in _NC_CACHE:
        _NC_CACHE["nc"] = build(debug=(), seg=99)
    nc, mkk = _NC_CACHE["nc"]
    res = run_bass_kernel_spmd(nc, in_maps, core_ids=list(range(8)))
    out = np.empty((4, 4096, 1024), np.float32)
    for c in range(8):
        b, h = c // 2, c % 2
        img = np.asarray(res.results[c]["out"], np.float32)
        fm = img.reshape(128, 8, 2048).transpose(1, 0, 2).reshape(1024, 2048)
        out[b, h * 2048:(h + 1) * 2048, :] = fm.T
    return out
